# version 15
# baseline (speedup 1.0000x reference)
import numpy as np
from contextlib import ExitStack

import concourse.bass as bass
import concourse.mybir as mybir
from concourse.bass_utils import run_bass_kernel_spmd
from concourse.bass_core_rust import IndirectOffsetOnAxis

F32 = mybir.dt.float32
I32 = mybir.dt.int32
ALU = mybir.AluOpType
ACT = mybir.ActivationFunctionType
AX = mybir.AxisListType

B, T, HID, H, D = 4, 4096, 1024, 16, 64
NCOR = 8
TL = T // 2
TCS = 512
NTC = TL // TCS
KT = HID // 128
E = 8
CAP = 2560
FTC = 512
NFTC = CAP // FTC
TLP8 = TL + 8
DEBUG = False


class Sem:
    def __init__(self, h):
        self.h = h
        self.n = 0


class Prog:

    def __init__(self, nc, sems):
        self.nc = nc
        self.sems = sems
        self.blocks = []
        self.cur = None
        self.bank_tok = {}

    def block(self):
        self.cur = {e: [] for e in ("sync", "tensor", "vector", "scalar", "gpsimd")}
        self.blocks.append(self.cur)

    def op(self, eng, fn):
        self.cur[eng].append(fn)

    def do(self, eng, sem=None, k=1, build=None):
        if sem is not None:
            sem.n += k

        def f(e, build=build, sem=sem, k=k):
            inst = build(e)
            if sem is not None:
                assert inst is not None, "sem'd op must return an instruction"
                inst.then_inc(sem.h, k)
        self.op(eng, f)
        return sem.n if sem is not None else None

    def wait(self, eng, sem, val=None):
        v = sem.n if val is None else val
        self.op(eng, lambda e, s=sem.h, v=v: e.wait_ge(s, v))

    def bank_wait(self, pb):
        t = self.bank_tok.get(pb)
        if t is not None:
            self.wait("tensor", t[0], t[1])

    def bank_set(self, pb, sem):
        self.bank_tok[pb] = (sem, sem.n)

    def emit(self):
        for blk in self.blocks:
            with self.nc.Block() as b:
                for eng in ("sync", "tensor", "vector", "scalar", "gpsimd"):
                    fns = blk[eng]
                    if fns:
                        getattr(b, eng)(lambda e, fns=fns: [f(e) for f in fns])


def build_program():
    nc = bass.Bass("TRN2", target_bir_lowering=False,
                   detect_race_conditions=False)

    def din(name, shape, dt=F32):
        return nc.dram_tensor(name, list(shape), dt, kind="ExternalInput")

    x_in = din("x_fm", [HID, TL])
    wq_in = din("wq", [128, 24, KT, 128])
    wo_in = din("wo", [128, 8, 16, 128])
    w1_in = din("w1", [128, 32, KT, 128])
    w2_in = din("w2", [128, 8, 32, 128])
    wr_in = din("wr", [128, KT, 8])
    al_in = din("alpha16", [16, 1])
    be_in = din("beta16", [16, 1])
    sel_in = din("selc", [128, KT, 16])
    selt_in = din("seltc", [16, KT, 128])
    ut_in = din("utri", [128, 128])
    id_in = din("ident", [128, 128])
    su9_in = din("su9", [8, 9])
    ltm_in = din("ltm", [8, 1])
    eown_in = din("eown", [8, 1])
    msk_in = din("maskcol", [128, 1])
    pi_in = din("pairidx", [128, 1], I32)
    gtk_in = din("gtokf", [128, 16])

    y_out = nc.dram_tensor("y_fm", [HID, TL], F32, kind="ExternalOutput")

    def scratch(name, shape, dbg=False):
        kind = "ExternalOutput" if (DEBUG and dbg) else "Internal"
        return nc.dram_tensor(name, list(shape), F32, kind=kind)

    stq = scratch("stq", [HID, TL], dbg=True)
    stk = scratch("stk", [HID, TL])
    stkv = scratch("stkv", [HID, TL], dbg=True)
    x1h = scratch("x1h", [HID, TL], dbg=True)
    state_b = scratch("state_b", [128, 32])
    allstate = scratch("allstate", [NCOR * 128, 32])
    xnth = scratch("xnth", [TL, HID], dbg=True)
    allxn = scratch("allxn", [NCOR * TL, HID])
    ivp = scratch("ivp", [TLP8, 1])
    allivp = scratch("allivp", [NCOR * TLP8, 1])
    disph = scratch("disph", [HID, CAP], dbg=True)
    eouth = scratch("eouth", [CAP, HID])
    alleout = scratch("alleout", [E * CAP, HID])
    if DEBUG:
        dbg_alleout = scratch("dbg_alleout", [E * CAP, HID], dbg=True)

    es = ExitStack()
    (SD, SST, SW0, SW1, SW2, SW3, SB0, SB1, GP0, GP1, SGD, SQ1, SP0, SP1,
     ST, SV, SS, SG, SCC) = (
        Sem(es.enter_context(nc.semaphore(n)))
        for n in ("sdx", "sst", "sw0", "sw1", "sw2", "sw3", "sb0", "sb1",
                  "gp0", "gp1", "sgd", "sq1", "sp0", "sp1",
                  "st", "sv", "ss", "sg", "scc"))
    SW = (SW0, SW1, SW2, SW3)
    SP = (SP0, SP1)
    SB = (SB0, SB1)
    GP = (GP0, GP1)

    def sb(name, shape, dt=F32, st=es):
        return st.enter_context(nc.sbuf_tensor(name, list(shape), dt))

    C_sel = sb("C_sel", [128, KT * 16])
    C_selt = sb("C_selt", [16, KT * 128])
    C_ut = sb("C_ut", [128, 128])
    C_id = sb("C_id", [128, 128])
    C_su9 = sb("C_su9", [8, 9])
    C_ltm = sb("C_ltm", [8, 1])
    C_eown = sb("C_eown", [8, 1])
    C_msk = sb("C_msk", [128, 1])
    C_ones1 = sb("C_ones1", [1, 128])
    C_onec = sb("C_onec", [128, 1])
    C_eps = sb("C_eps", [1, 1])
    ab16 = sb("ab16", [16, 4])
    dcols = sb("dcols", [128, 64])
    carry = sb("carry", [128, 32])
    sstate = sb("sstate", [128, 32])
    statec = sb("statec", [128, 64])
    rowb = sb("rowb", [1, 2048])
    S_idx = sb("S_idx", [128, 16])
    S_pos = sb("S_pos", [128, 16])
    S_pmax = sb("S_pmax", [128, 16])
    S_srt = sb("S_srt", [128, 16], I32)
    S_rows = sb("S_rows", [128, 16], I32)
    S_i32 = sb("S_i32", [128, 6], I32)
    S_kept = sb("S_kept", [128, 16])
    S_fsel = sb("S_fsel", [128, 16])
    mcol = sb("mcol", [9, 8])
    fill0 = sb("fill0", [1, 2])
    fillc = sb("fillc", [128, 2])
    cntsb = sb("cntsb", [8, 8])
    cnttsb = sb("cnttsb", [8, 8])
    ids32 = sb("ids32", [128, 20], I32)
    vcol = sb("vcol", [128, 20])
    scioc = sb("scioc", [128, 20])
    pi_sb = sb("pi_sb", [128, 1], I32)
    dcb = [sb(f"dcb{i}", [128, 1]) for i in range(16)]
    G_tk = sb("G_tk", [128, 16])
    ivpv = sb("ivpv", [128, 20])
    S_gi32 = sb("S_gi32", [128, 20], I32)
    R_a = sb("R_a", [128, 128])
    R_b = sb("R_b", [128, 128])
    M_gb = sb("M_gb", [128, 16])
    M_sl = sb("M_sl", [128, 16])
    M_pm1 = sb("M_pm1", [128, 16])
    S_j32 = sb("S_j32", [128, 96], I32)
    M_oh = sb("M_oh", [128, 8])
    M_sc = sb("M_sc", [128, 8])
    M_lg = sb("M_lg", [128, 8])
    M_cum = sb("M_cum", [128, 8])
    M_ct = sb("M_ct", [1, 8])
    M_sc2 = sb("M_sc2", [128, 8])
    M_io8 = sb("M_io8", [128, 8])
    M_base = sb("M_base", [1, 16])
    M_lb = sb("M_lb", [1, 8])
    M_lbbc = sb("M_lbbc", [128, 8])
    M_wr = sb("M_wr", [128, KT * 8])
    M_col = sb("M_col", [128, 4])

    pp = [es.enter_context(nc.psum_tensor(f"pp{i}", [128, 512], F32))
          for i in range(8)]

    P = Prog(nc, None)

    def ld(dst_ap, src_ap, sem=None):
        return P.do("sync", sem if sem is not None else SD, 16,
                    lambda e, d=dst_ap, s=src_ap: e.dma_start(out=d, in_=s))

    ALL_DMA_SEMS = (SD, SST, SW0, SW1, SW2, SW3, SB0, SB1, GP0, GP1, SGD,
                    SQ1, SP0, SP1)

    def fence_dram():
        for s in ALL_DMA_SEMS:
            P.wait("sync", s)
            P.wait("gpsimd", s)

    def grp(eng="sync"):
        P.wait(eng, SD)

    RW_MN, RW_R, RW_T = 0, 512, 1024

    def ln_stats(xt, sqscratch, xb=0, bs0=0, bs1=1):
        P.bank_wait(bs0)
        P.bank_wait(bs1)

        def f_sumx(e):
            for kt in range(KT):
                mm = e.matmul(pp[bs0][0:1, 0:TCS], C_onec[:, 0:1],
                              xt[:, xb + kt * TCS:xb + (kt + 1) * TCS],
                              start=(kt == 0), stop=(kt == KT - 1))
            return mm
        P.do("tensor", ST, 1, f_sumx)
        sqmm_tok = {}
        for kt in range(KT):
            if kt >= 2:
                P.wait("vector", ST, sqmm_tok[kt - 2])
            P.do("vector", SV, 1, lambda e, kt=kt:
                 e.tensor_mul(sqscratch[:, (kt % 2) * TCS:(kt % 2 + 1) * TCS],
                              xt[:, xb + kt * TCS:xb + (kt + 1) * TCS],
                              xt[:, xb + kt * TCS:xb + (kt + 1) * TCS]))
            P.wait("tensor", SV, SV.n)
            sqmm_tok[kt] = P.do("tensor", ST, 1, lambda e, kt=kt:
                 e.matmul(pp[bs1][0:1, 0:TCS], C_onec[:, 0:1],
                          sqscratch[:, (kt % 2) * TCS:(kt % 2 + 1) * TCS],
                          start=(kt == 0), stop=(kt == KT - 1)))
        P.wait("vector", ST)
        P.do("vector", None, 1, lambda e: e.tensor_scalar_mul(
            rowb[0:1, RW_MN:RW_MN + TCS], pp[bs0][0:1, 0:TCS], -1.0 / HID))
        P.do("vector", None, 1, lambda e: e.tensor_scalar_mul(
            rowb[0:1, RW_T:RW_T + TCS], pp[bs1][0:1, 0:TCS], 1.0 / HID))
        P.do("vector", None, 1, lambda e: e.tensor_mul(
            rowb[0:1, RW_R:RW_R + TCS], rowb[0:1, RW_MN:RW_MN + TCS],
            rowb[0:1, RW_MN:RW_MN + TCS]))
        P.do("vector", SV, 1, lambda e: e.tensor_sub(
            rowb[0:1, RW_T:RW_T + TCS], rowb[0:1, RW_T:RW_T + TCS],
            rowb[0:1, RW_R:RW_R + TCS]))
        P.bank_set(bs0, SV)
        P.bank_set(bs1, SV)
        P.wait("scalar", SV)
        P.do("scalar", SS, 1, lambda e: e.activation(
            rowb[0:1, RW_T:RW_T + TCS], rowb[0:1, RW_T:RW_T + TCS],
            ACT.Sqrt, bias=C_eps[0:1, 0:1]))
        P.wait("vector", SS)
        P.do("vector", SV, 1, lambda e: e.reciprocal(
            rowb[0:1, RW_R:RW_R + TCS], rowb[0:1, RW_T:RW_T + TCS]))

    P.block()
    ld(C_sel[:, :], sel_in.ap().rearrange("p k h -> p (k h)"))
    ld(C_selt[:, :], selt_in.ap().rearrange("h k p -> h (k p)"))
    ld(C_ut[:, :], ut_in.ap())
    ld(C_id[:, :], id_in.ap())
    ld(C_su9[:, :], su9_in.ap())
    ld(C_ltm[:, :], ltm_in.ap())
    ld(C_eown[:, :], eown_in.ap())
    ld(C_msk[:, :], msk_in.ap())
    ld(pi_sb[:, :], pi_in.ap())
    ld(G_tk[:, :], gtk_in.ap())
    ld(ab16[:, 0:1], al_in.ap())
    ld(ab16[:, 1:2], be_in.ap())
    P.do("vector", None, 1, lambda e: e.memset(C_ones1[:, :], 1.0))
    P.do("vector", None, 1, lambda e: e.memset(C_eps[:, :], 1e-5))
    P.do("vector", None, 1, lambda e: e.memset(C_onec[:, :], 1.0))
    P.do("vector", None, 1, lambda e: e.memset(carry[:, :], 0.0))
    P.do("vector", SV, 1, lambda e: e.memset(statec[:, :], 0.0))
    P.wait("scalar", SD)
    P.do("scalar", SS, 1, lambda e: e.activation(ab16[:, 2:4], ab16[:, 0:2],
                                                 ACT.Sigmoid))
    P.wait("vector", SS)
    P.do("vector", SV, 1, lambda e: e.tensor_scalar_max(ab16[:, 0:2],
                                                        ab16[:, 2:4], 1e-8))
    P.wait("tensor", SV)
    for fam in range(2):
        for kt in range(KT):
            i = fam * 8 + kt
            P.do("tensor", ST if i == 15 else None, 1, lambda e, fam=fam, kt=kt, i=i:
                 e.matmul(pp[0][:, i:i + 1],
                          C_selt[:, kt * 128:(kt + 1) * 128],
                          ab16[:, fam:fam + 1], start=True, stop=True))
    P.wait("vector", ST)
    P.do("vector", SV, 1, lambda e: e.tensor_copy(dcols[:, 0:16], pp[0][:, 0:16]))
    P.bank_set(0, SV)
    P.wait("scalar", SV)
    P.do("scalar", SS, 1, lambda e: e.activation(dcols[:, 16:32], dcols[:, 0:16],
                                                 ACT.Ln))
    P.wait("vector", SS)
    P.do("vector", SV, 1, lambda e: e.tensor_scalar_mul(dcols[:, 48:64],
                                                        dcols[:, 16:32], 512.0))
    P.wait("scalar", SV)
    P.do("scalar", SS, 1, lambda e: e.activation(dcols[:, 32:48], dcols[:, 48:64],
                                                 ACT.Exp))
    for i in range(16):
        P.do("vector", SV, 1, lambda e, i=i: e.tensor_copy(dcb[i][:, 0:1],
                                                           dcols[:, i:i + 1]))

    es1 = ExitStack()
    A_x = sb("A_x", [128, 2 * KT * TCS], st=es1)
    A_rb = sb("A_rb", [128, TCS], st=es1)
    A_u = sb("A_u", [128, TCS], st=es1)
    A_mn = sb("A_mn", [128, TCS], st=es1)
    A_e = sb("A_e", [128, TCS], st=es1)
    A_q = sb("A_q", [128, KT * TCS], st=es1)
    A_k = sb("A_k", [128, KT * TCS], st=es1)
    A_kv = sb("A_kv", [128, KT * TCS], st=es1)
    A_sq = sb("A_sq", [128, 2 * TCS], st=es1)
    A_pow = sb("A_pow", [128, 16 * TCS], st=es1)
    A_io = sb("A_io", [128, TCS], st=es1)
    A_w = sb("A_w", [128, 2 * 1024], st=es1)
    A_s1 = sb("A_s1", [128, TCS], st=es1)
    crow = sb("crow", [1, 3072], st=es1)

    P.block()
    P.do("gpsimd", SG, 1, lambda e: e.iota(A_io[:, :], [[-1, TCS]], base=TCS - 1,
                                           channel_multiplier=0,
                                           allow_small_or_imprecise_dtypes=True))
    P.wait("vector", SG)
    for fam in range(2):
        for kt in range(KT):
            i = fam * 8 + kt
            P.do("vector", SV, 1, lambda e, i=i: e.tensor_scalar_mul(
                A_s1[:, :], A_io[:, :], dcols[:, 16 + i:17 + i]))
            P.wait("scalar", SV)
            P.do("scalar", SS, 1, lambda e, i=i: e.activation(
                A_pow[:, i * TCS:(i + 1) * TCS], A_s1[:, :], ACT.Exp))
    csum_tok = {}
    for m in range(24):
        if m >= 2 and (m - 2) in csum_tok:
            P.wait("sync", ST, csum_tok[m - 2])
        wtok = ld(A_w[:, (m % 2) * 1024:(m % 2) * 1024 + 1024],
                  wq_in[:, m, :, :].rearrange("p k j -> p (k j)"), SW[m % 2])
        P.wait("tensor", SW[m % 2], wtok)
        P.bank_wait(2)

        def fmm(e, m=m):
            for k in range(KT):
                mm = e.matmul(pp[2][0:1, 0:128], C_onec[:, 0:1],
                              A_w[:, (m % 2) * 1024 + k * 128:(m % 2) * 1024 + k * 128 + 128],
                              start=(k == 0), stop=(k == KT - 1))
            return mm
        P.do("tensor", ST, 1, fmm)
        csum_tok[m] = ST.n
        P.wait("vector", ST)
        P.do("vector", SV, 1, lambda e, m=m: e.tensor_copy(
            crow[0:1, m * 128:(m + 1) * 128], pp[2][0:1, 0:128]))
        P.bank_set(2, SV)

    qkv_last = {}
    for tc in range(NTC):
        t0 = tc * TCS
        ab = (tc % 2) * KT * TCS
        if tc == 0:
            P.block()
            fence_dram()
        else:
            P.wait("sync", SD)
            if tc >= 2:
                P.wait("sync", ST, qkv_last[tc - 2])
        for kt in range(KT):
            ld(A_x[:, ab + kt * TCS:ab + (kt + 1) * TCS],
               x_in[kt * 128:(kt + 1) * 128, t0:t0 + TCS])
        dxn = SD.n
        P.wait("tensor", SD, dxn)
        P.wait("vector", SD, dxn)
        ln_stats(A_x, A_sq, xb=ab)
        P.wait("tensor", SV)
        P.bank_wait(2)
        P.do("tensor", ST, 1, lambda e: e.matmul(pp[2][:, 0:TCS], C_ones1[0:1, :],
                                                 rowb[0:1, RW_R:RW_R + TCS],
                                                 start=True, stop=True))
        P.wait("vector", ST)
        P.do("vector", SV, 1, lambda e: e.tensor_copy(A_rb[:, :], pp[2][:, 0:TCS]))
        P.bank_set(2, SV)

        if tc > 0:
            P.wait("vector", SQ1)
            wconsumed = {-2: prev_wcons[22], -1: prev_wcons[23]}
        else:
            wconsumed = {}
        for m in range(24):
            if (m - 2) in wconsumed:
                P.wait("sync", ST, wconsumed[m - 2])
            wtok = ld(A_w[:, (m % 2) * 1024:(m % 2) * 1024 + 1024],
                      wq_in[:, m, :, :].rearrange("p k j -> p (k j)"), SW[m % 2])
            pb = 3 + (m % 2)
            P.wait("tensor", SW[m % 2], wtok)
            P.bank_wait(pb)

            def fmm(e, m=m, pb=pb, ab=ab):
                for k in range(KT):
                    e.matmul(pp[pb][:, 0:TCS],
                             A_w[:, (m % 2) * 1024 + k * 128:(m % 2) * 1024 + k * 128 + 128],
                             A_x[:, ab + k * TCS:ab + (k + 1) * TCS],
                             start=(k == 0), stop=False)
                return e.matmul(pp[pb][:, 0:TCS], crow[0:1, m * 128:(m + 1) * 128],
                                rowb[0:1, RW_MN:RW_MN + TCS], start=False, stop=True)
            P.do("tensor", ST, 1, fmm)
            wconsumed[m] = ST.n
            role = "q" if m < 8 else ("k" if m < 16 else "v")
            j = m % 8
            P.wait("vector", ST)
            if role in ("q", "k"):
                dst = A_q if role == "q" else A_k
                P.do("vector", SV, 1, lambda e, pb=pb: (
                    e.tensor_mul(A_u[:, :], pp[pb][:, 0:TCS], A_rb[:, :]),
                    e.tensor_scalar_min(A_mn[:, :], A_u[:, :], 0.0))[1])
                P.bank_set(pb, SV)
                P.wait("scalar", SV)
                P.do("scalar", SS, 1, lambda e: e.activation(A_e[:, :], A_mn[:, :],
                                                             ACT.Exp))
                P.wait("vector", SS)
                P.do("vector", SV, 1, lambda e, dst=dst, j=j:
                     e.scalar_tensor_tensor(dst[:, j * TCS:(j + 1) * TCS],
                                            A_u[:, :], 0.0, A_e[:, :],
                                            ALU.max, ALU.add))
                P.wait("gpsimd", SV)
                dst_hbm = stq if role == "q" else stk
                P.do("gpsimd", SQ1, 16,
                     lambda e, j=j, t0=t0, dst=dst, dst_hbm=dst_hbm:
                     e.dma_start(out=dst_hbm[j * 128:(j + 1) * 128, t0:t0 + TCS],
                                 in_=dst[:, j * TCS:(j + 1) * TCS]))
            else:
                P.do("vector", SV, 1, lambda e, pb=pb, j=j: (
                    e.tensor_mul(A_u[:, :], pp[pb][:, 0:TCS], A_rb[:, :]),
                    e.tensor_mul(A_kv[:, j * TCS:(j + 1) * TCS], A_u[:, :],
                                 A_k[:, j * TCS:(j + 1) * TCS]))[1])
                P.bank_set(pb, SV)
                P.wait("gpsimd", SV)
                P.do("gpsimd", SQ1, 16, lambda e, j=j, t0=t0:
                     e.dma_start(out=stkv[j * 128:(j + 1) * 128, t0:t0 + TCS],
                                 in_=A_kv[:, j * TCS:(j + 1) * TCS]))
                for fam in range(2):
                    for vi, src in ((0, A_kv), (1, A_k)):
                        i = fam * 8 + j
                        col = (vi + 2 * fam) * 8 + j
                        sc_src = 32 * (tc % 2)
                        sc_dst = 32 * ((tc + 1) % 2)
                        P.do("vector", SV, 1,
                             lambda e, i=i, col=col, src=src, j=j,
                             sc_src=sc_src, sc_dst=sc_dst: (
                                 e.tensor_mul(A_s1[:, :],
                                              src[:, j * TCS:(j + 1) * TCS],
                                              A_pow[:, i * TCS:(i + 1) * TCS]),
                                 e.reduce_sum(A_mn[:, 0:1], A_s1[:, :], axis=AX.X),
                                 e.tensor_scalar(
                                     statec[:, sc_dst + col:sc_dst + col + 1],
                                     statec[:, sc_src + col:sc_src + col + 1],
                                     dcols[:, 32 + i:33 + i], A_mn[:, 0:1],
                                     ALU.mult, ALU.add))[2])
        qkv_last[tc] = wconsumed[23]
        prev_wcons = wconsumed

    es1.close()

    es2 = ExitStack()
    B_q = sb("B_q", [128, KT * TCS], st=es2)
    B_k = sb("B_k", [128, KT * TCS], st=es2)
    B_kv = sb("B_kv", [128, KT * TCS], st=es2)
    B_rkv = sb("B_rkv", [128, KT * TCS], st=es2)
    B_rks = sb("B_rks", [128, KT * TCS], st=es2)
    B_of = sb("B_of", [128, KT * TCS], st=es2)
    B_os = sb("B_os", [128, KT * TCS], st=es2)
    B_x = sb("B_x", [128, KT * TCS], st=es2)
    B_x1 = sb("B_x1", [128, KT * TCS], st=es2)
    B_w = sb("B_w", [128, 2 * 2048], st=es2)
    B_rd = sb("B_rd", [16, 4 * TCS], st=es2)
    B_t = sb("B_t", [128, 2 * TCS], st=es2)

    P.block()
    P.wait("sync", SV)
    ld(state_b.ap(), statec[:, 0:32], SST)
    P.wait("gpsimd", SST)
    P.wait("sync", SST)
    P.wait("sync", SD)
    P.wait("sync", SQ1)
    P.do("gpsimd", SCC, 1, lambda e: e.collective_compute(
        "AllGather", ALU.bypass, replica_groups=[list(range(NCOR))],
        ins=[state_b.ap().opt()], outs=[allstate.ap().opt()]))
    P.wait("gpsimd", SCC)
    P.do("gpsimd", SGD, 16, lambda e: e.indirect_dma_start(
        out=sstate[:, :], out_offset=None,
        in_=allstate.ap(), in_offset=IndirectOffsetOnAxis(pi_sb[:, :], 0)))
    P.wait("vector", SGD)
    P.do("vector", SV, 1, lambda e: e.tensor_scalar_mul(carry[:, :],
                                                        sstate[:, :], C_msk[:, :]))
    if DEBUG:
        dbg_carry = scratch("dbg_carry", [128, 32], dbg=True)
        dbg_sst = scratch("dbg_sst", [128, 33], dbg=True)
        P.wait("sync", SV)
        ld(dbg_carry.ap(), carry[:, :], SST)
        ld(dbg_sst[0:128, 0:32], sstate[:, :], SST)
        P.do("sync", SST, 16, lambda e: e.dma_start(
            out=dbg_sst[0:128, 32:33], in_=C_msk[:, :],
            allow_slow_non_contiguous=True))


    for kt in range(KT):
        ld(B_q[:, kt * TCS:(kt + 1) * TCS], stq[kt * 128:(kt + 1) * 128, 0:TCS])
        ld(B_k[:, kt * TCS:(kt + 1) * TCS], stk[kt * 128:(kt + 1) * 128, 0:TCS])
        ld(B_kv[:, kt * TCS:(kt + 1) * TCS], stkv[kt * 128:(kt + 1) * 128, 0:TCS])
        ld(B_x[:, kt * TCS:(kt + 1) * TCS], x_in[kt * 128:(kt + 1) * 128, 0:TCS])
    ld(M_wr[:, :], wr_in.ap().rearrange("p k j -> p (k j)"))
    P.do("gpsimd", SG, 1, lambda e: e.iota(M_io8[:, :], [[1, 8]], base=0,
                                           channel_multiplier=0,
                                           allow_small_or_imprecise_dtypes=True))
    P.do("vector", SV, 1, lambda e: e.memset(M_base[:, :], 0.0))

    scn_tok = {}
    reload_sd = {}
    pjev_tok = {}

    def emit_reloads(tc):
        t0 = tc * TCS
        P.wait("sync", SD)
        P.wait("sync", SV, pjev_tok[tc - 1])
        for kt in range(KT):
            ld(B_q[:, kt * TCS:(kt + 1) * TCS], stq[kt * 128:(kt + 1) * 128, t0:t0 + TCS])
            ld(B_k[:, kt * TCS:(kt + 1) * TCS], stk[kt * 128:(kt + 1) * 128, t0:t0 + TCS])
            ld(B_kv[:, kt * TCS:(kt + 1) * TCS], stkv[kt * 128:(kt + 1) * 128, t0:t0 + TCS])
            ld(B_x[:, kt * TCS:(kt + 1) * TCS], x_in[kt * 128:(kt + 1) * 128, t0:t0 + TCS])
        reload_sd[tc] = SD.n

    def emit_scans(tc, fam):
        for kt in range(KT):
            i = fam * 8 + kt
            ckv = (0 + 2 * fam) * 8 + kt
            cks = (1 + 2 * fam) * 8 + kt
            if kt == 0 and fam == 0:
                P.wait("vector", SD, reload_sd[tc])
            P.do("vector", SV, 1, lambda e, i=i, kt=kt, ckv=ckv:
                 e.tensor_tensor_scan(B_rkv[:, kt * TCS:(kt + 1) * TCS],
                                      bass.AP(dcb[i], 0, [[1, 128], [0, TCS]]),
                                      B_kv[:, kt * TCS:(kt + 1) * TCS],
                                      carry[:, ckv:ckv + 1], ALU.mult, ALU.add))
            P.wait("vector", SV)
            P.do("vector", SV, 1, lambda e, kt=kt, ckv=ckv:
                 e.tensor_copy(carry[:, ckv:ckv + 1],
                               B_rkv[:, kt * TCS + TCS - 1:kt * TCS + TCS]))
            P.do("vector", SV, 1, lambda e, i=i, kt=kt, cks=cks:
                 e.tensor_tensor_scan(B_rks[:, kt * TCS:(kt + 1) * TCS],
                                      bass.AP(dcb[i], 0, [[1, 128], [0, TCS]]),
                                      B_k[:, kt * TCS:(kt + 1) * TCS],
                                      carry[:, cks:cks + 1], ALU.mult, ALU.add))
            P.wait("vector", SV)
            P.do("vector", SV, 1, lambda e, kt=kt, cks=cks: (
                e.tensor_copy(carry[:, cks:cks + 1],
                              B_rks[:, kt * TCS + TCS - 1:kt * TCS + TCS]),
                e.tensor_mul(B_rks[:, kt * TCS:(kt + 1) * TCS],
                             B_q[:, kt * TCS:(kt + 1) * TCS],
                             B_rks[:, kt * TCS:(kt + 1) * TCS]))[1])
            scn_tok[(tc, fam, kt)] = SV.n

    def emit_den(tc, fam):
        outdst = B_of if fam == 0 else B_os
        P.bank_wait(0)
        for kt in range(KT):
            P.wait("tensor", SV, scn_tok[(tc, fam, kt)])
            P.do("tensor", ST if kt == KT - 1 else None, 1,
                 lambda e, kt=kt: e.matmul(pp[0][0:16, 0:TCS],
                                           C_sel[:, kt * 16:(kt + 1) * 16],
                                           B_rks[:, kt * TCS:(kt + 1) * TCS],
                                           start=(kt == 0), stop=(kt == KT - 1)))
        fb = fam * TCS
        P.wait("vector", ST)
        P.do("vector", SV, 1, lambda e, fb=fb: (
            e.tensor_scalar_max(B_rd[:, 2 * TCS + fb:2 * TCS + fb + TCS],
                                pp[0][0:16, 0:TCS], 1e-6),
            e.reciprocal(B_rd[:, fb:fb + TCS],
                         B_rd[:, 2 * TCS + fb:2 * TCS + fb + TCS]))[1])
        P.bank_set(0, SV)
        for kt in range(KT):
            pb = 1 + (kt % 2)
            P.wait("tensor", SV, SV.n)
            P.bank_wait(pb)
            P.do("tensor", ST, 1, lambda e, kt=kt, fb=fb, pb=pb:
                 e.matmul(pp[pb][:, 0:TCS], C_selt[:, kt * 128:(kt + 1) * 128],
                          B_rd[0:16, fb:fb + TCS], start=True, stop=True))
            P.wait("vector", ST)
            P.do("vector", SV, 1, lambda e, kt=kt, pb=pb, outdst=outdst: (
                e.tensor_mul(B_rks[:, kt * TCS:(kt + 1) * TCS],
                             B_q[:, kt * TCS:(kt + 1) * TCS],
                             B_rkv[:, kt * TCS:(kt + 1) * TCS]),
                e.tensor_mul(outdst[:, kt * TCS:(kt + 1) * TCS],
                             B_rks[:, kt * TCS:(kt + 1) * TCS],
                             pp[pb][:, 0:TCS]))[1])
            P.bank_set(pb, SV)

    for tc in range(NTC):
        t0 = tc * TCS
        if tc == 0:
            P.block()
            P.wait("vector", SD)
            reload_sd[0] = SD.n
            emit_scans(0, 0)
        emit_den(tc, 0)
        emit_scans(tc, 1)
        emit_den(tc, 1)

        wconsumed = ({} if tc == 0 else
                     {-2: prev_pj[6], -1: prev_pj[7]})
        for m in range(8):
            if (m - 2) in wconsumed:
                P.wait("sync", ST, wconsumed[m - 2])
            wtok = ld(B_w[:, (m % 2) * 2048:(m % 2) * 2048 + 2048],
                      wo_in[:, m, :, :].rearrange("p k j -> p (k j)"), SW[m % 2])
            pb = 3 + (m % 2)
            P.wait("tensor", SW[m % 2], wtok)
            P.bank_wait(pb)

            def fpj(e, m=m, pb=pb):
                for k in range(16):
                    src = B_of if k < 8 else B_os
                    mm = e.matmul(pp[pb][:, 0:TCS],
                                  B_w[:, (m % 2) * 2048 + k * 128:(m % 2) * 2048 + k * 128 + 128],
                                  src[:, (k % 8) * TCS:(k % 8 + 1) * TCS],
                                  start=(k == 0), stop=(k == 15))
                return mm
            P.do("tensor", ST, 1, fpj)
            wconsumed[m] = ST.n
            P.wait("vector", ST)
            P.wait("vector", SP[m % 2])
            P.do("vector", SV, 1, lambda e, m=m, pb=pb:
                 e.tensor_add(B_x1[:, m * TCS:(m + 1) * TCS],
                              pp[pb][:, 0:TCS], B_x[:, m * TCS:(m + 1) * TCS]))
            P.bank_set(pb, SV)
            P.wait("gpsimd", SV)
            P.do("gpsimd", SP[m % 2], 16, lambda e, m=m, t0=t0:
                 e.dma_start(out=x1h[m * 128:(m + 1) * 128, t0:t0 + TCS],
                             in_=B_x1[:, m * TCS:(m + 1) * TCS]))
        prev_pj = wconsumed
        pjev_tok[tc] = SV.n

        if tc + 1 < NTC:
            emit_reloads(tc + 1)
            emit_scans(tc + 1, 0)

        P.wait("tensor", SV, pjev_tok[tc])
        ln_stats(B_x1, B_t, bs0=1, bs1=2)
        ln_tok = SV.n
        P.wait("tensor", SV, ln_tok)
        P.bank_wait(0)
        P.bank_wait(3)
        P.do("tensor", None, 1, lambda e: e.matmul(pp[0][:, 0:TCS], C_ones1[0:1, :],
                                                   rowb[0:1, RW_R:RW_R + TCS],
                                                   start=True, stop=True))
        P.do("tensor", ST, 1, lambda e: e.matmul(pp[3][:, 0:TCS], C_ones1[0:1, :],
                                                 rowb[0:1, RW_MN:RW_MN + TCS],
                                                 start=True, stop=True))
        P.wait("vector", ST)
        for kt in range(KT):
            P.do("vector", SV, 1, lambda e, kt=kt: (
                e.tensor_add(B_t[:, 0:TCS],
                             B_x1[:, kt * TCS:(kt + 1) * TCS], pp[3][:, 0:TCS]),
                e.tensor_mul(B_os[:, kt * TCS:(kt + 1) * TCS],
                             B_t[:, 0:TCS], pp[0][:, 0:TCS]))[1])
        P.bank_set(0, SV)
        P.bank_set(3, SV)
        xnn = SV.n
        if tc == 0:
            P.wait("vector", SG)
        for tt in range(4):
            g = tc * 4 + tt
            c0 = tt * 128
            P.wait("tensor", SV, xnn)
            P.bank_wait(4)

            def frt(e, c0=c0):
                for k in range(KT):
                    mm = e.matmul(pp[4][0:128, 0:8],
                                  B_os[:, k * TCS + c0:k * TCS + c0 + 128],
                                  M_wr[:, k * 8:(k + 1) * 8],
                                  start=(k == 0), stop=(k == KT - 1))
                return mm
            P.do("tensor", ST, 1, frt)
            P.wait("vector", ST)
            P.do("vector", SV, 1, lambda e:
                 e.tensor_copy(M_lg[:, :], pp[4][0:128, 0:8]))
            P.bank_set(4, SV)
            P.wait("vector", SV)
            P.do("vector", SV, 1, lambda e:
                 e.reduce_max(M_col[:, 0:1], M_lg[:, :], axis=AX.X))
            P.wait("vector", SV)
            P.do("vector", SV, 1, lambda e:
                 e.tensor_scalar(M_sc[:, :], M_lg[:, :], M_col[:, 0:1], None,
                                 ALU.subtract))
            P.wait("scalar", SV)
            P.do("scalar", SS, 1, lambda e: e.activation(M_sc2[:, :], M_sc[:, :],
                                                         ACT.Exp))
            P.wait("vector", SS)
            P.do("vector", SV, 1, lambda e:
                 e.reduce_sum(M_col[:, 1:2], M_sc2[:, :], axis=AX.X))
            P.wait("vector", SV)
            P.do("vector", SV, 1, lambda e, g=g:
                 e.reciprocal(S_pmax[:, g:g + 1], M_col[:, 1:2]))
            P.do("vector", SV, 1, lambda e:
                 e.tensor_scalar(M_oh[:, :], M_lg[:, :], M_col[:, 0:1], None,
                                 ALU.is_ge))
            P.do("vector", SV, 1, lambda e:
                 e.tensor_scalar_add(M_sc2[:, :], M_io8[:, :], -8.0))
            P.wait("vector", SV)
            P.do("vector", SV, 1, lambda e:
                 e.tensor_mul(M_sc[:, :], M_sc2[:, :], M_oh[:, :]))
            P.wait("vector", SV)
            P.do("vector", SV, 1, lambda e:
                 e.tensor_reduce(M_col[:, 2:3], M_sc[:, :], axis=AX.X, op=ALU.min))
            P.wait("vector", SV)
            P.do("vector", SV, 1, lambda e, g=g:
                 e.tensor_scalar_add(S_idx[:, g:g + 1], M_col[:, 2:3], 8.0))
            P.wait("vector", SV)
            P.do("vector", SV, 1, lambda e, g=g:
                 e.tensor_scalar(M_oh[:, :], M_io8[:, :], S_idx[:, g:g + 1], None,
                                 ALU.is_equal))
            P.wait("tensor", SV)
            P.bank_wait(5)
            if g == 0:
                P.bank_wait(6)
            bc, bn = 8 * (g % 2), 8 * ((g + 1) % 2)
            P.do("tensor", ST, 1, lambda e, g=g, bc=bc: (
                e.matmul(pp[5][0:128, 0:8], C_ut[:, :], M_oh[:, :],
                         start=True, stop=False),
                e.matmul(pp[5][0:128, 0:8], C_ones1[0:1, :],
                         M_base[0:1, bc:bc + 8], start=False, stop=True),
                e.matmul(pp[6][0:1, g * 8:g * 8 + 8], C_onec[:, 0:1], M_oh[:, :],
                         start=True, stop=True))[2])
            P.wait("vector", ST)
            P.do("vector", SV, 1, lambda e:
                 e.tensor_copy(M_cum[:, :], pp[5][0:128, 0:8]))
            P.do("vector", SV, 1, lambda e, g=g:
                 e.tensor_copy(M_ct[0:1, :], pp[6][0:1, g * 8:g * 8 + 8]))
            P.bank_set(5, SV)
            P.wait("vector", SV)
            P.do("vector", SV, 1, lambda e:
                 e.tensor_mul(M_sc[:, :], M_cum[:, :], M_oh[:, :]))
            P.wait("vector", SV)
            P.do("vector", SV, 1, lambda e, g=g:
                 e.reduce_sum(S_pos[:, g:g + 1], M_sc[:, :], axis=AX.X))
            P.do("vector", SV, 1, lambda e, g=g, bc=bc, bn=bn:
                 e.tensor_add(M_base[0:1, bn:bn + 8], M_base[0:1, bc:bc + 8],
                              M_ct[0:1, :]))
            for kq in range(2):
                P.wait("tensor", SV, xnn)
                P.bank_wait(7)
                P.do("tensor", ST, 1, lambda e, c0=c0, kq=kq: (
                    [e.transpose(pp[7][:, (k % 4) * 128:(k % 4) * 128 + 128],
                                 B_os[:, k * TCS + c0:k * TCS + c0 + 128],
                                 C_id[:, :]) for k in range(4 * kq, 4 * kq + 4)][-1]))
                P.wait("vector", ST)
                P.do("vector", SV, 1, lambda e, kq=kq: e.tensor_copy(
                    B_t[:, kq * TCS:kq * TCS + TCS], pp[7][:, 0:TCS]))
                P.bank_set(7, SV)
            P.wait("sync", SV)
            ld(xnth[g * 128:(g + 1) * 128, :], B_t[:, 0:1024], SB0)
            P.wait("vector", SB0)
            if tc < NTC - 1:
                P.wait("gpsimd", SB0)
                P.do("gpsimd", SCC, 1, lambda e, g=g: e.collective_compute(
                    "AllGather", ALU.bypass,
                    replica_groups=[list(range(NCOR))],
                    ins=[bass.AP(xnth, g * 128 * HID, [[1, 128 * HID]]).opt()],
                    outs=[bass.AP(allxn, g * NCOR * 128 * HID,
                                  [[1, NCOR * 128 * HID]]).opt()]))
            elif tt == 3:
                P.wait("gpsimd", SB0)
                P.do("gpsimd", SCC, 1, lambda e: e.collective_compute(
                    "AllGather", ALU.bypass,
                    replica_groups=[list(range(NCOR))],
                    ins=[bass.AP(xnth, 12 * 128 * HID,
                                 [[1, TCS * HID]]).opt()],
                    outs=[bass.AP(allxn, 12 * NCOR * 128 * HID,
                                  [[1, NCOR * TCS * HID]]).opt()]))
        tr_prev = ST.n

    es2.close()

    es3 = ExitStack()
    M_x1 = sb("M_x1", [128, KT * TCS], st=es3)
    M_xn = sb("M_xn", [128, KT * TCS], st=es3)
    M_t = sb("M_t", [128, 1024], st=es3)
    M_t2 = sb("M_t2", [128, 1024], st=es3)
    M_t3 = sb("M_t3", [128, 1024], st=es3)
    idsro = sb("idsro", [1, 2560], st=es3)
    idsri = sb("idsri", [1, 2560], I32, st=es3)

    P.block()
    P.wait("sync", SD)
    if DEBUG:
        dbg_rt = scratch("dbg_rt", [128, 64], dbg=True)
        P.wait("sync", SV)
        ld(dbg_rt[0:128, 0:16], S_idx[:, :], SST)
        ld(dbg_rt[0:128, 16:32], S_pos[:, :], SST)
        ld(dbg_rt[0:128, 32:48], S_pmax[:, :], SST)
        ld(dbg_rt[0:1, 48:64], M_base[0:1, 0:16], SST)
    P.wait("sync", SV)
    ld(mcol[0:8, 0:1], M_base[0:1, 0:8])
    P.wait("tensor", SD)
    P.bank_wait(0)
    P.do("tensor", ST, 1, lambda e: e.matmul(pp[0][0:1, 0:8], mcol[0:8, 0:1],
                                             C_su9[:, 0:8], start=True, stop=True))
    P.wait("vector", ST)
    P.do("vector", SV, 1, lambda e: e.tensor_copy(M_lb[0:1, :], pp[0][0:1, 0:8]))
    P.bank_set(0, SV)
    P.wait("tensor", SV)
    P.bank_wait(1)
    P.do("tensor", ST, 1, lambda e: e.matmul(pp[1][0:128, 0:8], C_ones1[0:1, :],
                                             M_lb[0:1, :], start=True, stop=True))
    P.wait("vector", ST)
    P.do("vector", SV, 1, lambda e: e.tensor_copy(M_lbbc[:, :], pp[1][0:128, 0:8]))
    P.bank_set(1, SV)
    def lbsel16(lbb):
        for e_ in range(8):
            P.do("vector", SV, 1, lambda e, e_=e_: e.tensor_scalar(
                R_a[:, e_ * 16:(e_ + 1) * 16], S_idx[:, :], float(e_), None,
                ALU.is_equal))
        P.wait("vector", SV)
        for e_ in range(8):
            P.do("vector", SV, 1, lambda e, e_=e_, lbb=lbb: e.tensor_scalar(
                R_b[:, e_ * 16:(e_ + 1) * 16], R_a[:, e_ * 16:(e_ + 1) * 16],
                lbb[:, e_:e_ + 1], None, ALU.mult))
        P.wait("vector", SV)
        for q in range(4):
            P.do("vector", SV, 1, lambda e, q=q: e.tensor_add(
                R_a[:, q * 16:(q + 1) * 16],
                R_b[:, 2 * q * 16:(2 * q + 1) * 16],
                R_b[:, (2 * q + 1) * 16:(2 * q + 2) * 16]))
        P.wait("vector", SV)
        for q in range(2):
            P.do("vector", SV, 1, lambda e, q=q: e.tensor_add(
                R_b[:, q * 16:(q + 1) * 16],
                R_a[:, 2 * q * 16:(2 * q + 1) * 16],
                R_a[:, (2 * q + 1) * 16:(2 * q + 2) * 16]))
        P.wait("vector", SV)
        P.do("vector", SV, 1, lambda e: e.tensor_add(R_a[:, 0:16], R_b[:, 0:16],
                                                     R_b[:, 16:32]))
        P.wait("vector", SV)

    lbsel16(M_lbbc)
    P.do("vector", SV, 1, lambda e: e.tensor_add(R_b[:, 0:16], R_a[:, 0:16],
                                                 S_pos[:, :]))
    P.wait("vector", SV)
    P.do("vector", SV, 1, lambda e: e.tensor_scalar_add(R_a[:, 16:32],
                                                        R_b[:, 0:16], -1.0))
    P.wait("vector", SV)
    P.do("vector", SV, 1, lambda e: e.tensor_copy(S_srt[:, :], R_a[:, 16:32]))
    posn = SV.n
    P.wait("gpsimd", SV, posn)
    for g in range(16):
        P.do("gpsimd", GP0, 16, lambda e, g=g: e.indirect_dma_start(
            out=ivp.ap(), out_offset=IndirectOffsetOnAxis(S_srt[:, g:g + 1], 0),
            in_=G_tk[:, g:g + 1], in_offset=None))
    P.do("gpsimd", SQ1, 16, lambda e: e.dma_start(
        out=bass.AP(ivp, TL, [[1, 8]]), in_=M_base[0:1, 0:8]))
    P.wait("gpsimd", GP0)
    P.wait("gpsimd", SQ1)
    P.do("gpsimd", SCC, 1, lambda e: e.collective_compute(
        "AllGather", ALU.bypass, replica_groups=[list(range(NCOR))],
        ins=[ivp.ap().opt()], outs=[allivp.ap().opt()]))
    P.wait("gpsimd", SCC)
    P.wait("sync", SCC)

    P.block()
    cnt_ap = bass.AP(allivp, TL, [[TLP8, 8], [1, 8]])
    cntt_ap = bass.AP(allivp, TL, [[1, 8], [TLP8, 8], [1, 1]])
    ld(cntsb[:, :], cnt_ap)
    P.do("sync", SD, 16, lambda e: e.dma_start(out=cnttsb[:, :], in_=cntt_ap,
                                               allow_slow_non_contiguous=True))
    P.wait("tensor", SD)
    P.bank_wait(0)
    P.do("tensor", None, 1, lambda e: e.matmul(pp[0][0:8, 0:1], cnttsb[:, :],
                                               C_eown[:, :], start=True, stop=True))
    P.do("tensor", None, 1, lambda e: e.matmul(pp[0][0:8, 1:2], cnttsb[:, :],
                                               C_ltm[:, :], start=True, stop=True))
    P.do("tensor", ST, 1, lambda e: e.matmul(pp[0][0:1, 8:16], C_ltm[:, :],
                                             cntsb[:, :], start=True, stop=True))
    P.wait("vector", ST)
    P.do("vector", SV, 1, lambda e: (
        e.tensor_copy(mcol[0:8, 1:3], pp[0][0:8, 0:2]),
        e.tensor_copy(M_lb[0:1, :], pp[0][0:1, 8:16]))[1])
    P.bank_set(0, SV)
    P.wait("tensor", SV)
    P.bank_wait(1)
    P.do("tensor", None, 1, lambda e: e.matmul(pp[1][0:8, 0:1], C_su9[:, 0:8],
                                               mcol[0:8, 1:2], start=True, stop=True))
    P.do("tensor", ST, 1, lambda e: e.matmul(pp[1][0:8, 1:2], C_su9[:, 1:9],
                                             mcol[0:8, 1:2], start=True, stop=True))
    P.wait("vector", ST)
    P.do("vector", SV, 1, lambda e: e.tensor_copy(mcol[0:8, 3:5], pp[1][0:8, 0:2]))
    P.bank_set(1, SV)
    P.wait("sync", SV)
    ld(fill0[0:1, 0:1], mcol[7:8, 4:5])
    P.wait("vector", SD)
    P.do("vector", SV, 1, lambda e:
         e.tensor_scalar_min(fill0[0:1, 1:2], fill0[0:1, 0:1], float(CAP)))
    P.wait("tensor", SV)
    P.bank_wait(4)
    P.do("tensor", ST, 1, lambda e: e.matmul(pp[4][0:128, 0:1], C_ones1[0:1, :],
                                             fill0[0:1, 1:2], start=True, stop=True))
    P.wait("vector", ST)
    P.do("vector", SV, 1, lambda e: e.tensor_copy(fillc[:, 0:1], pp[4][0:128, 0:1]))
    P.bank_set(4, SV)
    P.do("gpsimd", SG, 1, lambda e: e.iota(M_col[0:8, 2:3], [[1, 1]], base=0,
                                           channel_multiplier=TLP8,
                                           allow_small_or_imprecise_dtypes=True))
    P.wait("vector", SG)
    P.do("vector", SV, 1, lambda e:
         e.tensor_add(M_col[0:8, 3:4], M_col[0:8, 2:3], mcol[0:8, 2:3]))
    P.wait("vector", SV)
    P.do("vector", SV, 1, lambda e:
         e.tensor_sub(M_col[0:8, 2:3], M_col[0:8, 3:4], mcol[0:8, 3:4]))
    P.do("gpsimd", SG, 1, lambda e: e.iota(idsro[0:1, :], [[1, 2560]], base=0,
                                           channel_multiplier=0,
                                           allow_small_or_imprecise_dtypes=True))
    M_st = M_xn
    seg_tok = None
    for ch in range(5):
        s0 = ch * 512
        if seg_tok is not None:
            P.wait("gpsimd", SV, seg_tok)
        P.do("gpsimd", SG, 1, lambda e, s0=s0: e.iota(
            M_st[0:8, 0:512], [[1, 512]], base=s0, channel_multiplier=0,
            allow_small_or_imprecise_dtypes=True))
        P.wait("vector", SG)
        P.do("vector", SV, 1, lambda e: (
            e.tensor_scalar(M_st[0:8, 512:1024], M_st[0:8, 0:512], mcol[0:8, 3:4],
                            None, ALU.is_ge),
            e.tensor_scalar(M_st[0:8, 1536:2048], M_st[0:8, 0:512], mcol[0:8, 4:5],
                            None, ALU.is_ge))[1])
        P.wait("vector", SV)
        P.do("vector", SV, 1, lambda e:
             e.tensor_sub(M_st[0:8, 1024:1536], M_st[0:8, 512:1024],
                          M_st[0:8, 1536:2048]))
        seg_tok = SV.n
        P.wait("tensor", SV)
        P.bank_wait(2)
        P.do("tensor", ST, 1, lambda e, s0=s0: (
            e.matmul(pp[2][0:1, 0:512], M_col[0:8, 2:3], M_st[0:8, 1024:1536],
                     start=True, stop=False),
            e.matmul(pp[2][0:1, 0:512], C_ones1[0:1, 0:1], idsro[0:1, s0:s0 + 512],
                     start=False, stop=True))[1])
        P.wait("vector", ST)
        P.do("vector", SV, 1, lambda e, s0=s0:
             e.tensor_scalar(M_st[0:1, 2048:2560], idsro[0:1, s0:s0 + 512],
                             fill0[0:1, 1:2], None, ALU.is_lt))
        P.wait("vector", SV)
        P.do("vector", SV, 1, lambda e:
             e.tensor_mul(M_st[0:1, 2560:3072], pp[2][0:1, 0:512],
                          M_st[0:1, 2048:2560]))
        P.wait("vector", SV)
        P.do("vector", SV, 1, lambda e, s0=s0:
             e.tensor_copy(idsri[0:1, s0:s0 + 512], M_st[0:1, 2560:3072]))
        P.bank_set(2, SV)
    P.wait("sync", SV)
    for gg in range(20):
        P.do("sync", SD, 16, lambda e, gg=gg: e.dma_start(
            out=ids32[:, gg:gg + 1], in_=idsri[0:1, gg * 128:(gg + 1) * 128]))
    P.wait("gpsimd", SD)
    for gg in range(20):
        P.do("gpsimd", GP1, 16, lambda e, gg=gg: e.indirect_dma_start(
            out=ivpv[:, gg:gg + 1], out_offset=None,
            in_=allivp.ap(), in_offset=IndirectOffsetOnAxis(ids32[:, gg:gg + 1],
                                                            0)))
    P.wait("vector", GP1)
    P.do("vector", SV, 1, lambda e: e.tensor_copy(S_gi32[:, :], ivpv[:, :]))
    P.do("gpsimd", SG, 1, lambda e: e.iota(scioc[:, :], [[128, 20]], base=0,
                                           channel_multiplier=1,
                                           allow_small_or_imprecise_dtypes=True))
    P.wait("vector", SG)
    P.do("vector", SV, 1, lambda e: e.tensor_scalar(vcol[:, :], scioc[:, :],
                                                    fillc[:, 0:1], None, ALU.is_lt))
    P.wait("tensor", SV)
    P.bank_wait(3)
    P.do("tensor", ST, 1, lambda e: e.matmul(pp[3][0:128, 0:8], C_ones1[0:1, :],
                                             M_lb[0:1, :], start=True, stop=True))
    P.wait("vector", ST)
    P.do("vector", SV, 1, lambda e: e.tensor_copy(M_lbbc[:, :], pp[3][0:128, 0:8]))
    P.bank_set(3, SV)
    lbsel16(M_lbbc)
    P.do("vector", SV, 1, lambda e: e.tensor_add(R_b[:, 0:16], R_a[:, 0:16],
                                                 S_pos[:, :]))
    P.wait("vector", SV)
    P.do("vector", SV, 1, lambda e: e.tensor_scalar_add(M_gb[:, :],
                                                        R_b[:, 0:16], -1.0))
    P.wait("vector", SV)
    P.do("vector", SV, 1, lambda e: e.tensor_scalar(
        S_kept[:, :], M_gb[:, :], float(CAP), None, ALU.is_lt))
    P.do("vector", SV, 1, lambda e: e.tensor_scalar_min(
        M_sl[:, :], M_gb[:, :], float(CAP - 1)))
    P.do("vector", SV, 1, lambda e: e.tensor_scalar_add(
        M_pm1[:, :], S_pmax[:, :], -1.0))
    P.wait("vector", SV)
    P.do("vector", SV, 1, lambda e: e.tensor_copy(S_j32[:, 0:16], M_sl[:, :]))
    P.do("vector", SV, 1, lambda e: e.tensor_copy(S_j32[:, 48:64], S_idx[:, :]))
    P.do("vector", SV, 1, lambda e: e.tensor_mul(R_a[:, 0:16], S_kept[:, :],
                                                 M_pm1[:, :]))
    P.wait("vector", SV)
    P.do("vector", SV, 1, lambda e: e.tensor_scalar(
        S_j32[:, 16:32], S_j32[:, 0:16], 9, None, ALU.arith_shift_right))
    P.do("vector", SV, 1, lambda e: e.tensor_scalar(
        S_j32[:, 32:48], S_j32[:, 0:16], 511, None, ALU.bitwise_and))
    P.do("vector", SV, 1, lambda e: e.tensor_scalar(
        S_j32[:, 80:96], S_j32[:, 48:64], FTC, None, ALU.mult))
    P.do("vector", SV, 1, lambda e: e.tensor_scalar_add(
        S_fsel[:, :], R_a[:, 0:16], 1.0))
    P.wait("vector", SV)
    P.do("vector", SV, 1, lambda e: e.tensor_scalar(
        S_j32[:, 64:80], S_j32[:, 16:32], NCOR * FTC, None, ALU.mult))
    P.wait("vector", SV)
    P.do("vector", SV, 1, lambda e: e.tensor_tensor(
        S_j32[:, 0:16], S_j32[:, 64:80], S_j32[:, 32:48], ALU.add))
    P.wait("vector", SV)
    P.do("vector", SV, 1, lambda e: e.tensor_tensor(
        S_rows[:, :], S_j32[:, 0:16], S_j32[:, 80:96], ALU.add))

    P.block()
    P.wait("gpsimd", SD)
    P.wait("gpsimd", SV)
    g_tok = {}
    tr_done = {}
    for g in range(20):
        buf = M_t if g % 2 == 0 else M_t2
        if g >= 2:
            P.wait("gpsimd", SV, g_tok[g - 2])
        P.do("gpsimd", GP[g % 2], 16, lambda e, g=g, buf=buf: e.indirect_dma_start(
            out=buf[:, 0:1024], out_offset=None,
            in_=allxn.ap(), in_offset=IndirectOffsetOnAxis(S_gi32[:, g:g + 1],
                                                           0)))
        P.wait("vector", GP[g % 2])
        if g >= 1:
            P.wait("vector", ST, tr_done[g - 1])
        P.do("vector", SV, 1, lambda e, g=g, buf=buf: e.tensor_scalar_mul(
            M_t3[:, 0:1024], buf[:, 0:1024], vcol[:, g:g + 1]))
        g_tok[g] = SV.n
        ebuf = M_x1
        for k in range(KT):
            pb = 4 + (k % 2)
            P.wait("tensor", SV)
            P.bank_wait(pb)
            P.do("tensor", ST, 1, lambda e, k=k, pb=pb: e.transpose(
                pp[pb][:, 0:128], M_t3[:, k * 128:(k + 1) * 128], C_id[:, :]))

            P.wait("vector", ST)
            P.do("vector", SV, 1, lambda e, k=k, pb=pb: e.tensor_copy(
                ebuf[:, k * 128:(k + 1) * 128], pp[pb][:, 0:128]))
            P.bank_set(pb, SV)
            if k == KT - 1:
                tr_done[g] = ST.n
        P.wait("sync", SV)
        for k in range(KT):
            ld(disph[k * 128:(k + 1) * 128, g * 128:(g + 1) * 128],
               ebuf[:, k * 128:(k + 1) * 128], SST)
        P.wait("vector", SST)

    es3.close()

    es4 = ExitStack()
    N_d = sb("N_d", [128, 2 * KT * FTC], st=es4)
    N_h = sb("N_h", [128, 32 * FTC], st=es4)
    N_w = sb("N_w", [128, 4 * 1024], st=es4)
    N_eo = sb("N_eo", [128, 8 * FTC], st=es4)
    N_t = sb("N_t", [128, 1024], st=es4)

    rd_tok = {}
    w1_last = {}
    si = 0
    for ftc in range(NFTC):
        f0 = ftc * FTC
        nb = (ftc % 2) * KT * FTC
        if ftc == 0:
            P.block()
            fence_dram()
        else:
            P.wait("sync", SD)
            if ftc >= 2:
                P.wait("sync", ST, w1_last[ftc - 2])
        for kt in range(KT):
            ld(N_d[:, nb + kt * FTC:nb + (kt + 1) * FTC],
               disph[kt * 128:(kt + 1) * 128, f0:f0 + FTC])
        dln = SD.n
        for dm in range(32):
            sl = si % 4
            if si - 4 in rd_tok:
                P.wait("sync", ST, rd_tok[si - 4])
            wtok = ld(N_w[:, sl * 1024:sl * 1024 + 1024],
                      w1_in[:, dm, :, :].rearrange("p k j -> p (k j)"), SW[sl])
            pb = dm % 2
            P.wait("tensor", SW[sl], wtok)
            P.wait("tensor", SD, dln)
            P.bank_wait(pb)

            def fh(e, sl=sl, pb=pb, nb=nb):
                for k in range(KT):
                    mm = e.matmul(pp[pb][:, 0:FTC],
                                  N_w[:, sl * 1024 + k * 128:sl * 1024 + k * 128 + 128],
                                  N_d[:, nb + k * FTC:nb + (k + 1) * FTC],
                                  start=(k == 0), stop=(k == KT - 1))
                return mm
            P.do("tensor", ST, 1, fh)
            rd_tok[si] = ST.n
            si += 1
            P.wait("vector", ST)
            P.do("vector", SV, 1, lambda e, dm=dm, pb=pb: e.tensor_scalar_max(
                N_h[:, dm * FTC:(dm + 1) * FTC], pp[pb][:, 0:FTC], 0.0))
            P.bank_set(pb, SV)
        w1_last[ftc] = ST.n
        hn = SV.n
        for m in range(8):
            pb = 2 + (m % 2)
            P.bank_wait(pb)
            for quart in range(4):
                sl = si % 4
                P.wait("sync", ST, rd_tok[si - 4])
                wtok = ld(N_w[:, sl * 1024:sl * 1024 + 1024],
                          w2_in[:, m, quart * 8:(quart + 1) * 8, :]
                          .rearrange("p k j -> p (k j)"), SW[sl])
                P.wait("tensor", SW[sl], wtok)
                if m == 0 and quart == 0:
                    P.wait("tensor", SV, hn)

                P.do("tensor", ST, 1, lambda e, quart=quart, pb=pb, sl=sl: (
                    [e.matmul(pp[pb][:, 0:FTC],
                              N_w[:, sl * 1024 + kk * 128:sl * 1024 + kk * 128 + 128],
                              N_h[:, (quart * 8 + kk) * FTC:(quart * 8 + kk + 1) * FTC],
                              start=(quart * 8 + kk == 0),
                              stop=(quart * 8 + kk == 31))
                     for kk in range(8)][-1]))
                rd_tok[si] = ST.n
                si += 1
            P.wait("vector", ST)
            P.do("vector", SV, 1, lambda e, m=m, pb=pb: e.tensor_copy(
                N_eo[:, m * FTC:(m + 1) * FTC], pp[pb][:, 0:FTC]))
            P.bank_set(pb, SV)
        eon = SV.n
        for tb in range(4):
            for m in range(8):
                pb = 6 + (m % 2)
                P.wait("tensor", SV, max(eon, SV.n))
                P.bank_wait(pb)
                P.do("tensor", ST, 1, lambda e, tb=tb, m=m, pb=pb: e.transpose(
                    pp[pb][:, 0:128],
                    N_eo[:, m * FTC + tb * 128:m * FTC + tb * 128 + 128],
                    C_id[:, :]))
                P.wait("vector", ST)
                P.do("vector", SV, 1, lambda e, m=m, pb=pb: e.tensor_copy(
                    N_t[:, m * 128:(m + 1) * 128], pp[pb][:, 0:128]))
                P.bank_set(pb, SV)
            P.wait("sync", SV)
            ld(eouth[f0 + tb * 128:f0 + tb * 128 + 128, :], N_t[:, 0:1024], SST)
            P.wait("vector", SST)
        P.wait("gpsimd", SST)
        P.do("gpsimd", SCC, 1, lambda e, f0=f0, ftc=ftc: e.collective_compute(
            "AllGather", ALU.bypass, replica_groups=[list(range(NCOR))],
            ins=[bass.AP(eouth, f0 * HID, [[1, FTC * HID]]).opt()],
            outs=[bass.AP(alleout, ftc * NCOR * FTC * HID,
                          [[1, NCOR * FTC * HID]]).opt()]))

    P.block()
    P.wait("gpsimd", SCC)
    if DEBUG:
        P.wait("sync", SCC)
    if DEBUG:
        P.do("sync", SST, 16, lambda e: e.dma_start(out=dbg_alleout.ap(),
                                                    in_=alleout.ap()))

    es4.close()
    es5 = ExitStack()
    C_gt = sb("C_gt", [128, 2 * 1024], st=es5)
    C_f = sb("C_f", [128, 2 * 1024], st=es5)
    C_cx = sb("C_cx", [128, 2 * 1024], st=es5)
    C_y = sb("C_y", [128, 2 * 1024], st=es5)
    C_t = sb("C_t", [128, 1024], st=es5)
    C_t2 = sb("C_t2", [128, 1024], st=es5)

    gtok = {}
    ldtok = {}
    subtok = {}
    evactok = {}
    ytok = {}
    P.block()
    for g in range(16):
        sl = g % 2
        gb = sl * 1024
        if g >= 2:
            P.wait("gpsimd", SV, subtok[g - 2])
        P.do("gpsimd", GP[sl], 16, lambda e, g=g, gb=gb: e.indirect_dma_start(
            out=C_gt[:, gb:gb + 1024], out_offset=None,
            in_=alleout.ap(), in_offset=IndirectOffsetOnAxis(S_rows[:, g:g + 1],
                                                             0)))
        gtok[g] = GP[sl].n
        if g >= 2:
            P.wait("sync", SV, evactok[g - 2])
        lsm = SW[2 + sl]
        ld(C_f[:, gb:gb + 1024], xnth[g * 128:(g + 1) * 128, :], lsm)
        for k in range(KT):
            ld(C_cx[:, gb + k * 128:gb + (k + 1) * 128],
               x1h[k * 128:(k + 1) * 128, g * 128:(g + 1) * 128], lsm)
        ldtok[g] = lsm.n
        P.wait("vector", GP[sl], gtok[g])
        P.wait("vector", lsm, ldtok[g])
        P.do("vector", SV, 1, lambda e, g=g, gb=gb: (
            e.tensor_sub(C_t[:, :], C_gt[:, gb:gb + 1024], C_f[:, gb:gb + 1024]),
            e.scalar_tensor_tensor(C_t2[:, :], C_t[:, :], S_kept[:, g:g + 1],
                                   C_f[:, gb:gb + 1024], ALU.mult, ALU.add),
            e.tensor_scalar_mul(C_t[:, :], C_t2[:, :], S_fsel[:, g:g + 1]))[2])
        subtok[g] = SV.n
        for k in range(KT):
            pb = k % 2
            P.wait("tensor", SV)
            P.bank_wait(pb)
            P.do("tensor", ST, 1, lambda e, k=k, pb=pb: e.transpose(
                pp[pb][:, 0:128], C_t[:, k * 128:(k + 1) * 128], C_id[:, :]))
            P.wait("vector", ST)
            if g >= 2 and k == 0:
                P.wait("vector", SQ1, ytok[g - 2])
            P.do("vector", SV, 1, lambda e, k=k, pb=pb, gb=gb: e.tensor_add(
                C_y[:, gb + k * 128:gb + (k + 1) * 128], pp[pb][:, 0:128],
                C_cx[:, gb + k * 128:gb + (k + 1) * 128]))
            P.bank_set(pb, SV)
        evactok[g] = SV.n
        P.wait("gpsimd", SV, evactok[g])
        for k in range(KT):
            P.do("gpsimd", SQ1, 16, lambda e, k=k, g=g, gb=gb:
                 e.dma_start(out=y_out[k * 128:(k + 1) * 128,
                                       g * 128:(g + 1) * 128],
                             in_=C_y[:, gb + k * 128:gb + (k + 1) * 128]))
        ytok[g] = SQ1.n

    P.block()
    for s in ALL_DMA_SEMS:
        P.wait("sync", s)

    es5.close()
    P.emit()
    es.close()
    return nc


def _pack_w(w):
    K, M = w.shape
    k, m = K // 128, M // 128
    return np.ascontiguousarray(
        w.reshape(k, 128, m, 128).transpose(1, 2, 0, 3))


def kernel(L0_attn_x, L0_attn_w_qkv, L0_attn_w_out, L0_attn_alpha, L0_attn_beta,
           L0_ffn_Wr, L0_ffn_W1, L0_ffn_W2):
    x = np.ascontiguousarray(np.asarray(L0_attn_x, dtype=np.float32))
    w_qkv = np.asarray(L0_attn_w_qkv, dtype=np.float32)
    w_out = np.asarray(L0_attn_w_out, dtype=np.float32)
    alpha = np.asarray(L0_attn_alpha, dtype=np.float32)
    beta = np.asarray(L0_attn_beta, dtype=np.float32)
    Wr = np.asarray(L0_ffn_Wr, dtype=np.float32)
    W1 = np.asarray(L0_ffn_W1, dtype=np.float32)
    W2 = np.asarray(L0_ffn_W2, dtype=np.float32)

    wq_p = _pack_w(w_qkv)
    wo_p = _pack_w(w_out)
    wr_p = np.ascontiguousarray(Wr.reshape(KT, 128, 8).transpose(1, 0, 2))
    selc = np.zeros((128, KT, 16), np.float32)
    seltc = np.zeros((16, KT, 128), np.float32)
    for kt in range(KT):
        for p in range(128):
            h = 2 * kt + p // 64
            selc[p, kt, h] = 1.0
            seltc[h, kt, p] = 1.0
    utri = np.triu(np.ones((128, 128), np.float32))
    ident = np.eye(128, dtype=np.float32)
    su9 = (np.arange(8)[:, None] < np.arange(9)[None, :]).astype(np.float32)

    in_maps = []
    for c in range(NCOR):
        b, hf = c // 2, c % 2
        x_fm = np.ascontiguousarray(x[b, hf * TL:(hf + 1) * TL, :].T)
        in_maps.append({
            "x_fm": x_fm,
            "wq": wq_p, "wo": wo_p,
            "w1": _pack_w(W1[c]),
            "w2": _pack_w(W2[c]),
            "wr": wr_p,
            "alpha16": alpha.reshape(16, 1),
            "beta16": beta.reshape(16, 1),
            "selc": selc, "seltc": seltc, "utri": utri, "ident": ident,
            "su9": su9,
            "ltm": (np.arange(8) < c).astype(np.float32).reshape(8, 1),
            "eown": (np.arange(8) == c).astype(np.float32).reshape(8, 1),
            "maskcol": np.full((128, 1), float(hf), np.float32),
            "pairidx": np.arange((c - hf) * 128, (c - hf) * 128 + 128,
                                 dtype=np.int32).reshape(128, 1),
            "gtokf": np.where(
                np.arange(16)[None, :] < 12,
                np.arange(128)[:, None]
                + np.arange(16)[None, :] * (NCOR * 128) + c * 128,
                12 * NCOR * 128 + c * TCS
                + (np.arange(16)[None, :] - 12) * 128
                + np.arange(128)[:, None]).astype(np.float32),
        })

    nc = build_program()
    res = run_bass_kernel_spmd(nc, in_maps, core_ids=list(range(NCOR)))
    y = np.empty((B, T, HID), np.float32)
    for c in range(NCOR):
        b, hf = c // 2, c % 2
        y[b, hf * TL:(hf + 1) * TL, :] = np.asarray(res.results[c]["y_fm"]).T
    return y


# version 16
# speedup vs baseline: 1.0066x; 1.0066x over previous
import numpy as np
from contextlib import ExitStack

import concourse.bass as bass
import concourse.mybir as mybir
from concourse.bass_utils import run_bass_kernel_spmd
from concourse.bass_core_rust import IndirectOffsetOnAxis

F32 = mybir.dt.float32
I32 = mybir.dt.int32
ALU = mybir.AluOpType
ACT = mybir.ActivationFunctionType
AX = mybir.AxisListType

B, T, HID, H, D = 4, 4096, 1024, 16, 64
NCOR = 8
TL = T // 2
TCS = 512
NTC = TL // TCS
KT = HID // 128
E = 8
CAP = 2560
FTC = 512
NFTC = CAP // FTC
TLP8 = TL + 8
DEBUG = False


class Sem:
    def __init__(self, h):
        self.h = h
        self.n = 0


class Prog:

    def __init__(self, nc, sems):
        self.nc = nc
        self.sems = sems
        self.blocks = []
        self.cur = None
        self.bank_tok = {}

    def block(self):
        self.cur = {e: [] for e in ("sync", "tensor", "vector", "scalar", "gpsimd")}
        self.blocks.append(self.cur)

    def op(self, eng, fn):
        self.cur[eng].append(fn)

    def do(self, eng, sem=None, k=1, build=None):
        if sem is not None:
            sem.n += k

        def f(e, build=build, sem=sem, k=k):
            inst = build(e)
            if sem is not None:
                assert inst is not None, "sem'd op must return an instruction"
                inst.then_inc(sem.h, k)
        self.op(eng, f)
        return sem.n if sem is not None else None

    def wait(self, eng, sem, val=None):
        v = sem.n if val is None else val
        self.op(eng, lambda e, s=sem.h, v=v: e.wait_ge(s, v))

    def bank_wait(self, pb):
        t = self.bank_tok.get(pb)
        if t is not None:
            self.wait("tensor", t[0], t[1])

    def bank_set(self, pb, sem):
        self.bank_tok[pb] = (sem, sem.n)

    def emit(self):
        for blk in self.blocks:
            with self.nc.Block() as b:
                for eng in ("sync", "tensor", "vector", "scalar", "gpsimd"):
                    fns = blk[eng]
                    if fns:
                        getattr(b, eng)(lambda e, fns=fns: [f(e) for f in fns])


def build_program():
    nc = bass.Bass("TRN2", target_bir_lowering=False,
                   detect_race_conditions=False)

    def din(name, shape, dt=F32):
        return nc.dram_tensor(name, list(shape), dt, kind="ExternalInput")

    x_in = din("x_fm", [HID, TL])
    wq_in = din("wq", [128, 24, KT, 128])
    wo_in = din("wo", [128, 8, 16, 128])
    w1_in = din("w1", [128, 32, KT, 128])
    w2_in = din("w2", [128, 8, 32, 128])
    wr_in = din("wr", [128, KT, 8])
    al_in = din("alpha16", [16, 1])
    be_in = din("beta16", [16, 1])
    sel_in = din("selc", [128, KT, 16])
    selt_in = din("seltc", [16, KT, 128])
    ut_in = din("utri", [128, 128])
    id_in = din("ident", [128, 128])
    su9_in = din("su9", [8, 9])
    ltm_in = din("ltm", [8, 1])
    eown_in = din("eown", [8, 1])
    msk_in = din("maskcol", [128, 1])
    pi_in = din("pairidx", [128, 1], I32)
    gtk_in = din("gtokf", [128, 16])

    y_out = nc.dram_tensor("y_fm", [HID, TL], F32, kind="ExternalOutput")

    def scratch(name, shape, dbg=False):
        kind = "ExternalOutput" if (DEBUG and dbg) else "Internal"
        return nc.dram_tensor(name, list(shape), F32, kind=kind)

    stq = scratch("stq", [HID, TL], dbg=True)
    stk = scratch("stk", [HID, TL])
    stkv = scratch("stkv", [HID, TL], dbg=True)
    x1h = scratch("x1h", [HID, TL], dbg=True)
    state_b = scratch("state_b", [128, 32])
    allstate = scratch("allstate", [NCOR * 128, 32])
    xnth = scratch("xnth", [TL, HID], dbg=True)
    allxn = scratch("allxn", [NCOR * TL, HID])
    ivp = scratch("ivp", [TLP8, 1])
    allivp = scratch("allivp", [NCOR * TLP8, 1])
    disph = scratch("disph", [HID, CAP], dbg=True)
    eouth = scratch("eouth", [CAP, HID])
    alleout = scratch("alleout", [E * CAP, HID])
    if DEBUG:
        dbg_alleout = scratch("dbg_alleout", [E * CAP, HID], dbg=True)

    es = ExitStack()
    (SD, SST, SW0, SW1, SW2, SW3, SB0, SB1, GP0, GP1, SGD, SQ1, SP0, SP1,
     ST, SV, SS, SG, SCC) = (
        Sem(es.enter_context(nc.semaphore(n)))
        for n in ("sdx", "sst", "sw0", "sw1", "sw2", "sw3", "sb0", "sb1",
                  "gp0", "gp1", "sgd", "sq1", "sp0", "sp1",
                  "st", "sv", "ss", "sg", "scc"))
    SW = (SW0, SW1, SW2, SW3)
    SP = (SP0, SP1)
    SB = (SB0, SB1)
    GP = (GP0, GP1)

    def sb(name, shape, dt=F32, st=es):
        return st.enter_context(nc.sbuf_tensor(name, list(shape), dt))

    C_sel = sb("C_sel", [128, KT * 16])
    C_selt = sb("C_selt", [16, KT * 128])
    C_ut = sb("C_ut", [128, 128])
    C_id = sb("C_id", [128, 128])
    C_su9 = sb("C_su9", [8, 9])
    C_ltm = sb("C_ltm", [8, 1])
    C_eown = sb("C_eown", [8, 1])
    C_msk = sb("C_msk", [128, 1])
    C_ones1 = sb("C_ones1", [1, 128])
    C_onec = sb("C_onec", [128, 1])
    C_eps = sb("C_eps", [1, 1])
    ab16 = sb("ab16", [16, 4])
    dcols = sb("dcols", [128, 64])
    carry = sb("carry", [128, 32])
    sstate = sb("sstate", [128, 32])
    statec = sb("statec", [128, 64])
    rowb = sb("rowb", [1, 2048])
    S_idx = sb("S_idx", [128, 16])
    S_pos = sb("S_pos", [128, 16])
    S_pmax = sb("S_pmax", [128, 16])
    S_srt = sb("S_srt", [128, 16], I32)
    S_rows = sb("S_rows", [128, 16], I32)
    S_i32 = sb("S_i32", [128, 6], I32)
    S_kept = sb("S_kept", [128, 16])
    S_fsel = sb("S_fsel", [128, 16])
    mcol = sb("mcol", [9, 8])
    fill0 = sb("fill0", [1, 2])
    fillc = sb("fillc", [128, 2])
    cntsb = sb("cntsb", [8, 8])
    cnttsb = sb("cnttsb", [8, 8])
    ids32 = sb("ids32", [128, 20], I32)
    vcol = sb("vcol", [128, 20])
    scioc = sb("scioc", [128, 20])
    pi_sb = sb("pi_sb", [128, 1], I32)
    dcb = [sb(f"dcb{i}", [128, 1]) for i in range(16)]
    G_tk = sb("G_tk", [128, 16])
    ivpv = sb("ivpv", [128, 20])
    S_gi32 = sb("S_gi32", [128, 20], I32)
    R_a = sb("R_a", [128, 128])
    R_b = sb("R_b", [128, 128])
    M_gb = sb("M_gb", [128, 16])
    M_sl = sb("M_sl", [128, 16])
    M_pm1 = sb("M_pm1", [128, 16])
    S_j32 = sb("S_j32", [128, 96], I32)
    M_oh = sb("M_oh", [128, 8])
    M_sc = sb("M_sc", [128, 8])
    M_lg = sb("M_lg", [128, 8])
    M_cum = sb("M_cum", [128, 8])
    M_ct = sb("M_ct", [1, 8])
    M_sc2 = sb("M_sc2", [128, 8])
    M_io8 = sb("M_io8", [128, 8])
    M_base = sb("M_base", [1, 16])
    M_lb = sb("M_lb", [1, 8])
    M_lbbc = sb("M_lbbc", [128, 8])
    M_wr = sb("M_wr", [128, KT * 8])
    M_col = sb("M_col", [128, 4])

    pp = [es.enter_context(nc.psum_tensor(f"pp{i}", [128, 512], F32))
          for i in range(8)]

    P = Prog(nc, None)

    def ld(dst_ap, src_ap, sem=None):
        return P.do("sync", sem if sem is not None else SD, 16,
                    lambda e, d=dst_ap, s=src_ap: e.dma_start(out=d, in_=s))

    ALL_DMA_SEMS = (SD, SST, SW0, SW1, SW2, SW3, SB0, SB1, GP0, GP1, SGD,
                    SQ1, SP0, SP1)

    def fence_dram():
        for s in ALL_DMA_SEMS:
            P.wait("sync", s)
            P.wait("gpsimd", s)

    def grp(eng="sync"):
        P.wait(eng, SD)

    RW_MN, RW_R, RW_T = 0, 512, 1024

    def ln_stats(xt, sqscratch, xb=0, bs0=0, bs1=1):
        P.bank_wait(bs0)
        P.bank_wait(bs1)

        def f_sumx(e):
            for kt in range(KT):
                mm = e.matmul(pp[bs0][0:1, 0:TCS], C_onec[:, 0:1],
                              xt[:, xb + kt * TCS:xb + (kt + 1) * TCS],
                              start=(kt == 0), stop=(kt == KT - 1))
            return mm
        P.do("tensor", ST, 1, f_sumx)
        sqmm_tok = {}
        for kt in range(KT):
            if kt >= 2:
                P.wait("vector", ST, sqmm_tok[kt - 2])
            P.do("vector", SV, 1, lambda e, kt=kt:
                 e.tensor_mul(sqscratch[:, (kt % 2) * TCS:(kt % 2 + 1) * TCS],
                              xt[:, xb + kt * TCS:xb + (kt + 1) * TCS],
                              xt[:, xb + kt * TCS:xb + (kt + 1) * TCS]))
            P.wait("tensor", SV, SV.n)
            sqmm_tok[kt] = P.do("tensor", ST, 1, lambda e, kt=kt:
                 e.matmul(pp[bs1][0:1, 0:TCS], C_onec[:, 0:1],
                          sqscratch[:, (kt % 2) * TCS:(kt % 2 + 1) * TCS],
                          start=(kt == 0), stop=(kt == KT - 1)))
        P.wait("vector", ST)
        P.do("vector", None, 1, lambda e: e.tensor_scalar_mul(
            rowb[0:1, RW_MN:RW_MN + TCS], pp[bs0][0:1, 0:TCS], -1.0 / HID))
        P.do("vector", None, 1, lambda e: e.tensor_scalar_mul(
            rowb[0:1, RW_T:RW_T + TCS], pp[bs1][0:1, 0:TCS], 1.0 / HID))
        P.do("vector", None, 1, lambda e: e.tensor_mul(
            rowb[0:1, RW_R:RW_R + TCS], rowb[0:1, RW_MN:RW_MN + TCS],
            rowb[0:1, RW_MN:RW_MN + TCS]))
        P.do("vector", SV, 1, lambda e: e.tensor_sub(
            rowb[0:1, RW_T:RW_T + TCS], rowb[0:1, RW_T:RW_T + TCS],
            rowb[0:1, RW_R:RW_R + TCS]))
        P.bank_set(bs0, SV)
        P.bank_set(bs1, SV)
        P.wait("scalar", SV)
        P.do("scalar", SS, 1, lambda e: e.activation(
            rowb[0:1, RW_T:RW_T + TCS], rowb[0:1, RW_T:RW_T + TCS],
            ACT.Sqrt, bias=C_eps[0:1, 0:1]))
        P.wait("vector", SS)
        P.do("vector", SV, 1, lambda e: e.reciprocal(
            rowb[0:1, RW_R:RW_R + TCS], rowb[0:1, RW_T:RW_T + TCS]))

    P.block()
    ld(C_sel[:, :], sel_in.ap().rearrange("p k h -> p (k h)"))
    ld(C_selt[:, :], selt_in.ap().rearrange("h k p -> h (k p)"))
    ld(C_ut[:, :], ut_in.ap())
    ld(C_id[:, :], id_in.ap())
    ld(C_su9[:, :], su9_in.ap())
    ld(C_ltm[:, :], ltm_in.ap())
    ld(C_eown[:, :], eown_in.ap())
    ld(C_msk[:, :], msk_in.ap())
    ld(pi_sb[:, :], pi_in.ap())
    ld(G_tk[:, :], gtk_in.ap())
    ld(ab16[:, 0:1], al_in.ap())
    ld(ab16[:, 1:2], be_in.ap())
    P.do("vector", None, 1, lambda e: e.memset(C_ones1[:, :], 1.0))
    P.do("vector", None, 1, lambda e: e.memset(C_eps[:, :], 1e-5))
    P.do("vector", None, 1, lambda e: e.memset(C_onec[:, :], 1.0))
    P.do("vector", None, 1, lambda e: e.memset(carry[:, :], 0.0))
    P.do("vector", SV, 1, lambda e: e.memset(statec[:, :], 0.0))
    P.wait("scalar", SD)
    P.do("scalar", SS, 1, lambda e: e.activation(ab16[:, 2:4], ab16[:, 0:2],
                                                 ACT.Sigmoid))
    P.wait("vector", SS)
    P.do("vector", SV, 1, lambda e: e.tensor_scalar_max(ab16[:, 0:2],
                                                        ab16[:, 2:4], 1e-8))
    P.wait("tensor", SV)
    for fam in range(2):
        for kt in range(KT):
            i = fam * 8 + kt
            P.do("tensor", ST if i == 15 else None, 1, lambda e, fam=fam, kt=kt, i=i:
                 e.matmul(pp[0][:, i:i + 1],
                          C_selt[:, kt * 128:(kt + 1) * 128],
                          ab16[:, fam:fam + 1], start=True, stop=True))
    P.wait("vector", ST)
    P.do("vector", SV, 1, lambda e: e.tensor_copy(dcols[:, 0:16], pp[0][:, 0:16]))
    P.bank_set(0, SV)
    P.wait("scalar", SV)
    P.do("scalar", SS, 1, lambda e: e.activation(dcols[:, 16:32], dcols[:, 0:16],
                                                 ACT.Ln))
    P.wait("vector", SS)
    P.do("vector", SV, 1, lambda e: e.tensor_scalar_mul(dcols[:, 48:64],
                                                        dcols[:, 16:32], 512.0))
    P.wait("scalar", SV)
    P.do("scalar", SS, 1, lambda e: e.activation(dcols[:, 32:48], dcols[:, 48:64],
                                                 ACT.Exp))
    for i in range(16):
        P.do("vector", SV, 1, lambda e, i=i: e.tensor_copy(dcb[i][:, 0:1],
                                                           dcols[:, i:i + 1]))

    es1 = ExitStack()
    A_x = sb("A_x", [128, 2 * KT * TCS], st=es1)
    A_rb = sb("A_rb", [128, TCS], st=es1)
    A_u = sb("A_u", [128, TCS], st=es1)
    A_mn = sb("A_mn", [128, TCS], st=es1)
    A_e = sb("A_e", [128, TCS], st=es1)
    A_q = sb("A_q", [128, KT * TCS], st=es1)
    A_k = sb("A_k", [128, KT * TCS], st=es1)
    A_kv = sb("A_kv", [128, KT * TCS], st=es1)
    A_sq = sb("A_sq", [128, 2 * TCS], st=es1)
    A_pow = sb("A_pow", [128, 16 * TCS], st=es1)
    A_io = sb("A_io", [128, TCS], st=es1)
    A_w = sb("A_w", [128, 2 * 1024], st=es1)
    A_s1 = sb("A_s1", [128, TCS], st=es1)
    crow = sb("crow", [1, 3072], st=es1)

    P.block()
    P.do("gpsimd", SG, 1, lambda e: e.iota(A_io[:, :], [[-1, TCS]], base=TCS - 1,
                                           channel_multiplier=0,
                                           allow_small_or_imprecise_dtypes=True))
    P.wait("vector", SG)
    for fam in range(2):
        for kt in range(KT):
            i = fam * 8 + kt
            P.do("vector", SV, 1, lambda e, i=i: e.tensor_scalar_mul(
                A_s1[:, :], A_io[:, :], dcols[:, 16 + i:17 + i]))
            P.wait("scalar", SV)
            P.do("scalar", SS, 1, lambda e, i=i: e.activation(
                A_pow[:, i * TCS:(i + 1) * TCS], A_s1[:, :], ACT.Exp))
    csum_tok = {}
    for m in range(24):
        if m >= 2 and (m - 2) in csum_tok:
            P.wait("sync", ST, csum_tok[m - 2])
        wtok = ld(A_w[:, (m % 2) * 1024:(m % 2) * 1024 + 1024],
                  wq_in[:, m, :, :].rearrange("p k j -> p (k j)"), SW[m % 2])
        P.wait("tensor", SW[m % 2], wtok)
        P.bank_wait(2)

        def fmm(e, m=m):
            for k in range(KT):
                mm = e.matmul(pp[2][0:1, 0:128], C_onec[:, 0:1],
                              A_w[:, (m % 2) * 1024 + k * 128:(m % 2) * 1024 + k * 128 + 128],
                              start=(k == 0), stop=(k == KT - 1))
            return mm
        P.do("tensor", ST, 1, fmm)
        csum_tok[m] = ST.n
        P.wait("vector", ST)
        P.do("vector", SV, 1, lambda e, m=m: e.tensor_copy(
            crow[0:1, m * 128:(m + 1) * 128], pp[2][0:1, 0:128]))
        P.bank_set(2, SV)

    qkv_last = {}
    for tc in range(NTC):
        t0 = tc * TCS
        ab = (tc % 2) * KT * TCS
        if tc == 0:
            P.block()
            fence_dram()
        else:
            P.wait("sync", SD)
            if tc >= 2:
                P.wait("sync", ST, qkv_last[tc - 2])
        for kt in range(KT):
            ld(A_x[:, ab + kt * TCS:ab + (kt + 1) * TCS],
               x_in[kt * 128:(kt + 1) * 128, t0:t0 + TCS])
        dxn = SD.n
        P.wait("tensor", SD, dxn)
        P.wait("vector", SD, dxn)
        ln_stats(A_x, A_sq, xb=ab)
        P.wait("tensor", SV)
        P.bank_wait(2)
        P.do("tensor", ST, 1, lambda e: e.matmul(pp[2][:, 0:TCS], C_ones1[0:1, :],
                                                 rowb[0:1, RW_R:RW_R + TCS],
                                                 start=True, stop=True))
        P.wait("vector", ST)
        P.do("vector", SV, 1, lambda e: e.tensor_copy(A_rb[:, :], pp[2][:, 0:TCS]))
        P.bank_set(2, SV)

        if tc > 0:
            P.wait("vector", SQ1)
            wconsumed = {-2: prev_wcons[22], -1: prev_wcons[23]}
        else:
            wconsumed = {}
        for m in range(24):
            if (m - 2) in wconsumed:
                P.wait("sync", ST, wconsumed[m - 2])
            wtok = ld(A_w[:, (m % 2) * 1024:(m % 2) * 1024 + 1024],
                      wq_in[:, m, :, :].rearrange("p k j -> p (k j)"), SW[m % 2])
            pb = 3 + (m % 2)
            P.wait("tensor", SW[m % 2], wtok)
            P.bank_wait(pb)

            def fmm(e, m=m, pb=pb, ab=ab):
                for k in range(KT):
                    e.matmul(pp[pb][:, 0:TCS],
                             A_w[:, (m % 2) * 1024 + k * 128:(m % 2) * 1024 + k * 128 + 128],
                             A_x[:, ab + k * TCS:ab + (k + 1) * TCS],
                             start=(k == 0), stop=False)
                return e.matmul(pp[pb][:, 0:TCS], crow[0:1, m * 128:(m + 1) * 128],
                                rowb[0:1, RW_MN:RW_MN + TCS], start=False, stop=True)
            P.do("tensor", ST, 1, fmm)
            wconsumed[m] = ST.n
            role = "q" if m < 8 else ("k" if m < 16 else "v")
            j = m % 8
            P.wait("vector", ST)
            if role in ("q", "k"):
                dst = A_q if role == "q" else A_k
                P.do("vector", SV, 1, lambda e, pb=pb: (
                    e.tensor_mul(A_u[:, :], pp[pb][:, 0:TCS], A_rb[:, :]),
                    e.tensor_scalar_min(A_mn[:, :], A_u[:, :], 0.0))[1])
                P.bank_set(pb, SV)
                P.wait("scalar", SV)
                P.do("scalar", SS, 1, lambda e: e.activation(A_e[:, :], A_mn[:, :],
                                                             ACT.Exp))
                P.wait("vector", SS)
                P.do("vector", SV, 1, lambda e, dst=dst, j=j:
                     e.scalar_tensor_tensor(dst[:, j * TCS:(j + 1) * TCS],
                                            A_u[:, :], 0.0, A_e[:, :],
                                            ALU.max, ALU.add))
                P.wait("gpsimd", SV)
                dst_hbm = stq if role == "q" else stk
                P.do("gpsimd", SQ1, 16,
                     lambda e, j=j, t0=t0, dst=dst, dst_hbm=dst_hbm:
                     e.dma_start(out=dst_hbm[j * 128:(j + 1) * 128, t0:t0 + TCS],
                                 in_=dst[:, j * TCS:(j + 1) * TCS]))
            else:
                P.do("vector", SV, 1, lambda e, pb=pb, j=j: (
                    e.tensor_mul(A_u[:, :], pp[pb][:, 0:TCS], A_rb[:, :]),
                    e.tensor_mul(A_kv[:, j * TCS:(j + 1) * TCS], A_u[:, :],
                                 A_k[:, j * TCS:(j + 1) * TCS]))[1])
                P.bank_set(pb, SV)
                P.wait("gpsimd", SV)
                P.do("gpsimd", SQ1, 16, lambda e, j=j, t0=t0:
                     e.dma_start(out=stkv[j * 128:(j + 1) * 128, t0:t0 + TCS],
                                 in_=A_kv[:, j * TCS:(j + 1) * TCS]))
                for fam in range(2):
                    for vi, src in ((0, A_kv), (1, A_k)):
                        i = fam * 8 + j
                        col = (vi + 2 * fam) * 8 + j
                        sc_src = 32 * (tc % 2)
                        sc_dst = 32 * ((tc + 1) % 2)
                        P.do("vector", SV, 1,
                             lambda e, i=i, col=col, src=src, j=j,
                             sc_src=sc_src, sc_dst=sc_dst: (
                                 e.tensor_mul(A_s1[:, :],
                                              src[:, j * TCS:(j + 1) * TCS],
                                              A_pow[:, i * TCS:(i + 1) * TCS]),
                                 e.reduce_sum(A_mn[:, 0:1], A_s1[:, :], axis=AX.X),
                                 e.tensor_scalar(
                                     statec[:, sc_dst + col:sc_dst + col + 1],
                                     statec[:, sc_src + col:sc_src + col + 1],
                                     dcols[:, 32 + i:33 + i], A_mn[:, 0:1],
                                     ALU.mult, ALU.add))[2])
        qkv_last[tc] = wconsumed[23]
        prev_wcons = wconsumed

    es1.close()

    es2 = ExitStack()
    B_q = sb("B_q", [128, KT * TCS], st=es2)
    B_k = sb("B_k", [128, KT * TCS], st=es2)
    B_kv = sb("B_kv", [128, KT * TCS], st=es2)
    B_rkv = sb("B_rkv", [128, KT * TCS], st=es2)
    B_rks = sb("B_rks", [128, KT * TCS], st=es2)
    B_of = sb("B_of", [128, KT * TCS], st=es2)
    B_os = sb("B_os", [128, KT * TCS], st=es2)
    B_x = sb("B_x", [128, KT * TCS], st=es2)
    B_x1 = sb("B_x1", [128, KT * TCS], st=es2)
    B_w = sb("B_w", [128, 2 * 2048], st=es2)
    B_rd = sb("B_rd", [16, 4 * TCS], st=es2)
    B_t = sb("B_t", [128, 2 * TCS], st=es2)

    P.block()
    P.wait("sync", SV)
    ld(state_b.ap(), statec[:, 0:32], SST)
    P.wait("gpsimd", SST)
    P.wait("sync", SST)
    P.wait("sync", SD)
    P.wait("sync", SQ1)
    P.do("gpsimd", SCC, 1, lambda e: e.collective_compute(
        "AllGather", ALU.bypass, replica_groups=[list(range(NCOR))],
        ins=[state_b.ap().opt()], outs=[allstate.ap().opt()]))
    P.wait("gpsimd", SCC)
    P.do("gpsimd", SGD, 16, lambda e: e.indirect_dma_start(
        out=sstate[:, :], out_offset=None,
        in_=allstate.ap(), in_offset=IndirectOffsetOnAxis(pi_sb[:, :], 0)))
    P.wait("vector", SGD)
    P.do("vector", SV, 1, lambda e: e.tensor_scalar_mul(carry[:, :],
                                                        sstate[:, :], C_msk[:, :]))
    if DEBUG:
        dbg_carry = scratch("dbg_carry", [128, 32], dbg=True)
        dbg_sst = scratch("dbg_sst", [128, 33], dbg=True)
        P.wait("sync", SV)
        ld(dbg_carry.ap(), carry[:, :], SST)
        ld(dbg_sst[0:128, 0:32], sstate[:, :], SST)
        P.do("sync", SST, 16, lambda e: e.dma_start(
            out=dbg_sst[0:128, 32:33], in_=C_msk[:, :],
            allow_slow_non_contiguous=True))


    for kt in range(KT):
        ld(B_q[:, kt * TCS:(kt + 1) * TCS], stq[kt * 128:(kt + 1) * 128, 0:TCS])
        ld(B_k[:, kt * TCS:(kt + 1) * TCS], stk[kt * 128:(kt + 1) * 128, 0:TCS])
        ld(B_kv[:, kt * TCS:(kt + 1) * TCS], stkv[kt * 128:(kt + 1) * 128, 0:TCS])
        ld(B_x[:, kt * TCS:(kt + 1) * TCS], x_in[kt * 128:(kt + 1) * 128, 0:TCS])
    ld(M_wr[:, :], wr_in.ap().rearrange("p k j -> p (k j)"))
    P.do("gpsimd", SG, 1, lambda e: e.iota(M_io8[:, :], [[1, 8]], base=0,
                                           channel_multiplier=0,
                                           allow_small_or_imprecise_dtypes=True))
    P.do("vector", SV, 1, lambda e: e.memset(M_base[:, :], 0.0))

    scn_tok = {}
    reload_sd = {}
    pjev_tok = {}

    def emit_reloads(tc):
        t0 = tc * TCS
        P.wait("sync", SD)
        P.wait("sync", SV, pjev_tok[tc - 1])
        for kt in range(KT):
            ld(B_q[:, kt * TCS:(kt + 1) * TCS], stq[kt * 128:(kt + 1) * 128, t0:t0 + TCS])
            ld(B_k[:, kt * TCS:(kt + 1) * TCS], stk[kt * 128:(kt + 1) * 128, t0:t0 + TCS])
            ld(B_kv[:, kt * TCS:(kt + 1) * TCS], stkv[kt * 128:(kt + 1) * 128, t0:t0 + TCS])
            ld(B_x[:, kt * TCS:(kt + 1) * TCS], x_in[kt * 128:(kt + 1) * 128, t0:t0 + TCS])
        reload_sd[tc] = SD.n

    def emit_scans(tc, fam):
        for kt in range(KT):
            i = fam * 8 + kt
            ckv = (0 + 2 * fam) * 8 + kt
            cks = (1 + 2 * fam) * 8 + kt
            if kt == 0 and fam == 0:
                P.wait("vector", SD, reload_sd[tc])
            P.do("vector", SV, 1, lambda e, i=i, kt=kt, ckv=ckv:
                 e.tensor_tensor_scan(B_rkv[:, kt * TCS:(kt + 1) * TCS],
                                      bass.AP(dcb[i], 0, [[1, 128], [0, TCS]]),
                                      B_kv[:, kt * TCS:(kt + 1) * TCS],
                                      carry[:, ckv:ckv + 1], ALU.mult, ALU.add))
            P.wait("vector", SV)
            P.do("vector", SV, 1, lambda e, kt=kt, ckv=ckv:
                 e.tensor_copy(carry[:, ckv:ckv + 1],
                               B_rkv[:, kt * TCS + TCS - 1:kt * TCS + TCS]))
            P.do("vector", SV, 1, lambda e, i=i, kt=kt, cks=cks:
                 e.tensor_tensor_scan(B_rks[:, kt * TCS:(kt + 1) * TCS],
                                      bass.AP(dcb[i], 0, [[1, 128], [0, TCS]]),
                                      B_k[:, kt * TCS:(kt + 1) * TCS],
                                      carry[:, cks:cks + 1], ALU.mult, ALU.add))
            P.wait("vector", SV)
            P.do("vector", SV, 1, lambda e, kt=kt, cks=cks: (
                e.tensor_copy(carry[:, cks:cks + 1],
                              B_rks[:, kt * TCS + TCS - 1:kt * TCS + TCS]),
                e.tensor_mul(B_rks[:, kt * TCS:(kt + 1) * TCS],
                             B_q[:, kt * TCS:(kt + 1) * TCS],
                             B_rks[:, kt * TCS:(kt + 1) * TCS]))[1])
            scn_tok[(tc, fam, kt)] = SV.n

    def emit_den(tc, fam):
        outdst = B_of if fam == 0 else B_os
        P.bank_wait(0)
        for kt in range(KT):
            P.wait("tensor", SV, scn_tok[(tc, fam, kt)])
            P.do("tensor", ST if kt == KT - 1 else None, 1,
                 lambda e, kt=kt: e.matmul(pp[0][0:16, 0:TCS],
                                           C_sel[:, kt * 16:(kt + 1) * 16],
                                           B_rks[:, kt * TCS:(kt + 1) * TCS],
                                           start=(kt == 0), stop=(kt == KT - 1)))
        fb = fam * TCS
        P.wait("vector", ST)
        P.do("vector", SV, 1, lambda e, fb=fb: (
            e.tensor_scalar_max(B_rd[:, 2 * TCS + fb:2 * TCS + fb + TCS],
                                pp[0][0:16, 0:TCS], 1e-6),
            e.reciprocal(B_rd[:, fb:fb + TCS],
                         B_rd[:, 2 * TCS + fb:2 * TCS + fb + TCS]))[1])
        P.bank_set(0, SV)
        for kt in range(KT):
            pb = 1 + (kt % 2)
            P.wait("tensor", SV, SV.n)
            P.bank_wait(pb)
            P.do("tensor", ST, 1, lambda e, kt=kt, fb=fb, pb=pb:
                 e.matmul(pp[pb][:, 0:TCS], C_selt[:, kt * 128:(kt + 1) * 128],
                          B_rd[0:16, fb:fb + TCS], start=True, stop=True))
            P.wait("vector", ST)
            P.do("vector", SV, 1, lambda e, kt=kt, pb=pb, outdst=outdst: (
                e.tensor_mul(B_rks[:, kt * TCS:(kt + 1) * TCS],
                             B_q[:, kt * TCS:(kt + 1) * TCS],
                             B_rkv[:, kt * TCS:(kt + 1) * TCS]),
                e.tensor_mul(outdst[:, kt * TCS:(kt + 1) * TCS],
                             B_rks[:, kt * TCS:(kt + 1) * TCS],
                             pp[pb][:, 0:TCS]))[1])
            P.bank_set(pb, SV)

    for tc in range(NTC):
        t0 = tc * TCS
        if tc == 0:
            P.block()
            P.wait("vector", SD)
            reload_sd[0] = SD.n
            emit_scans(0, 0)
        emit_den(tc, 0)
        emit_scans(tc, 1)
        emit_den(tc, 1)

        wconsumed = ({} if tc == 0 else
                     {-2: prev_pj[6], -1: prev_pj[7]})
        for m in range(8):
            if (m - 2) in wconsumed:
                P.wait("sync", ST, wconsumed[m - 2])
            wtok = ld(B_w[:, (m % 2) * 2048:(m % 2) * 2048 + 2048],
                      wo_in[:, m, :, :].rearrange("p k j -> p (k j)"), SW[m % 2])
            pb = 3 + (m % 2)
            P.wait("tensor", SW[m % 2], wtok)
            P.bank_wait(pb)

            def fpj(e, m=m, pb=pb):
                for k in range(16):
                    src = B_of if k < 8 else B_os
                    mm = e.matmul(pp[pb][:, 0:TCS],
                                  B_w[:, (m % 2) * 2048 + k * 128:(m % 2) * 2048 + k * 128 + 128],
                                  src[:, (k % 8) * TCS:(k % 8 + 1) * TCS],
                                  start=(k == 0), stop=(k == 15))
                return mm
            P.do("tensor", ST, 1, fpj)
            wconsumed[m] = ST.n
            P.wait("vector", ST)
            P.wait("vector", SP[m % 2])
            P.do("vector", SV, 1, lambda e, m=m, pb=pb:
                 e.tensor_add(B_x1[:, m * TCS:(m + 1) * TCS],
                              pp[pb][:, 0:TCS], B_x[:, m * TCS:(m + 1) * TCS]))
            P.bank_set(pb, SV)
            P.wait("gpsimd", SV)
            P.do("gpsimd", SP[m % 2], 16, lambda e, m=m, t0=t0:
                 e.dma_start(out=x1h[m * 128:(m + 1) * 128, t0:t0 + TCS],
                             in_=B_x1[:, m * TCS:(m + 1) * TCS]))
        prev_pj = wconsumed
        pjev_tok[tc] = SV.n

        if tc + 1 < NTC:
            emit_reloads(tc + 1)
            emit_scans(tc + 1, 0)

        P.wait("tensor", SV, pjev_tok[tc])
        ln_stats(B_x1, B_t, bs0=1, bs1=2)
        ln_tok = SV.n
        P.wait("tensor", SV, ln_tok)
        P.bank_wait(0)
        P.bank_wait(3)
        P.do("tensor", None, 1, lambda e: e.matmul(pp[0][:, 0:TCS], C_ones1[0:1, :],
                                                   rowb[0:1, RW_R:RW_R + TCS],
                                                   start=True, stop=True))
        P.do("tensor", ST, 1, lambda e: e.matmul(pp[3][:, 0:TCS], C_ones1[0:1, :],
                                                 rowb[0:1, RW_MN:RW_MN + TCS],
                                                 start=True, stop=True))
        P.wait("vector", ST)
        for kt in range(KT):
            P.do("vector", SV, 1, lambda e, kt=kt: (
                e.tensor_add(B_t[:, 0:TCS],
                             B_x1[:, kt * TCS:(kt + 1) * TCS], pp[3][:, 0:TCS]),
                e.tensor_mul(B_os[:, kt * TCS:(kt + 1) * TCS],
                             B_t[:, 0:TCS], pp[0][:, 0:TCS]))[1])
        P.bank_set(0, SV)
        P.bank_set(3, SV)
        xnn = SV.n
        if tc == 0:
            P.wait("vector", SG)
        for tt in range(4):
            g = tc * 4 + tt
            c0 = tt * 128
            P.wait("tensor", SV, xnn)
            P.bank_wait(4)

            def frt(e, c0=c0):
                for k in range(KT):
                    mm = e.matmul(pp[4][0:128, 0:8],
                                  B_os[:, k * TCS + c0:k * TCS + c0 + 128],
                                  M_wr[:, k * 8:(k + 1) * 8],
                                  start=(k == 0), stop=(k == KT - 1))
                return mm
            P.do("tensor", ST, 1, frt)
            P.wait("vector", ST)
            P.do("vector", SV, 1, lambda e:
                 e.tensor_copy(M_lg[:, :], pp[4][0:128, 0:8]))
            P.bank_set(4, SV)
            P.wait("vector", SV)
            P.do("vector", SV, 1, lambda e:
                 e.reduce_max(M_col[:, 0:1], M_lg[:, :], axis=AX.X))
            P.wait("vector", SV)
            P.do("vector", SV, 1, lambda e:
                 e.tensor_scalar(M_sc[:, :], M_lg[:, :], M_col[:, 0:1], None,
                                 ALU.subtract))
            P.wait("scalar", SV)
            P.do("scalar", SS, 1, lambda e: e.activation(M_sc2[:, :], M_sc[:, :],
                                                         ACT.Exp))
            P.wait("vector", SS)
            P.do("vector", SV, 1, lambda e:
                 e.reduce_sum(M_col[:, 1:2], M_sc2[:, :], axis=AX.X))
            P.wait("vector", SV)
            P.do("vector", SV, 1, lambda e, g=g:
                 e.reciprocal(S_pmax[:, g:g + 1], M_col[:, 1:2]))
            P.do("vector", SV, 1, lambda e:
                 e.tensor_scalar(M_oh[:, :], M_lg[:, :], M_col[:, 0:1], None,
                                 ALU.is_ge))
            P.do("vector", SV, 1, lambda e:
                 e.tensor_scalar_add(M_sc2[:, :], M_io8[:, :], -8.0))
            P.wait("vector", SV)
            P.do("vector", SV, 1, lambda e:
                 e.tensor_mul(M_sc[:, :], M_sc2[:, :], M_oh[:, :]))
            P.wait("vector", SV)
            P.do("vector", SV, 1, lambda e:
                 e.tensor_reduce(M_col[:, 2:3], M_sc[:, :], axis=AX.X, op=ALU.min))
            P.wait("vector", SV)
            P.do("vector", SV, 1, lambda e, g=g:
                 e.tensor_scalar_add(S_idx[:, g:g + 1], M_col[:, 2:3], 8.0))
            P.wait("vector", SV)
            P.do("vector", SV, 1, lambda e, g=g:
                 e.tensor_scalar(M_oh[:, :], M_io8[:, :], S_idx[:, g:g + 1], None,
                                 ALU.is_equal))
            P.wait("tensor", SV)
            P.bank_wait(5)
            if g == 0:
                P.bank_wait(6)
            bc, bn = 8 * (g % 2), 8 * ((g + 1) % 2)
            P.do("tensor", ST, 1, lambda e, g=g, bc=bc: (
                e.matmul(pp[5][0:128, 0:8], C_ut[:, :], M_oh[:, :],
                         start=True, stop=False),
                e.matmul(pp[5][0:128, 0:8], C_ones1[0:1, :],
                         M_base[0:1, bc:bc + 8], start=False, stop=True),
                e.matmul(pp[6][0:1, g * 8:g * 8 + 8], C_onec[:, 0:1], M_oh[:, :],
                         start=True, stop=True))[2])
            P.wait("vector", ST)
            P.do("vector", SV, 1, lambda e:
                 e.tensor_copy(M_cum[:, :], pp[5][0:128, 0:8]))
            P.do("vector", SV, 1, lambda e, g=g:
                 e.tensor_copy(M_ct[0:1, :], pp[6][0:1, g * 8:g * 8 + 8]))
            P.bank_set(5, SV)
            P.wait("vector", SV)
            P.do("vector", SV, 1, lambda e:
                 e.tensor_mul(M_sc[:, :], M_cum[:, :], M_oh[:, :]))
            P.wait("vector", SV)
            P.do("vector", SV, 1, lambda e, g=g:
                 e.reduce_sum(S_pos[:, g:g + 1], M_sc[:, :], axis=AX.X))
            P.do("vector", SV, 1, lambda e, g=g, bc=bc, bn=bn:
                 e.tensor_add(M_base[0:1, bn:bn + 8], M_base[0:1, bc:bc + 8],
                              M_ct[0:1, :]))
            for kq in range(2):
                P.wait("tensor", SV, xnn)
                P.bank_wait(7)
                P.do("tensor", ST, 1, lambda e, c0=c0, kq=kq: (
                    [e.transpose(pp[7][:, (k % 4) * 128:(k % 4) * 128 + 128],
                                 B_os[:, k * TCS + c0:k * TCS + c0 + 128],
                                 C_id[:, :]) for k in range(4 * kq, 4 * kq + 4)][-1]))
                P.wait("vector", ST)
                P.do("vector", SV, 1, lambda e, kq=kq: e.tensor_copy(
                    B_t[:, kq * TCS:kq * TCS + TCS], pp[7][:, 0:TCS]))
                P.bank_set(7, SV)
            P.wait("sync", SV)
            ld(xnth[g * 128:(g + 1) * 128, :], B_t[:, 0:1024], SB0)
            P.wait("vector", SB0)
            if tc < NTC - 1:
                P.wait("gpsimd", SB0)
                P.do("gpsimd", SCC, 1, lambda e, g=g: e.collective_compute(
                    "AllGather", ALU.bypass,
                    replica_groups=[list(range(NCOR))],
                    ins=[bass.AP(xnth, g * 128 * HID, [[1, 128 * HID]]).opt()],
                    outs=[bass.AP(allxn, g * NCOR * 128 * HID,
                                  [[1, NCOR * 128 * HID]]).opt()]))
            elif tt == 3:
                P.wait("gpsimd", SB0)
                P.do("gpsimd", SCC, 1, lambda e: e.collective_compute(
                    "AllGather", ALU.bypass,
                    replica_groups=[list(range(NCOR))],
                    ins=[bass.AP(xnth, 12 * 128 * HID,
                                 [[1, TCS * HID]]).opt()],
                    outs=[bass.AP(allxn, 12 * NCOR * 128 * HID,
                                  [[1, NCOR * TCS * HID]]).opt()]))
        tr_prev = ST.n

    es2.close()

    es3 = ExitStack()
    M_x1 = sb("M_x1", [128, KT * TCS], st=es3)
    M_xn = sb("M_xn", [128, KT * TCS], st=es3)
    M_t = sb("M_t", [128, 1024], st=es3)
    M_t2 = sb("M_t2", [128, 1024], st=es3)
    M_t3 = sb("M_t3", [128, 1024], st=es3)
    idsro = sb("idsro", [1, 2560], st=es3)
    idsri = sb("idsri", [1, 2560], I32, st=es3)

    P.block()
    P.wait("sync", SD)
    if DEBUG:
        dbg_rt = scratch("dbg_rt", [128, 64], dbg=True)
        P.wait("sync", SV)
        ld(dbg_rt[0:128, 0:16], S_idx[:, :], SST)
        ld(dbg_rt[0:128, 16:32], S_pos[:, :], SST)
        ld(dbg_rt[0:128, 32:48], S_pmax[:, :], SST)
        ld(dbg_rt[0:1, 48:64], M_base[0:1, 0:16], SST)
    P.wait("sync", SV)
    ld(mcol[0:8, 0:1], M_base[0:1, 0:8])
    P.wait("tensor", SD)
    P.bank_wait(0)
    P.do("tensor", ST, 1, lambda e: e.matmul(pp[0][0:1, 0:8], mcol[0:8, 0:1],
                                             C_su9[:, 0:8], start=True, stop=True))
    P.wait("vector", ST)
    P.do("vector", SV, 1, lambda e: e.tensor_copy(M_lb[0:1, :], pp[0][0:1, 0:8]))
    P.bank_set(0, SV)
    P.wait("tensor", SV)
    P.bank_wait(1)
    P.do("tensor", ST, 1, lambda e: e.matmul(pp[1][0:128, 0:8], C_ones1[0:1, :],
                                             M_lb[0:1, :], start=True, stop=True))
    P.wait("vector", ST)
    P.do("vector", SV, 1, lambda e: e.tensor_copy(M_lbbc[:, :], pp[1][0:128, 0:8]))
    P.bank_set(1, SV)
    def lbsel16(lbb):
        for e_ in range(8):
            P.do("vector", SV, 1, lambda e, e_=e_: e.tensor_scalar(
                R_a[:, e_ * 16:(e_ + 1) * 16], S_idx[:, :], float(e_), None,
                ALU.is_equal))
        P.wait("vector", SV)
        for e_ in range(8):
            P.do("vector", SV, 1, lambda e, e_=e_, lbb=lbb: e.tensor_scalar(
                R_b[:, e_ * 16:(e_ + 1) * 16], R_a[:, e_ * 16:(e_ + 1) * 16],
                lbb[:, e_:e_ + 1], None, ALU.mult))
        P.wait("vector", SV)
        for q in range(4):
            P.do("vector", SV, 1, lambda e, q=q: e.tensor_add(
                R_a[:, q * 16:(q + 1) * 16],
                R_b[:, 2 * q * 16:(2 * q + 1) * 16],
                R_b[:, (2 * q + 1) * 16:(2 * q + 2) * 16]))
        P.wait("vector", SV)
        for q in range(2):
            P.do("vector", SV, 1, lambda e, q=q: e.tensor_add(
                R_b[:, q * 16:(q + 1) * 16],
                R_a[:, 2 * q * 16:(2 * q + 1) * 16],
                R_a[:, (2 * q + 1) * 16:(2 * q + 2) * 16]))
        P.wait("vector", SV)
        P.do("vector", SV, 1, lambda e: e.tensor_add(R_a[:, 0:16], R_b[:, 0:16],
                                                     R_b[:, 16:32]))
        P.wait("vector", SV)

    lbsel16(M_lbbc)
    P.do("vector", SV, 1, lambda e: e.tensor_add(R_b[:, 0:16], R_a[:, 0:16],
                                                 S_pos[:, :]))
    P.wait("vector", SV)
    P.do("vector", SV, 1, lambda e: e.tensor_scalar_add(R_a[:, 16:32],
                                                        R_b[:, 0:16], -1.0))
    P.wait("vector", SV)
    P.do("vector", SV, 1, lambda e: e.tensor_copy(S_srt[:, :], R_a[:, 16:32]))
    posn = SV.n
    P.wait("gpsimd", SV, posn)
    for g in range(16):
        P.do("gpsimd", GP0, 16, lambda e, g=g: e.indirect_dma_start(
            out=ivp.ap(), out_offset=IndirectOffsetOnAxis(S_srt[:, g:g + 1], 0),
            in_=G_tk[:, g:g + 1], in_offset=None))
    P.do("gpsimd", SQ1, 16, lambda e: e.dma_start(
        out=bass.AP(ivp, TL, [[1, 8]]), in_=M_base[0:1, 0:8]))
    P.wait("gpsimd", GP0)
    P.wait("gpsimd", SQ1)
    P.do("gpsimd", SCC, 1, lambda e: e.collective_compute(
        "AllGather", ALU.bypass, replica_groups=[list(range(NCOR))],
        ins=[ivp.ap().opt()], outs=[allivp.ap().opt()]))
    P.wait("gpsimd", SCC)
    P.wait("sync", SCC)

    P.block()
    cnt_ap = bass.AP(allivp, TL, [[TLP8, 8], [1, 8]])
    cntt_ap = bass.AP(allivp, TL, [[1, 8], [TLP8, 8], [1, 1]])
    ld(cntsb[:, :], cnt_ap)
    P.do("sync", SD, 16, lambda e: e.dma_start(out=cnttsb[:, :], in_=cntt_ap,
                                               allow_slow_non_contiguous=True))
    P.wait("tensor", SD)
    P.bank_wait(0)
    P.do("tensor", None, 1, lambda e: e.matmul(pp[0][0:8, 0:1], cnttsb[:, :],
                                               C_eown[:, :], start=True, stop=True))
    P.do("tensor", None, 1, lambda e: e.matmul(pp[0][0:8, 1:2], cnttsb[:, :],
                                               C_ltm[:, :], start=True, stop=True))
    P.do("tensor", ST, 1, lambda e: e.matmul(pp[0][0:1, 8:16], C_ltm[:, :],
                                             cntsb[:, :], start=True, stop=True))
    P.wait("vector", ST)
    P.do("vector", SV, 1, lambda e: (
        e.tensor_copy(mcol[0:8, 1:3], pp[0][0:8, 0:2]),
        e.tensor_copy(M_lb[0:1, :], pp[0][0:1, 8:16]))[1])
    P.bank_set(0, SV)
    P.wait("tensor", SV)
    P.bank_wait(1)
    P.do("tensor", None, 1, lambda e: e.matmul(pp[1][0:8, 0:1], C_su9[:, 0:8],
                                               mcol[0:8, 1:2], start=True, stop=True))
    P.do("tensor", ST, 1, lambda e: e.matmul(pp[1][0:8, 1:2], C_su9[:, 1:9],
                                             mcol[0:8, 1:2], start=True, stop=True))
    P.wait("vector", ST)
    P.do("vector", SV, 1, lambda e: e.tensor_copy(mcol[0:8, 3:5], pp[1][0:8, 0:2]))
    P.bank_set(1, SV)
    P.wait("sync", SV)
    ld(fill0[0:1, 0:1], mcol[7:8, 4:5])
    P.wait("vector", SD)
    P.do("vector", SV, 1, lambda e:
         e.tensor_scalar_min(fill0[0:1, 1:2], fill0[0:1, 0:1], float(CAP)))
    P.wait("tensor", SV)
    P.bank_wait(4)
    P.do("tensor", ST, 1, lambda e: e.matmul(pp[4][0:128, 0:1], C_ones1[0:1, :],
                                             fill0[0:1, 1:2], start=True, stop=True))
    P.wait("vector", ST)
    P.do("vector", SV, 1, lambda e: e.tensor_copy(fillc[:, 0:1], pp[4][0:128, 0:1]))
    P.bank_set(4, SV)
    P.do("gpsimd", SG, 1, lambda e: e.iota(M_col[0:8, 2:3], [[1, 1]], base=0,
                                           channel_multiplier=TLP8,
                                           allow_small_or_imprecise_dtypes=True))
    P.wait("vector", SG)
    P.do("vector", SV, 1, lambda e:
         e.tensor_add(M_col[0:8, 3:4], M_col[0:8, 2:3], mcol[0:8, 2:3]))
    P.wait("vector", SV)
    P.do("vector", SV, 1, lambda e:
         e.tensor_sub(M_col[0:8, 2:3], M_col[0:8, 3:4], mcol[0:8, 3:4]))
    P.do("gpsimd", SG, 1, lambda e: e.iota(idsro[0:1, :], [[1, 2560]], base=0,
                                           channel_multiplier=0,
                                           allow_small_or_imprecise_dtypes=True))
    M_st = M_xn
    seg_tok = None
    for ch in range(5):
        s0 = ch * 512
        if seg_tok is not None:
            P.wait("gpsimd", SV, seg_tok)
        P.do("gpsimd", SG, 1, lambda e, s0=s0: e.iota(
            M_st[0:8, 0:512], [[1, 512]], base=s0, channel_multiplier=0,
            allow_small_or_imprecise_dtypes=True))
        P.wait("vector", SG)
        P.do("vector", SV, 1, lambda e: (
            e.tensor_scalar(M_st[0:8, 512:1024], M_st[0:8, 0:512], mcol[0:8, 3:4],
                            None, ALU.is_ge),
            e.tensor_scalar(M_st[0:8, 1536:2048], M_st[0:8, 0:512], mcol[0:8, 4:5],
                            None, ALU.is_ge))[1])
        P.wait("vector", SV)
        P.do("vector", SV, 1, lambda e:
             e.tensor_sub(M_st[0:8, 1024:1536], M_st[0:8, 512:1024],
                          M_st[0:8, 1536:2048]))
        seg_tok = SV.n
        P.wait("tensor", SV)
        P.bank_wait(2)
        P.do("tensor", ST, 1, lambda e, s0=s0: (
            e.matmul(pp[2][0:1, 0:512], M_col[0:8, 2:3], M_st[0:8, 1024:1536],
                     start=True, stop=False),
            e.matmul(pp[2][0:1, 0:512], C_ones1[0:1, 0:1], idsro[0:1, s0:s0 + 512],
                     start=False, stop=True))[1])
        P.wait("vector", ST)
        P.do("vector", SV, 1, lambda e, s0=s0:
             e.tensor_scalar(M_st[0:1, 2048:2560], idsro[0:1, s0:s0 + 512],
                             fill0[0:1, 1:2], None, ALU.is_lt))
        P.wait("vector", SV)
        P.do("vector", SV, 1, lambda e:
             e.tensor_mul(M_st[0:1, 2560:3072], pp[2][0:1, 0:512],
                          M_st[0:1, 2048:2560]))
        P.wait("vector", SV)
        P.do("vector", SV, 1, lambda e, s0=s0:
             e.tensor_copy(idsri[0:1, s0:s0 + 512], M_st[0:1, 2560:3072]))
        P.bank_set(2, SV)
    P.wait("sync", SV)
    for gg in range(20):
        P.do("sync", SD, 16, lambda e, gg=gg: e.dma_start(
            out=ids32[:, gg:gg + 1], in_=idsri[0:1, gg * 128:(gg + 1) * 128]))
    P.wait("gpsimd", SD)
    for gg in range(20):
        P.do("gpsimd", GP1, 16, lambda e, gg=gg: e.indirect_dma_start(
            out=ivpv[:, gg:gg + 1], out_offset=None,
            in_=allivp.ap(), in_offset=IndirectOffsetOnAxis(ids32[:, gg:gg + 1],
                                                            0)))
    P.wait("vector", GP1)
    P.do("vector", SV, 1, lambda e: e.tensor_copy(S_gi32[:, :], ivpv[:, :]))
    P.do("gpsimd", SG, 1, lambda e: e.iota(scioc[:, :], [[128, 20]], base=0,
                                           channel_multiplier=1,
                                           allow_small_or_imprecise_dtypes=True))
    P.wait("vector", SG)
    P.do("vector", SV, 1, lambda e: e.tensor_scalar(vcol[:, :], scioc[:, :],
                                                    fillc[:, 0:1], None, ALU.is_lt))
    P.wait("tensor", SV)
    P.bank_wait(3)
    P.do("tensor", ST, 1, lambda e: e.matmul(pp[3][0:128, 0:8], C_ones1[0:1, :],
                                             M_lb[0:1, :], start=True, stop=True))
    P.wait("vector", ST)
    P.do("vector", SV, 1, lambda e: e.tensor_copy(M_lbbc[:, :], pp[3][0:128, 0:8]))
    P.bank_set(3, SV)
    lbsel16(M_lbbc)
    P.do("vector", SV, 1, lambda e: e.tensor_add(R_b[:, 0:16], R_a[:, 0:16],
                                                 S_pos[:, :]))
    P.wait("vector", SV)
    P.do("vector", SV, 1, lambda e: e.tensor_scalar_add(M_gb[:, :],
                                                        R_b[:, 0:16], -1.0))
    P.wait("vector", SV)
    P.do("vector", SV, 1, lambda e: e.tensor_scalar(
        S_kept[:, :], M_gb[:, :], float(CAP), None, ALU.is_lt))
    P.do("vector", SV, 1, lambda e: e.tensor_scalar_min(
        M_sl[:, :], M_gb[:, :], float(CAP - 1)))
    P.do("vector", SV, 1, lambda e: e.tensor_scalar_add(
        M_pm1[:, :], S_pmax[:, :], -1.0))
    P.wait("vector", SV)
    P.do("vector", SV, 1, lambda e: e.tensor_copy(S_j32[:, 0:16], M_sl[:, :]))
    P.do("vector", SV, 1, lambda e: e.tensor_copy(S_j32[:, 48:64], S_idx[:, :]))
    P.do("vector", SV, 1, lambda e: e.tensor_mul(R_a[:, 0:16], S_kept[:, :],
                                                 M_pm1[:, :]))
    P.wait("vector", SV)
    P.do("vector", SV, 1, lambda e: e.tensor_scalar(
        S_j32[:, 16:32], S_j32[:, 0:16], 9, None, ALU.arith_shift_right))
    P.do("vector", SV, 1, lambda e: e.tensor_scalar(
        S_j32[:, 32:48], S_j32[:, 0:16], 511, None, ALU.bitwise_and))
    P.do("vector", SV, 1, lambda e: e.tensor_scalar(
        S_j32[:, 80:96], S_j32[:, 48:64], FTC, None, ALU.mult))
    P.do("vector", SV, 1, lambda e: e.tensor_scalar_add(
        S_fsel[:, :], R_a[:, 0:16], 1.0))
    P.wait("vector", SV)
    P.do("vector", SV, 1, lambda e: e.tensor_scalar(
        S_j32[:, 64:80], S_j32[:, 16:32], NCOR * FTC, None, ALU.mult))
    P.wait("vector", SV)
    P.do("vector", SV, 1, lambda e: e.tensor_tensor(
        S_j32[:, 0:16], S_j32[:, 64:80], S_j32[:, 32:48], ALU.add))
    P.wait("vector", SV)
    P.do("vector", SV, 1, lambda e: e.tensor_tensor(
        S_rows[:, :], S_j32[:, 0:16], S_j32[:, 80:96], ALU.add))

    P.block()
    P.wait("gpsimd", SD)
    P.wait("gpsimd", SV)
    g_tok = {}
    tr_done = {}
    for g in range(20):
        buf = M_t if g % 2 == 0 else M_t2
        if g >= 2:
            P.wait("gpsimd", SV, g_tok[g - 2])
        P.do("gpsimd", GP[g % 2], 16, lambda e, g=g, buf=buf: e.indirect_dma_start(
            out=buf[:, 0:1024], out_offset=None,
            in_=allxn.ap(), in_offset=IndirectOffsetOnAxis(S_gi32[:, g:g + 1],
                                                           0)))
        P.wait("vector", GP[g % 2])
        if g >= 1:
            P.wait("vector", ST, tr_done[g - 1])
        P.do("vector", SV, 1, lambda e, g=g, buf=buf: e.tensor_scalar_mul(
            M_t3[:, 0:1024], buf[:, 0:1024], vcol[:, g:g + 1]))
        g_tok[g] = SV.n
        ebuf = M_x1
        for k in range(KT):
            pb = 4 + (k % 2)
            P.wait("tensor", SV)
            P.bank_wait(pb)
            P.do("tensor", ST, 1, lambda e, k=k, pb=pb: e.transpose(
                pp[pb][:, 0:128], M_t3[:, k * 128:(k + 1) * 128], C_id[:, :]))

            P.wait("vector", ST)
            P.do("vector", SV, 1, lambda e, k=k, pb=pb: e.tensor_copy(
                ebuf[:, k * 128:(k + 1) * 128], pp[pb][:, 0:128]))
            P.bank_set(pb, SV)
            if k == KT - 1:
                tr_done[g] = ST.n
        P.wait("sync", SV)
        for k in range(KT):
            ld(disph[k * 128:(k + 1) * 128, g * 128:(g + 1) * 128],
               ebuf[:, k * 128:(k + 1) * 128], SST)
        P.wait("vector", SST)

    es3.close()

    es4 = ExitStack()
    N_d = sb("N_d", [128, 2 * KT * FTC], st=es4)
    N_h = sb("N_h", [128, 32 * FTC], st=es4)
    N_w = sb("N_w", [128, 4 * 1024], st=es4)
    N_eo = sb("N_eo", [128, 8 * FTC], st=es4)
    N_t = sb("N_t", [128, 1024], st=es4)
    N_t2 = sb("N_t2", [128, 1024], st=es4)

    rd_tok = {}
    w1_last = {}
    eost_tok = {}
    si = 0
    for ftc in range(NFTC):
        f0 = ftc * FTC
        nb = (ftc % 2) * KT * FTC
        if ftc == 0:
            P.block()
            fence_dram()
        else:
            P.wait("sync", SD)
            if ftc >= 2:
                P.wait("sync", ST, w1_last[ftc - 2])
        for kt in range(KT):
            ld(N_d[:, nb + kt * FTC:nb + (kt + 1) * FTC],
               disph[kt * 128:(kt + 1) * 128, f0:f0 + FTC])
        dln = SD.n
        for dm in range(32):
            sl = si % 4
            if si - 4 in rd_tok:
                P.wait("sync", ST, rd_tok[si - 4])
            wtok = ld(N_w[:, sl * 1024:sl * 1024 + 1024],
                      w1_in[:, dm, :, :].rearrange("p k j -> p (k j)"), SW[sl])
            pb = dm % 2
            P.wait("tensor", SW[sl], wtok)
            P.wait("tensor", SD, dln)
            P.bank_wait(pb)

            def fh(e, sl=sl, pb=pb, nb=nb):
                for k in range(KT):
                    mm = e.matmul(pp[pb][:, 0:FTC],
                                  N_w[:, sl * 1024 + k * 128:sl * 1024 + k * 128 + 128],
                                  N_d[:, nb + k * FTC:nb + (k + 1) * FTC],
                                  start=(k == 0), stop=(k == KT - 1))
                return mm
            P.do("tensor", ST, 1, fh)
            rd_tok[si] = ST.n
            si += 1
            P.wait("vector", ST)
            P.do("vector", SV, 1, lambda e, dm=dm, pb=pb: e.tensor_scalar_max(
                N_h[:, dm * FTC:(dm + 1) * FTC], pp[pb][:, 0:FTC], 0.0))
            P.bank_set(pb, SV)
        w1_last[ftc] = ST.n
        hn = SV.n
        for m in range(8):
            pb = 2 + (m % 2)
            P.bank_wait(pb)
            for quart in range(4):
                sl = si % 4
                P.wait("sync", ST, rd_tok[si - 4])
                wtok = ld(N_w[:, sl * 1024:sl * 1024 + 1024],
                          w2_in[:, m, quart * 8:(quart + 1) * 8, :]
                          .rearrange("p k j -> p (k j)"), SW[sl])
                P.wait("tensor", SW[sl], wtok)
                if m == 0 and quart == 0:
                    P.wait("tensor", SV, hn)

                P.do("tensor", ST, 1, lambda e, quart=quart, pb=pb, sl=sl: (
                    [e.matmul(pp[pb][:, 0:FTC],
                              N_w[:, sl * 1024 + kk * 128:sl * 1024 + kk * 128 + 128],
                              N_h[:, (quart * 8 + kk) * FTC:(quart * 8 + kk + 1) * FTC],
                              start=(quart * 8 + kk == 0),
                              stop=(quart * 8 + kk == 31))
                     for kk in range(8)][-1]))
                rd_tok[si] = ST.n
                si += 1
            P.wait("vector", ST)
            P.do("vector", SV, 1, lambda e, m=m, pb=pb: e.tensor_copy(
                N_eo[:, m * FTC:(m + 1) * FTC], pp[pb][:, 0:FTC]))
            P.bank_set(pb, SV)
        eon = SV.n
        for tb in range(4):
            buf = N_t if tb % 2 == 0 else N_t2
            for m in range(8):
                pb = 6 + (m % 2)
                P.wait("tensor", SV, max(eon, SV.n))
                P.bank_wait(pb)
                P.do("tensor", ST, 1, lambda e, tb=tb, m=m, pb=pb: e.transpose(
                    pp[pb][:, 0:128],
                    N_eo[:, m * FTC + tb * 128:m * FTC + tb * 128 + 128],
                    C_id[:, :]))
                P.wait("vector", ST)
                if m == 0 and (tb % 2) in eost_tok:
                    P.wait("vector", SST, eost_tok[tb % 2])
                P.do("vector", SV, 1, lambda e, m=m, pb=pb, buf=buf:
                     e.tensor_copy(buf[:, m * 128:(m + 1) * 128],
                                   pp[pb][:, 0:128]))
                P.bank_set(pb, SV)
            P.wait("sync", SV)
            ld(eouth[f0 + tb * 128:f0 + tb * 128 + 128, :], buf[:, 0:1024], SST)
            eost_tok[tb % 2] = SST.n
        P.wait("gpsimd", SST)
        P.do("gpsimd", SCC, 1, lambda e, f0=f0, ftc=ftc: e.collective_compute(
            "AllGather", ALU.bypass, replica_groups=[list(range(NCOR))],
            ins=[bass.AP(eouth, f0 * HID, [[1, FTC * HID]]).opt()],
            outs=[bass.AP(alleout, ftc * NCOR * FTC * HID,
                          [[1, NCOR * FTC * HID]]).opt()]))

    P.block()
    P.wait("gpsimd", SCC)
    if DEBUG:
        P.wait("sync", SCC)
    if DEBUG:
        P.do("sync", SST, 16, lambda e: e.dma_start(out=dbg_alleout.ap(),
                                                    in_=alleout.ap()))

    es4.close()
    es5 = ExitStack()
    C_gt = sb("C_gt", [128, 2 * 1024], st=es5)
    C_f = sb("C_f", [128, 2 * 1024], st=es5)
    C_cx = sb("C_cx", [128, 2 * 1024], st=es5)
    C_y = sb("C_y", [128, 2 * 1024], st=es5)
    C_t = sb("C_t", [128, 1024], st=es5)
    C_t2 = sb("C_t2", [128, 1024], st=es5)

    gtok = {}
    ldtok = {}
    subtok = {}
    evactok = {}
    ytok = {}
    P.block()
    for g in range(16):
        sl = g % 2
        gb = sl * 1024
        if g >= 2:
            P.wait("gpsimd", SV, subtok[g - 2])
        P.do("gpsimd", GP[sl], 16, lambda e, g=g, gb=gb: e.indirect_dma_start(
            out=C_gt[:, gb:gb + 1024], out_offset=None,
            in_=alleout.ap(), in_offset=IndirectOffsetOnAxis(S_rows[:, g:g + 1],
                                                             0)))
        gtok[g] = GP[sl].n
        if g >= 2:
            P.wait("sync", SV, evactok[g - 2])
        lsm = SW[2 + sl]
        ld(C_f[:, gb:gb + 1024], xnth[g * 128:(g + 1) * 128, :], lsm)
        for k in range(KT):
            ld(C_cx[:, gb + k * 128:gb + (k + 1) * 128],
               x1h[k * 128:(k + 1) * 128, g * 128:(g + 1) * 128], lsm)
        ldtok[g] = lsm.n
        P.wait("vector", GP[sl], gtok[g])
        P.wait("vector", lsm, ldtok[g])
        P.do("vector", SV, 1, lambda e, g=g, gb=gb: (
            e.tensor_sub(C_t[:, :], C_gt[:, gb:gb + 1024], C_f[:, gb:gb + 1024]),
            e.scalar_tensor_tensor(C_t2[:, :], C_t[:, :], S_kept[:, g:g + 1],
                                   C_f[:, gb:gb + 1024], ALU.mult, ALU.add),
            e.tensor_scalar_mul(C_t[:, :], C_t2[:, :], S_fsel[:, g:g + 1]))[2])
        subtok[g] = SV.n
        for k in range(KT):
            pb = k % 2
            P.wait("tensor", SV)
            P.bank_wait(pb)
            P.do("tensor", ST, 1, lambda e, k=k, pb=pb: e.transpose(
                pp[pb][:, 0:128], C_t[:, k * 128:(k + 1) * 128], C_id[:, :]))
            P.wait("vector", ST)
            if g >= 2 and k == 0:
                P.wait("vector", SQ1, ytok[g - 2])
            P.do("vector", SV, 1, lambda e, k=k, pb=pb, gb=gb: e.tensor_add(
                C_y[:, gb + k * 128:gb + (k + 1) * 128], pp[pb][:, 0:128],
                C_cx[:, gb + k * 128:gb + (k + 1) * 128]))
            P.bank_set(pb, SV)
        evactok[g] = SV.n
        P.wait("gpsimd", SV, evactok[g])
        for k in range(KT):
            P.do("gpsimd", SQ1, 16, lambda e, k=k, g=g, gb=gb:
                 e.dma_start(out=y_out[k * 128:(k + 1) * 128,
                                       g * 128:(g + 1) * 128],
                             in_=C_y[:, gb + k * 128:gb + (k + 1) * 128]))
        ytok[g] = SQ1.n

    P.block()
    for s in ALL_DMA_SEMS:
        P.wait("sync", s)

    es5.close()
    P.emit()
    es.close()
    return nc


def _pack_w(w):
    K, M = w.shape
    k, m = K // 128, M // 128
    return np.ascontiguousarray(
        w.reshape(k, 128, m, 128).transpose(1, 2, 0, 3))


def kernel(L0_attn_x, L0_attn_w_qkv, L0_attn_w_out, L0_attn_alpha, L0_attn_beta,
           L0_ffn_Wr, L0_ffn_W1, L0_ffn_W2):
    x = np.ascontiguousarray(np.asarray(L0_attn_x, dtype=np.float32))
    w_qkv = np.asarray(L0_attn_w_qkv, dtype=np.float32)
    w_out = np.asarray(L0_attn_w_out, dtype=np.float32)
    alpha = np.asarray(L0_attn_alpha, dtype=np.float32)
    beta = np.asarray(L0_attn_beta, dtype=np.float32)
    Wr = np.asarray(L0_ffn_Wr, dtype=np.float32)
    W1 = np.asarray(L0_ffn_W1, dtype=np.float32)
    W2 = np.asarray(L0_ffn_W2, dtype=np.float32)

    wq_p = _pack_w(w_qkv)
    wo_p = _pack_w(w_out)
    wr_p = np.ascontiguousarray(Wr.reshape(KT, 128, 8).transpose(1, 0, 2))
    selc = np.zeros((128, KT, 16), np.float32)
    seltc = np.zeros((16, KT, 128), np.float32)
    for kt in range(KT):
        for p in range(128):
            h = 2 * kt + p // 64
            selc[p, kt, h] = 1.0
            seltc[h, kt, p] = 1.0
    utri = np.triu(np.ones((128, 128), np.float32))
    ident = np.eye(128, dtype=np.float32)
    su9 = (np.arange(8)[:, None] < np.arange(9)[None, :]).astype(np.float32)

    in_maps = []
    for c in range(NCOR):
        b, hf = c // 2, c % 2
        x_fm = np.ascontiguousarray(x[b, hf * TL:(hf + 1) * TL, :].T)
        in_maps.append({
            "x_fm": x_fm,
            "wq": wq_p, "wo": wo_p,
            "w1": _pack_w(W1[c]),
            "w2": _pack_w(W2[c]),
            "wr": wr_p,
            "alpha16": alpha.reshape(16, 1),
            "beta16": beta.reshape(16, 1),
            "selc": selc, "seltc": seltc, "utri": utri, "ident": ident,
            "su9": su9,
            "ltm": (np.arange(8) < c).astype(np.float32).reshape(8, 1),
            "eown": (np.arange(8) == c).astype(np.float32).reshape(8, 1),
            "maskcol": np.full((128, 1), float(hf), np.float32),
            "pairidx": np.arange((c - hf) * 128, (c - hf) * 128 + 128,
                                 dtype=np.int32).reshape(128, 1),
            "gtokf": np.where(
                np.arange(16)[None, :] < 12,
                np.arange(128)[:, None]
                + np.arange(16)[None, :] * (NCOR * 128) + c * 128,
                12 * NCOR * 128 + c * TCS
                + (np.arange(16)[None, :] - 12) * 128
                + np.arange(128)[:, None]).astype(np.float32),
        })

    nc = build_program()
    res = run_bass_kernel_spmd(nc, in_maps, core_ids=list(range(NCOR)))
    y = np.empty((B, T, HID), np.float32)
    for c in range(NCOR):
        b, hf = c // 2, c % 2
        y[b, hf * TL:(hf + 1) * TL, :] = np.asarray(res.results[c]["y_fm"]).T
    return y


# version 17 (final)
# speedup vs baseline: 1.0195x; 1.0128x over previous
import numpy as np
from contextlib import ExitStack

import concourse.bass as bass
import concourse.mybir as mybir
from concourse.bass_utils import run_bass_kernel_spmd
from concourse.bass_core_rust import IndirectOffsetOnAxis

F32 = mybir.dt.float32
I32 = mybir.dt.int32
ALU = mybir.AluOpType
ACT = mybir.ActivationFunctionType
AX = mybir.AxisListType

B, T, HID, H, D = 4, 4096, 1024, 16, 64
NCOR = 8
TL = T // 2
TCS = 512
NTC = TL // TCS
KT = HID // 128
E = 8
CAP = 2560
FTC = 512
NFTC = CAP // FTC
TLP8 = TL + 8
DEBUG = False


class Sem:
    def __init__(self, h):
        self.h = h
        self.n = 0


class Prog:

    def __init__(self, nc, sems):
        self.nc = nc
        self.sems = sems
        self.blocks = []
        self.cur = None
        self.bank_tok = {}

    def block(self):
        self.cur = {e: [] for e in ("sync", "tensor", "vector", "scalar", "gpsimd")}
        self.blocks.append(self.cur)

    def op(self, eng, fn):
        self.cur[eng].append(fn)

    def do(self, eng, sem=None, k=1, build=None):
        if sem is not None:
            sem.n += k

        def f(e, build=build, sem=sem, k=k):
            inst = build(e)
            if sem is not None:
                assert inst is not None, "sem'd op must return an instruction"
                inst.then_inc(sem.h, k)
        self.op(eng, f)
        return sem.n if sem is not None else None

    def wait(self, eng, sem, val=None):
        v = sem.n if val is None else val
        self.op(eng, lambda e, s=sem.h, v=v: e.wait_ge(s, v))

    def bank_wait(self, pb):
        t = self.bank_tok.get(pb)
        if t is not None:
            self.wait("tensor", t[0], t[1])

    def bank_set(self, pb, sem):
        self.bank_tok[pb] = (sem, sem.n)

    def emit(self):
        for blk in self.blocks:
            with self.nc.Block() as b:
                for eng in ("sync", "tensor", "vector", "scalar", "gpsimd"):
                    fns = blk[eng]
                    if fns:
                        getattr(b, eng)(lambda e, fns=fns: [f(e) for f in fns])


def build_program():
    nc = bass.Bass("TRN2", target_bir_lowering=False,
                   detect_race_conditions=False)

    def din(name, shape, dt=F32):
        return nc.dram_tensor(name, list(shape), dt, kind="ExternalInput")

    x_in = din("x_fm", [HID, TL])
    wq_in = din("wq", [128, 24, KT, 128])
    wo_in = din("wo", [128, 8, 16, 128])
    w1_in = din("w1", [128, 32, KT, 128])
    w2_in = din("w2", [128, 8, 32, 128])
    wr_in = din("wr", [128, KT, 8])
    al_in = din("alpha16", [16, 1])
    be_in = din("beta16", [16, 1])
    sel_in = din("selc", [128, KT, 16])
    selt_in = din("seltc", [16, KT, 128])
    ut_in = din("utri", [128, 128])
    id_in = din("ident", [128, 128])
    su9_in = din("su9", [8, 9])
    ltm_in = din("ltm", [8, 1])
    eown_in = din("eown", [8, 1])
    msk_in = din("maskcol", [128, 1])
    pi_in = din("pairidx", [128, 1], I32)
    gtk_in = din("gtokf", [128, 16])

    y_out = nc.dram_tensor("y_fm", [HID, TL], F32, kind="ExternalOutput")

    def scratch(name, shape, dbg=False):
        kind = "ExternalOutput" if (DEBUG and dbg) else "Internal"
        return nc.dram_tensor(name, list(shape), F32, kind=kind)

    stq = scratch("stq", [HID, TL], dbg=True)
    stk = scratch("stk", [HID, TL])
    stkv = scratch("stkv", [HID, TL], dbg=True)
    x1h = scratch("x1h", [HID, TL], dbg=True)
    state_b = scratch("state_b", [128, 32])
    allstate = scratch("allstate", [NCOR * 128, 32])
    xnth = scratch("xnth", [TL, HID], dbg=True)
    allxn = scratch("allxn", [NCOR * TL, HID])
    ivp = scratch("ivp", [TLP8, 1])
    allivp = scratch("allivp", [NCOR * TLP8, 1])
    disph = scratch("disph", [HID, CAP], dbg=True)
    eouth = scratch("eouth", [CAP, HID])
    alleout = scratch("alleout", [E * CAP, HID])
    if DEBUG:
        dbg_alleout = scratch("dbg_alleout", [E * CAP, HID], dbg=True)

    es = ExitStack()
    (SD, SST, SW0, SW1, SW2, SW3, SB0, SB1, GP0, GP1, SGD, SQ1, SP0, SP1,
     ST, SV, SS, SG, SCC) = (
        Sem(es.enter_context(nc.semaphore(n)))
        for n in ("sdx", "sst", "sw0", "sw1", "sw2", "sw3", "sb0", "sb1",
                  "gp0", "gp1", "sgd", "sq1", "sp0", "sp1",
                  "st", "sv", "ss", "sg", "scc"))
    SW = (SW0, SW1, SW2, SW3)
    SP = (SP0, SP1)
    SB = (SB0, SB1)
    GP = (GP0, GP1)

    def sb(name, shape, dt=F32, st=es):
        return st.enter_context(nc.sbuf_tensor(name, list(shape), dt))

    C_sel = sb("C_sel", [128, KT * 16])
    C_selt = sb("C_selt", [16, KT * 128])
    C_ut = sb("C_ut", [128, 128])
    C_id = sb("C_id", [128, 128])
    C_su9 = sb("C_su9", [8, 9])
    C_ltm = sb("C_ltm", [8, 1])
    C_eown = sb("C_eown", [8, 1])
    C_msk = sb("C_msk", [128, 1])
    C_ones1 = sb("C_ones1", [1, 128])
    C_onec = sb("C_onec", [128, 1])
    C_eps = sb("C_eps", [1, 1])
    ab16 = sb("ab16", [16, 4])
    dcols = sb("dcols", [128, 64])
    carry = sb("carry", [128, 32])
    sstate = sb("sstate", [128, 32])
    statec = sb("statec", [128, 64])
    rowb = sb("rowb", [1, 2048])
    S_idx = sb("S_idx", [128, 16])
    S_pos = sb("S_pos", [128, 16])
    S_pmax = sb("S_pmax", [128, 16])
    S_srt = sb("S_srt", [128, 16], I32)
    S_rows = sb("S_rows", [128, 16], I32)
    S_i32 = sb("S_i32", [128, 6], I32)
    S_kept = sb("S_kept", [128, 16])
    S_fsel = sb("S_fsel", [128, 16])
    mcol = sb("mcol", [9, 8])
    fill0 = sb("fill0", [1, 2])
    fillc = sb("fillc", [128, 2])
    cntsb = sb("cntsb", [8, 8])
    cnttsb = sb("cnttsb", [8, 8])
    ids32 = sb("ids32", [128, 20], I32)
    vcol = sb("vcol", [128, 20])
    scioc = sb("scioc", [128, 20])
    pi_sb = sb("pi_sb", [128, 1], I32)
    dcb = [sb(f"dcb{i}", [128, 1]) for i in range(16)]
    G_tk = sb("G_tk", [128, 16])
    ivpv = sb("ivpv", [128, 20])
    S_gi32 = sb("S_gi32", [128, 20], I32)
    R_a = sb("R_a", [128, 128])
    R_b = sb("R_b", [128, 128])
    M_gb = sb("M_gb", [128, 16])
    M_sl = sb("M_sl", [128, 16])
    M_pm1 = sb("M_pm1", [128, 16])
    S_j32 = sb("S_j32", [128, 96], I32)
    M_oh = sb("M_oh", [128, 8])
    M_sc = sb("M_sc", [128, 8])
    M_lg = sb("M_lg", [128, 8])
    M_cum = sb("M_cum", [128, 8])
    M_ct = sb("M_ct", [1, 8])
    M_sc2 = sb("M_sc2", [128, 8])
    M_io8 = sb("M_io8", [128, 8])
    M_base = sb("M_base", [1, 16])
    M_lb = sb("M_lb", [1, 8])
    M_lbbc = sb("M_lbbc", [128, 8])
    M_wr = sb("M_wr", [128, KT * 8])
    M_col = sb("M_col", [128, 4])

    pp = [es.enter_context(nc.psum_tensor(f"pp{i}", [128, 512], F32))
          for i in range(8)]

    P = Prog(nc, None)

    def ld(dst_ap, src_ap, sem=None):
        return P.do("sync", sem if sem is not None else SD, 16,
                    lambda e, d=dst_ap, s=src_ap: e.dma_start(out=d, in_=s))

    ALL_DMA_SEMS = (SD, SST, SW0, SW1, SW2, SW3, SB0, SB1, GP0, GP1, SGD,
                    SQ1, SP0, SP1)

    def fence_dram():
        for s in ALL_DMA_SEMS:
            P.wait("sync", s)
            P.wait("gpsimd", s)

    def grp(eng="sync"):
        P.wait(eng, SD)

    RW_MN, RW_R, RW_T = 0, 512, 1024

    def ln_stats(xt, sqscratch, xb=0, bs0=0, bs1=1):
        P.bank_wait(bs0)
        P.bank_wait(bs1)

        def f_sumx(e):
            for kt in range(KT):
                mm = e.matmul(pp[bs0][0:1, 0:TCS], C_onec[:, 0:1],
                              xt[:, xb + kt * TCS:xb + (kt + 1) * TCS],
                              start=(kt == 0), stop=(kt == KT - 1))
            return mm
        P.do("tensor", ST, 1, f_sumx)
        sqmm_tok = {}
        for kt in range(KT):
            if kt >= 2:
                P.wait("vector", ST, sqmm_tok[kt - 2])
            P.do("vector", SV, 1, lambda e, kt=kt:
                 e.tensor_mul(sqscratch[:, (kt % 2) * TCS:(kt % 2 + 1) * TCS],
                              xt[:, xb + kt * TCS:xb + (kt + 1) * TCS],
                              xt[:, xb + kt * TCS:xb + (kt + 1) * TCS]))
            P.wait("tensor", SV, SV.n)
            sqmm_tok[kt] = P.do("tensor", ST, 1, lambda e, kt=kt:
                 e.matmul(pp[bs1][0:1, 0:TCS], C_onec[:, 0:1],
                          sqscratch[:, (kt % 2) * TCS:(kt % 2 + 1) * TCS],
                          start=(kt == 0), stop=(kt == KT - 1)))
        P.wait("vector", ST)
        P.do("vector", None, 1, lambda e: e.tensor_scalar_mul(
            rowb[0:1, RW_MN:RW_MN + TCS], pp[bs0][0:1, 0:TCS], -1.0 / HID))
        P.do("vector", None, 1, lambda e: e.tensor_scalar_mul(
            rowb[0:1, RW_T:RW_T + TCS], pp[bs1][0:1, 0:TCS], 1.0 / HID))
        P.do("vector", None, 1, lambda e: e.tensor_mul(
            rowb[0:1, RW_R:RW_R + TCS], rowb[0:1, RW_MN:RW_MN + TCS],
            rowb[0:1, RW_MN:RW_MN + TCS]))
        P.do("vector", SV, 1, lambda e: e.tensor_sub(
            rowb[0:1, RW_T:RW_T + TCS], rowb[0:1, RW_T:RW_T + TCS],
            rowb[0:1, RW_R:RW_R + TCS]))
        P.bank_set(bs0, SV)
        P.bank_set(bs1, SV)
        P.wait("scalar", SV)
        P.do("scalar", SS, 1, lambda e: e.activation(
            rowb[0:1, RW_T:RW_T + TCS], rowb[0:1, RW_T:RW_T + TCS],
            ACT.Sqrt, bias=C_eps[0:1, 0:1]))
        P.wait("vector", SS)
        P.do("vector", SV, 1, lambda e: e.reciprocal(
            rowb[0:1, RW_R:RW_R + TCS], rowb[0:1, RW_T:RW_T + TCS]))

    P.block()
    ld(C_sel[:, :], sel_in.ap().rearrange("p k h -> p (k h)"))
    ld(C_selt[:, :], selt_in.ap().rearrange("h k p -> h (k p)"))
    ld(C_ut[:, :], ut_in.ap())
    ld(C_id[:, :], id_in.ap())
    ld(C_su9[:, :], su9_in.ap())
    ld(C_ltm[:, :], ltm_in.ap())
    ld(C_eown[:, :], eown_in.ap())
    ld(C_msk[:, :], msk_in.ap())
    ld(pi_sb[:, :], pi_in.ap())
    ld(G_tk[:, :], gtk_in.ap())
    ld(ab16[:, 0:1], al_in.ap())
    ld(ab16[:, 1:2], be_in.ap())
    P.do("vector", None, 1, lambda e: e.memset(C_ones1[:, :], 1.0))
    P.do("vector", None, 1, lambda e: e.memset(C_eps[:, :], 1e-5))
    P.do("vector", None, 1, lambda e: e.memset(C_onec[:, :], 1.0))
    P.do("vector", None, 1, lambda e: e.memset(carry[:, :], 0.0))
    P.do("vector", SV, 1, lambda e: e.memset(statec[:, :], 0.0))
    P.wait("scalar", SD)
    P.do("scalar", SS, 1, lambda e: e.activation(ab16[:, 2:4], ab16[:, 0:2],
                                                 ACT.Sigmoid))
    P.wait("vector", SS)
    P.do("vector", SV, 1, lambda e: e.tensor_scalar_max(ab16[:, 0:2],
                                                        ab16[:, 2:4], 1e-8))
    P.wait("tensor", SV)
    for fam in range(2):
        for kt in range(KT):
            i = fam * 8 + kt
            P.do("tensor", ST if i == 15 else None, 1, lambda e, fam=fam, kt=kt, i=i:
                 e.matmul(pp[0][:, i:i + 1],
                          C_selt[:, kt * 128:(kt + 1) * 128],
                          ab16[:, fam:fam + 1], start=True, stop=True))
    P.wait("vector", ST)
    P.do("vector", SV, 1, lambda e: e.tensor_copy(dcols[:, 0:16], pp[0][:, 0:16]))
    P.bank_set(0, SV)
    P.wait("scalar", SV)
    P.do("scalar", SS, 1, lambda e: e.activation(dcols[:, 16:32], dcols[:, 0:16],
                                                 ACT.Ln))
    P.wait("vector", SS)
    P.do("vector", SV, 1, lambda e: e.tensor_scalar_mul(dcols[:, 48:64],
                                                        dcols[:, 16:32], 512.0))
    P.wait("scalar", SV)
    P.do("scalar", SS, 1, lambda e: e.activation(dcols[:, 32:48], dcols[:, 48:64],
                                                 ACT.Exp))
    for i in range(16):
        P.do("vector", SV, 1, lambda e, i=i: e.tensor_copy(dcb[i][:, 0:1],
                                                           dcols[:, i:i + 1]))

    es1 = ExitStack()
    A_x = sb("A_x", [128, 2 * KT * TCS], st=es1)
    A_rb = sb("A_rb", [128, TCS], st=es1)
    A_u = sb("A_u", [128, TCS], st=es1)
    A_mn = sb("A_mn", [128, TCS], st=es1)
    A_e = sb("A_e", [128, TCS], st=es1)
    A_q = sb("A_q", [128, KT * TCS], st=es1)
    A_k = sb("A_k", [128, KT * TCS], st=es1)
    A_kv = sb("A_kv", [128, KT * TCS], st=es1)
    A_sq = sb("A_sq", [128, 2 * TCS], st=es1)
    A_pow = sb("A_pow", [128, 16 * TCS], st=es1)
    A_io = sb("A_io", [128, TCS], st=es1)
    A_w = sb("A_w", [128, 2 * 1024], st=es1)
    A_s1 = sb("A_s1", [128, TCS], st=es1)
    crow = sb("crow", [1, 3072], st=es1)

    P.block()
    P.do("gpsimd", SG, 1, lambda e: e.iota(A_io[:, :], [[-1, TCS]], base=TCS - 1,
                                           channel_multiplier=0,
                                           allow_small_or_imprecise_dtypes=True))
    P.wait("vector", SG)
    for fam in range(2):
        for kt in range(KT):
            i = fam * 8 + kt
            P.do("vector", SV, 1, lambda e, i=i: e.tensor_scalar_mul(
                A_s1[:, :], A_io[:, :], dcols[:, 16 + i:17 + i]))
            P.wait("scalar", SV)
            P.do("scalar", SS, 1, lambda e, i=i: e.activation(
                A_pow[:, i * TCS:(i + 1) * TCS], A_s1[:, :], ACT.Exp))
    csum_tok = {}
    for m in range(24):
        if m >= 2 and (m - 2) in csum_tok:
            P.wait("sync", ST, csum_tok[m - 2])
        wtok = ld(A_w[:, (m % 2) * 1024:(m % 2) * 1024 + 1024],
                  wq_in[:, m, :, :].rearrange("p k j -> p (k j)"), SW[m % 2])
        P.wait("tensor", SW[m % 2], wtok)
        P.bank_wait(2)

        def fmm(e, m=m):
            for k in range(KT):
                mm = e.matmul(pp[2][0:1, 0:128], C_onec[:, 0:1],
                              A_w[:, (m % 2) * 1024 + k * 128:(m % 2) * 1024 + k * 128 + 128],
                              start=(k == 0), stop=(k == KT - 1))
            return mm
        P.do("tensor", ST, 1, fmm)
        csum_tok[m] = ST.n
        P.wait("vector", ST)
        P.do("vector", SV, 1, lambda e, m=m: e.tensor_copy(
            crow[0:1, m * 128:(m + 1) * 128], pp[2][0:1, 0:128]))
        P.bank_set(2, SV)

    qkv_last = {}
    for tc in range(NTC):
        t0 = tc * TCS
        ab = (tc % 2) * KT * TCS
        if tc == 0:
            P.block()
            fence_dram()
        else:
            P.wait("sync", SD)
            if tc >= 2:
                P.wait("sync", ST, qkv_last[tc - 2])
        for kt in range(KT):
            ld(A_x[:, ab + kt * TCS:ab + (kt + 1) * TCS],
               x_in[kt * 128:(kt + 1) * 128, t0:t0 + TCS])
        dxn = SD.n
        P.wait("tensor", SD, dxn)
        P.wait("vector", SD, dxn)
        ln_stats(A_x, A_sq, xb=ab)
        P.wait("tensor", SV)
        P.bank_wait(2)
        P.do("tensor", ST, 1, lambda e: e.matmul(pp[2][:, 0:TCS], C_ones1[0:1, :],
                                                 rowb[0:1, RW_R:RW_R + TCS],
                                                 start=True, stop=True))
        P.wait("vector", ST)
        P.do("vector", SV, 1, lambda e: e.tensor_copy(A_rb[:, :], pp[2][:, 0:TCS]))
        P.bank_set(2, SV)

        if tc > 0:
            P.wait("vector", SQ1)
            wconsumed = {-2: prev_wcons[22], -1: prev_wcons[23]}
        else:
            wconsumed = {}
        for m in range(24):
            if (m - 2) in wconsumed:
                P.wait("sync", ST, wconsumed[m - 2])
            wtok = ld(A_w[:, (m % 2) * 1024:(m % 2) * 1024 + 1024],
                      wq_in[:, m, :, :].rearrange("p k j -> p (k j)"), SW[m % 2])
            pb = 3 + (m % 2)
            P.wait("tensor", SW[m % 2], wtok)
            P.bank_wait(pb)

            def fmm(e, m=m, pb=pb, ab=ab):
                for k in range(KT):
                    e.matmul(pp[pb][:, 0:TCS],
                             A_w[:, (m % 2) * 1024 + k * 128:(m % 2) * 1024 + k * 128 + 128],
                             A_x[:, ab + k * TCS:ab + (k + 1) * TCS],
                             start=(k == 0), stop=False)
                return e.matmul(pp[pb][:, 0:TCS], crow[0:1, m * 128:(m + 1) * 128],
                                rowb[0:1, RW_MN:RW_MN + TCS], start=False, stop=True)
            P.do("tensor", ST, 1, fmm)
            wconsumed[m] = ST.n
            role = "q" if m < 8 else ("k" if m < 16 else "v")
            j = m % 8
            P.wait("vector", ST)
            if role in ("q", "k"):
                dst = A_q if role == "q" else A_k
                P.do("vector", SV, 1, lambda e, pb=pb: (
                    e.tensor_mul(A_u[:, :], pp[pb][:, 0:TCS], A_rb[:, :]),
                    e.tensor_scalar_min(A_mn[:, :], A_u[:, :], 0.0))[1])
                P.bank_set(pb, SV)
                P.wait("scalar", SV)
                P.do("scalar", SS, 1, lambda e: e.activation(A_e[:, :], A_mn[:, :],
                                                             ACT.Exp))
                P.wait("vector", SS)
                P.do("vector", SV, 1, lambda e, dst=dst, j=j:
                     e.scalar_tensor_tensor(dst[:, j * TCS:(j + 1) * TCS],
                                            A_u[:, :], 0.0, A_e[:, :],
                                            ALU.max, ALU.add))
                P.wait("gpsimd", SV)
                dst_hbm = stq if role == "q" else stk
                P.do("gpsimd", SQ1, 16,
                     lambda e, j=j, t0=t0, dst=dst, dst_hbm=dst_hbm:
                     e.dma_start(out=dst_hbm[j * 128:(j + 1) * 128, t0:t0 + TCS],
                                 in_=dst[:, j * TCS:(j + 1) * TCS]))
            else:
                P.do("vector", SV, 1, lambda e, pb=pb, j=j: (
                    e.tensor_mul(A_u[:, :], pp[pb][:, 0:TCS], A_rb[:, :]),
                    e.tensor_mul(A_kv[:, j * TCS:(j + 1) * TCS], A_u[:, :],
                                 A_k[:, j * TCS:(j + 1) * TCS]))[1])
                P.bank_set(pb, SV)
                P.wait("gpsimd", SV)
                P.do("gpsimd", SQ1, 16, lambda e, j=j, t0=t0:
                     e.dma_start(out=stkv[j * 128:(j + 1) * 128, t0:t0 + TCS],
                                 in_=A_kv[:, j * TCS:(j + 1) * TCS]))
                for fam in range(2):
                    for vi, src in ((0, A_kv), (1, A_k)):
                        i = fam * 8 + j
                        col = (vi + 2 * fam) * 8 + j
                        sc_src = 32 * (tc % 2)
                        sc_dst = 32 * ((tc + 1) % 2)
                        P.do("vector", SV, 1,
                             lambda e, i=i, col=col, src=src, j=j,
                             sc_src=sc_src, sc_dst=sc_dst: (
                                 e.tensor_mul(A_s1[:, :],
                                              src[:, j * TCS:(j + 1) * TCS],
                                              A_pow[:, i * TCS:(i + 1) * TCS]),
                                 e.reduce_sum(A_mn[:, 0:1], A_s1[:, :], axis=AX.X),
                                 e.tensor_scalar(
                                     statec[:, sc_dst + col:sc_dst + col + 1],
                                     statec[:, sc_src + col:sc_src + col + 1],
                                     dcols[:, 32 + i:33 + i], A_mn[:, 0:1],
                                     ALU.mult, ALU.add))[2])
        qkv_last[tc] = wconsumed[23]
        prev_wcons = wconsumed

    es1.close()

    es2 = ExitStack()
    B_q = sb("B_q", [128, KT * TCS], st=es2)
    B_k = sb("B_k", [128, KT * TCS], st=es2)
    B_kv = sb("B_kv", [128, KT * TCS], st=es2)
    B_rkv = sb("B_rkv", [128, KT * TCS], st=es2)
    B_rks = sb("B_rks", [128, KT * TCS], st=es2)
    B_of = sb("B_of", [128, KT * TCS], st=es2)
    B_os = sb("B_os", [128, KT * TCS], st=es2)
    B_x = sb("B_x", [128, KT * TCS], st=es2)
    B_x1 = sb("B_x1", [128, KT * TCS], st=es2)
    B_w = sb("B_w", [128, 2 * 2048], st=es2)
    B_rd = sb("B_rd", [16, 4 * TCS], st=es2)
    B_t = sb("B_t", [128, 2 * TCS], st=es2)

    P.block()
    P.wait("sync", SV)
    ld(state_b.ap(), statec[:, 0:32], SST)
    P.wait("gpsimd", SST)
    P.wait("sync", SST)
    P.wait("sync", SD)
    P.wait("sync", SQ1)
    P.do("gpsimd", SCC, 1, lambda e: e.collective_compute(
        "AllGather", ALU.bypass, replica_groups=[list(range(NCOR))],
        ins=[state_b.ap().opt()], outs=[allstate.ap().opt()]))
    P.wait("gpsimd", SCC)
    P.do("gpsimd", SGD, 16, lambda e: e.indirect_dma_start(
        out=sstate[:, :], out_offset=None,
        in_=allstate.ap(), in_offset=IndirectOffsetOnAxis(pi_sb[:, :], 0)))
    P.wait("vector", SGD)
    P.do("vector", SV, 1, lambda e: e.tensor_scalar_mul(carry[:, :],
                                                        sstate[:, :], C_msk[:, :]))
    if DEBUG:
        dbg_carry = scratch("dbg_carry", [128, 32], dbg=True)
        dbg_sst = scratch("dbg_sst", [128, 33], dbg=True)
        P.wait("sync", SV)
        ld(dbg_carry.ap(), carry[:, :], SST)
        ld(dbg_sst[0:128, 0:32], sstate[:, :], SST)
        P.do("sync", SST, 16, lambda e: e.dma_start(
            out=dbg_sst[0:128, 32:33], in_=C_msk[:, :],
            allow_slow_non_contiguous=True))


    for kt in range(KT):
        ld(B_q[:, kt * TCS:(kt + 1) * TCS], stq[kt * 128:(kt + 1) * 128, 0:TCS])
        ld(B_k[:, kt * TCS:(kt + 1) * TCS], stk[kt * 128:(kt + 1) * 128, 0:TCS])
        ld(B_kv[:, kt * TCS:(kt + 1) * TCS], stkv[kt * 128:(kt + 1) * 128, 0:TCS])
        ld(B_x[:, kt * TCS:(kt + 1) * TCS], x_in[kt * 128:(kt + 1) * 128, 0:TCS])
    ld(M_wr[:, :], wr_in.ap().rearrange("p k j -> p (k j)"))
    P.do("gpsimd", SG, 1, lambda e: e.iota(M_io8[:, :], [[1, 8]], base=0,
                                           channel_multiplier=0,
                                           allow_small_or_imprecise_dtypes=True))
    P.do("vector", SV, 1, lambda e: e.memset(M_base[:, :], 0.0))

    scn_tok = {}
    reload_sd = {}
    pjev_tok = {}

    def emit_reloads(tc):
        t0 = tc * TCS
        P.wait("sync", SD)
        P.wait("sync", SV, pjev_tok[tc - 1])
        for kt in range(KT):
            ld(B_q[:, kt * TCS:(kt + 1) * TCS], stq[kt * 128:(kt + 1) * 128, t0:t0 + TCS])
            ld(B_k[:, kt * TCS:(kt + 1) * TCS], stk[kt * 128:(kt + 1) * 128, t0:t0 + TCS])
            ld(B_kv[:, kt * TCS:(kt + 1) * TCS], stkv[kt * 128:(kt + 1) * 128, t0:t0 + TCS])
            ld(B_x[:, kt * TCS:(kt + 1) * TCS], x_in[kt * 128:(kt + 1) * 128, t0:t0 + TCS])
        reload_sd[tc] = SD.n

    def emit_scans(tc, fam):
        for kt in range(KT):
            i = fam * 8 + kt
            ckv = (0 + 2 * fam) * 8 + kt
            cks = (1 + 2 * fam) * 8 + kt
            if kt == 0 and fam == 0:
                P.wait("vector", SD, reload_sd[tc])
            P.do("vector", SV, 1, lambda e, i=i, kt=kt, ckv=ckv:
                 e.tensor_tensor_scan(B_rkv[:, kt * TCS:(kt + 1) * TCS],
                                      bass.AP(dcb[i], 0, [[1, 128], [0, TCS]]),
                                      B_kv[:, kt * TCS:(kt + 1) * TCS],
                                      carry[:, ckv:ckv + 1], ALU.mult, ALU.add))
            P.wait("vector", SV)
            P.do("vector", SV, 1, lambda e, kt=kt, ckv=ckv:
                 e.tensor_copy(carry[:, ckv:ckv + 1],
                               B_rkv[:, kt * TCS + TCS - 1:kt * TCS + TCS]))
            P.do("vector", SV, 1, lambda e, i=i, kt=kt, cks=cks:
                 e.tensor_tensor_scan(B_rks[:, kt * TCS:(kt + 1) * TCS],
                                      bass.AP(dcb[i], 0, [[1, 128], [0, TCS]]),
                                      B_k[:, kt * TCS:(kt + 1) * TCS],
                                      carry[:, cks:cks + 1], ALU.mult, ALU.add))
            P.wait("vector", SV)
            P.do("vector", SV, 1, lambda e, kt=kt, cks=cks: (
                e.tensor_copy(carry[:, cks:cks + 1],
                              B_rks[:, kt * TCS + TCS - 1:kt * TCS + TCS]),
                e.tensor_mul(B_rks[:, kt * TCS:(kt + 1) * TCS],
                             B_q[:, kt * TCS:(kt + 1) * TCS],
                             B_rks[:, kt * TCS:(kt + 1) * TCS]))[1])
            scn_tok[(tc, fam, kt)] = SV.n

    def emit_den(tc, fam):
        outdst = B_of if fam == 0 else B_os
        P.bank_wait(0)
        for kt in range(KT):
            P.wait("tensor", SV, scn_tok[(tc, fam, kt)])
            P.do("tensor", ST if kt == KT - 1 else None, 1,
                 lambda e, kt=kt: e.matmul(pp[0][0:16, 0:TCS],
                                           C_sel[:, kt * 16:(kt + 1) * 16],
                                           B_rks[:, kt * TCS:(kt + 1) * TCS],
                                           start=(kt == 0), stop=(kt == KT - 1)))
        fb = fam * TCS
        P.wait("vector", ST)
        P.do("vector", SV, 1, lambda e, fb=fb: (
            e.tensor_scalar_max(B_rd[:, 2 * TCS + fb:2 * TCS + fb + TCS],
                                pp[0][0:16, 0:TCS], 1e-6),
            e.reciprocal(B_rd[:, fb:fb + TCS],
                         B_rd[:, 2 * TCS + fb:2 * TCS + fb + TCS]))[1])
        P.bank_set(0, SV)
        for kt in range(KT):
            pb = 1 + (kt % 2)
            P.wait("tensor", SV, SV.n)
            P.bank_wait(pb)
            P.do("tensor", ST, 1, lambda e, kt=kt, fb=fb, pb=pb:
                 e.matmul(pp[pb][:, 0:TCS], C_selt[:, kt * 128:(kt + 1) * 128],
                          B_rd[0:16, fb:fb + TCS], start=True, stop=True))
            P.wait("vector", ST)
            P.do("vector", SV, 1, lambda e, kt=kt, pb=pb, outdst=outdst: (
                e.tensor_mul(B_rks[:, kt * TCS:(kt + 1) * TCS],
                             B_q[:, kt * TCS:(kt + 1) * TCS],
                             B_rkv[:, kt * TCS:(kt + 1) * TCS]),
                e.tensor_mul(outdst[:, kt * TCS:(kt + 1) * TCS],
                             B_rks[:, kt * TCS:(kt + 1) * TCS],
                             pp[pb][:, 0:TCS]))[1])
            P.bank_set(pb, SV)

    for tc in range(NTC):
        t0 = tc * TCS
        if tc == 0:
            P.block()
            P.wait("vector", SD)
            reload_sd[0] = SD.n
            emit_scans(0, 0)
        emit_den(tc, 0)
        emit_scans(tc, 1)
        emit_den(tc, 1)

        wconsumed = ({} if tc == 0 else
                     {-2: prev_pj[6], -1: prev_pj[7]})
        for m in range(8):
            if (m - 2) in wconsumed:
                P.wait("sync", ST, wconsumed[m - 2])
            wtok = ld(B_w[:, (m % 2) * 2048:(m % 2) * 2048 + 2048],
                      wo_in[:, m, :, :].rearrange("p k j -> p (k j)"), SW[m % 2])
            pb = 3 + (m % 2)
            P.wait("tensor", SW[m % 2], wtok)
            P.bank_wait(pb)

            def fpj(e, m=m, pb=pb):
                for k in range(16):
                    src = B_of if k < 8 else B_os
                    mm = e.matmul(pp[pb][:, 0:TCS],
                                  B_w[:, (m % 2) * 2048 + k * 128:(m % 2) * 2048 + k * 128 + 128],
                                  src[:, (k % 8) * TCS:(k % 8 + 1) * TCS],
                                  start=(k == 0), stop=(k == 15))
                return mm
            P.do("tensor", ST, 1, fpj)
            wconsumed[m] = ST.n
            P.wait("vector", ST)
            P.wait("vector", SP[m % 2])
            P.do("vector", SV, 1, lambda e, m=m, pb=pb:
                 e.tensor_add(B_x1[:, m * TCS:(m + 1) * TCS],
                              pp[pb][:, 0:TCS], B_x[:, m * TCS:(m + 1) * TCS]))
            P.bank_set(pb, SV)
            P.wait("gpsimd", SV)
            P.do("gpsimd", SP[m % 2], 16, lambda e, m=m, t0=t0:
                 e.dma_start(out=x1h[m * 128:(m + 1) * 128, t0:t0 + TCS],
                             in_=B_x1[:, m * TCS:(m + 1) * TCS]))
        prev_pj = wconsumed
        pjev_tok[tc] = SV.n

        if tc + 1 < NTC:
            emit_reloads(tc + 1)
            emit_scans(tc + 1, 0)

        P.wait("tensor", SV, pjev_tok[tc])
        ln_stats(B_x1, B_t, bs0=1, bs1=2)
        ln_tok = SV.n
        P.wait("tensor", SV, ln_tok)
        P.bank_wait(0)
        P.bank_wait(3)
        P.do("tensor", None, 1, lambda e: e.matmul(pp[0][:, 0:TCS], C_ones1[0:1, :],
                                                   rowb[0:1, RW_R:RW_R + TCS],
                                                   start=True, stop=True))
        P.do("tensor", ST, 1, lambda e: e.matmul(pp[3][:, 0:TCS], C_ones1[0:1, :],
                                                 rowb[0:1, RW_MN:RW_MN + TCS],
                                                 start=True, stop=True))
        P.wait("vector", ST)
        for kt in range(KT):
            P.do("vector", SV, 1, lambda e, kt=kt: (
                e.tensor_add(B_t[:, 0:TCS],
                             B_x1[:, kt * TCS:(kt + 1) * TCS], pp[3][:, 0:TCS]),
                e.tensor_mul(B_os[:, kt * TCS:(kt + 1) * TCS],
                             B_t[:, 0:TCS], pp[0][:, 0:TCS]))[1])
        P.bank_set(0, SV)
        P.bank_set(3, SV)
        xnn = SV.n
        if tc == 0:
            P.wait("vector", SG)
        for tt in range(4):
            g = tc * 4 + tt
            c0 = tt * 128
            P.wait("tensor", SV, xnn)
            P.bank_wait(4)

            def frt(e, c0=c0):
                for k in range(KT):
                    mm = e.matmul(pp[4][0:128, 0:8],
                                  B_os[:, k * TCS + c0:k * TCS + c0 + 128],
                                  M_wr[:, k * 8:(k + 1) * 8],
                                  start=(k == 0), stop=(k == KT - 1))
                return mm
            P.do("tensor", ST, 1, frt)
            P.wait("vector", ST)
            P.do("vector", SV, 1, lambda e:
                 e.tensor_copy(M_lg[:, :], pp[4][0:128, 0:8]))
            P.bank_set(4, SV)
            P.wait("vector", SV)
            P.do("vector", SV, 1, lambda e:
                 e.reduce_max(M_col[:, 0:1], M_lg[:, :], axis=AX.X))
            P.wait("vector", SV)
            P.do("vector", SV, 1, lambda e:
                 e.tensor_scalar(M_sc[:, :], M_lg[:, :], M_col[:, 0:1], None,
                                 ALU.subtract))
            P.wait("scalar", SV)
            P.do("scalar", SS, 1, lambda e: e.activation(M_sc2[:, :], M_sc[:, :],
                                                         ACT.Exp))
            P.wait("vector", SS)
            P.do("vector", SV, 1, lambda e:
                 e.reduce_sum(M_col[:, 1:2], M_sc2[:, :], axis=AX.X))
            P.wait("vector", SV)
            P.do("vector", SV, 1, lambda e, g=g:
                 e.reciprocal(S_pmax[:, g:g + 1], M_col[:, 1:2]))
            P.do("vector", SV, 1, lambda e:
                 e.tensor_scalar(M_oh[:, :], M_lg[:, :], M_col[:, 0:1], None,
                                 ALU.is_ge))
            P.do("vector", SV, 1, lambda e:
                 e.tensor_scalar_add(M_sc2[:, :], M_io8[:, :], -8.0))
            P.wait("vector", SV)
            P.do("vector", SV, 1, lambda e:
                 e.tensor_mul(M_sc[:, :], M_sc2[:, :], M_oh[:, :]))
            P.wait("vector", SV)
            P.do("vector", SV, 1, lambda e:
                 e.tensor_reduce(M_col[:, 2:3], M_sc[:, :], axis=AX.X, op=ALU.min))
            P.wait("vector", SV)
            P.do("vector", SV, 1, lambda e, g=g:
                 e.tensor_scalar_add(S_idx[:, g:g + 1], M_col[:, 2:3], 8.0))
            P.wait("vector", SV)
            P.do("vector", SV, 1, lambda e, g=g:
                 e.tensor_scalar(M_oh[:, :], M_io8[:, :], S_idx[:, g:g + 1], None,
                                 ALU.is_equal))
            P.wait("tensor", SV)
            P.bank_wait(5)
            if g == 0:
                P.bank_wait(6)
            bc, bn = 8 * (g % 2), 8 * ((g + 1) % 2)
            P.do("tensor", ST, 1, lambda e, g=g, bc=bc: (
                e.matmul(pp[5][0:128, 0:8], C_ut[:, :], M_oh[:, :],
                         start=True, stop=False),
                e.matmul(pp[5][0:128, 0:8], C_ones1[0:1, :],
                         M_base[0:1, bc:bc + 8], start=False, stop=True),
                e.matmul(pp[6][0:1, g * 8:g * 8 + 8], C_onec[:, 0:1], M_oh[:, :],
                         start=True, stop=True))[2])
            P.wait("vector", ST)
            P.do("vector", SV, 1, lambda e:
                 e.tensor_copy(M_cum[:, :], pp[5][0:128, 0:8]))
            P.do("vector", SV, 1, lambda e, g=g:
                 e.tensor_copy(M_ct[0:1, :], pp[6][0:1, g * 8:g * 8 + 8]))
            P.bank_set(5, SV)
            P.wait("vector", SV)
            P.do("vector", SV, 1, lambda e:
                 e.tensor_mul(M_sc[:, :], M_cum[:, :], M_oh[:, :]))
            P.wait("vector", SV)
            P.do("vector", SV, 1, lambda e, g=g:
                 e.reduce_sum(S_pos[:, g:g + 1], M_sc[:, :], axis=AX.X))
            P.do("vector", SV, 1, lambda e, g=g, bc=bc, bn=bn:
                 e.tensor_add(M_base[0:1, bn:bn + 8], M_base[0:1, bc:bc + 8],
                              M_ct[0:1, :]))
            for kq in range(2):
                P.wait("tensor", SV, xnn)
                P.bank_wait(7)
                P.do("tensor", ST, 1, lambda e, c0=c0, kq=kq: (
                    [e.transpose(pp[7][:, (k % 4) * 128:(k % 4) * 128 + 128],
                                 B_os[:, k * TCS + c0:k * TCS + c0 + 128],
                                 C_id[:, :]) for k in range(4 * kq, 4 * kq + 4)][-1]))
                P.wait("vector", ST)
                P.do("vector", SV, 1, lambda e, kq=kq: e.tensor_copy(
                    B_t[:, kq * TCS:kq * TCS + TCS], pp[7][:, 0:TCS]))
                P.bank_set(7, SV)
            P.wait("sync", SV)
            ld(xnth[g * 128:(g + 1) * 128, :], B_t[:, 0:1024], SB0)
            P.wait("vector", SB0)
            if tc < NTC - 1:
                P.wait("gpsimd", SB0)
                P.do("gpsimd", SCC, 1, lambda e, g=g: e.collective_compute(
                    "AllGather", ALU.bypass,
                    replica_groups=[list(range(NCOR))],
                    ins=[bass.AP(xnth, g * 128 * HID, [[1, 128 * HID]]).opt()],
                    outs=[bass.AP(allxn, g * NCOR * 128 * HID,
                                  [[1, NCOR * 128 * HID]]).opt()]))
            elif tt == 3:
                P.wait("gpsimd", SB0)
                P.do("gpsimd", SCC, 1, lambda e: e.collective_compute(
                    "AllGather", ALU.bypass,
                    replica_groups=[list(range(NCOR))],
                    ins=[bass.AP(xnth, 12 * 128 * HID,
                                 [[1, TCS * HID]]).opt()],
                    outs=[bass.AP(allxn, 12 * NCOR * 128 * HID,
                                  [[1, NCOR * TCS * HID]]).opt()]))
        tr_prev = ST.n

    es2.close()

    es3 = ExitStack()
    M_x1 = sb("M_x1", [128, KT * TCS], st=es3)
    M_xn = sb("M_xn", [128, KT * TCS], st=es3)
    M_t = sb("M_t", [128, 1024], st=es3)
    M_t2 = sb("M_t2", [128, 1024], st=es3)
    M_t3 = sb("M_t3", [128, 1024], st=es3)
    idsro = sb("idsro", [1, 2560], st=es3)
    idsri = sb("idsri", [1, 2560], I32, st=es3)

    P.block()
    P.wait("sync", SD)
    if DEBUG:
        dbg_rt = scratch("dbg_rt", [128, 64], dbg=True)
        P.wait("sync", SV)
        ld(dbg_rt[0:128, 0:16], S_idx[:, :], SST)
        ld(dbg_rt[0:128, 16:32], S_pos[:, :], SST)
        ld(dbg_rt[0:128, 32:48], S_pmax[:, :], SST)
        ld(dbg_rt[0:1, 48:64], M_base[0:1, 0:16], SST)
    P.wait("sync", SV)
    ld(mcol[0:8, 0:1], M_base[0:1, 0:8])
    P.wait("tensor", SD)
    P.bank_wait(0)
    P.do("tensor", ST, 1, lambda e: e.matmul(pp[0][0:1, 0:8], mcol[0:8, 0:1],
                                             C_su9[:, 0:8], start=True, stop=True))
    P.wait("vector", ST)
    P.do("vector", SV, 1, lambda e: e.tensor_copy(M_lb[0:1, :], pp[0][0:1, 0:8]))
    P.bank_set(0, SV)
    P.wait("tensor", SV)
    P.bank_wait(1)
    P.do("tensor", ST, 1, lambda e: e.matmul(pp[1][0:128, 0:8], C_ones1[0:1, :],
                                             M_lb[0:1, :], start=True, stop=True))
    P.wait("vector", ST)
    P.do("vector", SV, 1, lambda e: e.tensor_copy(M_lbbc[:, :], pp[1][0:128, 0:8]))
    P.bank_set(1, SV)
    def lbsel16(lbb):
        for e_ in range(8):
            P.do("vector", SV, 1, lambda e, e_=e_: e.tensor_scalar(
                R_a[:, e_ * 16:(e_ + 1) * 16], S_idx[:, :], float(e_), None,
                ALU.is_equal))
        P.wait("vector", SV)
        for e_ in range(8):
            P.do("vector", SV, 1, lambda e, e_=e_, lbb=lbb: e.tensor_scalar(
                R_b[:, e_ * 16:(e_ + 1) * 16], R_a[:, e_ * 16:(e_ + 1) * 16],
                lbb[:, e_:e_ + 1], None, ALU.mult))
        P.wait("vector", SV)
        for q in range(4):
            P.do("vector", SV, 1, lambda e, q=q: e.tensor_add(
                R_a[:, q * 16:(q + 1) * 16],
                R_b[:, 2 * q * 16:(2 * q + 1) * 16],
                R_b[:, (2 * q + 1) * 16:(2 * q + 2) * 16]))
        P.wait("vector", SV)
        for q in range(2):
            P.do("vector", SV, 1, lambda e, q=q: e.tensor_add(
                R_b[:, q * 16:(q + 1) * 16],
                R_a[:, 2 * q * 16:(2 * q + 1) * 16],
                R_a[:, (2 * q + 1) * 16:(2 * q + 2) * 16]))
        P.wait("vector", SV)
        P.do("vector", SV, 1, lambda e: e.tensor_add(R_a[:, 0:16], R_b[:, 0:16],
                                                     R_b[:, 16:32]))
        P.wait("vector", SV)

    lbsel16(M_lbbc)
    P.do("vector", SV, 1, lambda e: e.tensor_add(R_b[:, 0:16], R_a[:, 0:16],
                                                 S_pos[:, :]))
    P.wait("vector", SV)
    P.do("vector", SV, 1, lambda e: e.tensor_scalar_add(R_a[:, 16:32],
                                                        R_b[:, 0:16], -1.0))
    P.wait("vector", SV)
    P.do("vector", SV, 1, lambda e: e.tensor_copy(S_srt[:, :], R_a[:, 16:32]))
    posn = SV.n
    P.wait("gpsimd", SV, posn)
    for g in range(16):
        P.do("gpsimd", GP0, 16, lambda e, g=g: e.indirect_dma_start(
            out=ivp.ap(), out_offset=IndirectOffsetOnAxis(S_srt[:, g:g + 1], 0),
            in_=G_tk[:, g:g + 1], in_offset=None))
    P.do("gpsimd", SQ1, 16, lambda e: e.dma_start(
        out=bass.AP(ivp, TL, [[1, 8]]), in_=M_base[0:1, 0:8]))
    P.wait("gpsimd", GP0)
    P.wait("gpsimd", SQ1)
    P.do("gpsimd", SCC, 1, lambda e: e.collective_compute(
        "AllGather", ALU.bypass, replica_groups=[list(range(NCOR))],
        ins=[ivp.ap().opt()], outs=[allivp.ap().opt()]))
    P.wait("gpsimd", SCC)
    P.wait("sync", SCC)

    P.block()
    cnt_ap = bass.AP(allivp, TL, [[TLP8, 8], [1, 8]])
    cntt_ap = bass.AP(allivp, TL, [[1, 8], [TLP8, 8], [1, 1]])
    ld(cntsb[:, :], cnt_ap)
    P.do("sync", SD, 16, lambda e: e.dma_start(out=cnttsb[:, :], in_=cntt_ap,
                                               allow_slow_non_contiguous=True))
    P.wait("tensor", SD)
    P.bank_wait(0)
    P.do("tensor", None, 1, lambda e: e.matmul(pp[0][0:8, 0:1], cnttsb[:, :],
                                               C_eown[:, :], start=True, stop=True))
    P.do("tensor", None, 1, lambda e: e.matmul(pp[0][0:8, 1:2], cnttsb[:, :],
                                               C_ltm[:, :], start=True, stop=True))
    P.do("tensor", ST, 1, lambda e: e.matmul(pp[0][0:1, 8:16], C_ltm[:, :],
                                             cntsb[:, :], start=True, stop=True))
    P.wait("vector", ST)
    P.do("vector", SV, 1, lambda e: (
        e.tensor_copy(mcol[0:8, 1:3], pp[0][0:8, 0:2]),
        e.tensor_copy(M_lb[0:1, :], pp[0][0:1, 8:16]))[1])
    P.bank_set(0, SV)
    P.wait("tensor", SV)
    P.bank_wait(1)
    P.do("tensor", None, 1, lambda e: e.matmul(pp[1][0:8, 0:1], C_su9[:, 0:8],
                                               mcol[0:8, 1:2], start=True, stop=True))
    P.do("tensor", ST, 1, lambda e: e.matmul(pp[1][0:8, 1:2], C_su9[:, 1:9],
                                             mcol[0:8, 1:2], start=True, stop=True))
    P.wait("vector", ST)
    P.do("vector", SV, 1, lambda e: e.tensor_copy(mcol[0:8, 3:5], pp[1][0:8, 0:2]))
    P.bank_set(1, SV)
    P.wait("sync", SV)
    ld(fill0[0:1, 0:1], mcol[7:8, 4:5])
    P.wait("vector", SD)
    P.do("vector", SV, 1, lambda e:
         e.tensor_scalar_min(fill0[0:1, 1:2], fill0[0:1, 0:1], float(CAP)))
    P.wait("tensor", SV)
    P.bank_wait(4)
    P.do("tensor", ST, 1, lambda e: e.matmul(pp[4][0:128, 0:1], C_ones1[0:1, :],
                                             fill0[0:1, 1:2], start=True, stop=True))
    P.wait("vector", ST)
    P.do("vector", SV, 1, lambda e: e.tensor_copy(fillc[:, 0:1], pp[4][0:128, 0:1]))
    P.bank_set(4, SV)
    P.do("gpsimd", SG, 1, lambda e: e.iota(M_col[0:8, 2:3], [[1, 1]], base=0,
                                           channel_multiplier=TLP8,
                                           allow_small_or_imprecise_dtypes=True))
    P.wait("vector", SG)
    P.do("vector", SV, 1, lambda e:
         e.tensor_add(M_col[0:8, 3:4], M_col[0:8, 2:3], mcol[0:8, 2:3]))
    P.wait("vector", SV)
    P.do("vector", SV, 1, lambda e:
         e.tensor_sub(M_col[0:8, 2:3], M_col[0:8, 3:4], mcol[0:8, 3:4]))
    P.do("gpsimd", SG, 1, lambda e: e.iota(idsro[0:1, :], [[1, 2560]], base=0,
                                           channel_multiplier=0,
                                           allow_small_or_imprecise_dtypes=True))
    M_st = M_xn
    seg_tok = None
    for ch in range(5):
        s0 = ch * 512
        if seg_tok is not None:
            P.wait("gpsimd", SV, seg_tok)
        P.do("gpsimd", SG, 1, lambda e, s0=s0: e.iota(
            M_st[0:8, 0:512], [[1, 512]], base=s0, channel_multiplier=0,
            allow_small_or_imprecise_dtypes=True))
        P.wait("vector", SG)
        P.do("vector", SV, 1, lambda e: (
            e.tensor_scalar(M_st[0:8, 512:1024], M_st[0:8, 0:512], mcol[0:8, 3:4],
                            None, ALU.is_ge),
            e.tensor_scalar(M_st[0:8, 1536:2048], M_st[0:8, 0:512], mcol[0:8, 4:5],
                            None, ALU.is_ge))[1])
        P.wait("vector", SV)
        P.do("vector", SV, 1, lambda e:
             e.tensor_sub(M_st[0:8, 1024:1536], M_st[0:8, 512:1024],
                          M_st[0:8, 1536:2048]))
        seg_tok = SV.n
        P.wait("tensor", SV)
        P.bank_wait(2)
        P.do("tensor", ST, 1, lambda e, s0=s0: (
            e.matmul(pp[2][0:1, 0:512], M_col[0:8, 2:3], M_st[0:8, 1024:1536],
                     start=True, stop=False),
            e.matmul(pp[2][0:1, 0:512], C_ones1[0:1, 0:1], idsro[0:1, s0:s0 + 512],
                     start=False, stop=True))[1])
        P.wait("vector", ST)
        P.do("vector", SV, 1, lambda e, s0=s0:
             e.tensor_scalar(M_st[0:1, 2048:2560], idsro[0:1, s0:s0 + 512],
                             fill0[0:1, 1:2], None, ALU.is_lt))
        P.wait("vector", SV)
        P.do("vector", SV, 1, lambda e:
             e.tensor_mul(M_st[0:1, 2560:3072], pp[2][0:1, 0:512],
                          M_st[0:1, 2048:2560]))
        P.wait("vector", SV)
        P.do("vector", SV, 1, lambda e, s0=s0:
             e.tensor_copy(idsri[0:1, s0:s0 + 512], M_st[0:1, 2560:3072]))
        P.bank_set(2, SV)
    P.wait("sync", SV)
    for gg in range(20):
        P.do("sync", SD, 16, lambda e, gg=gg: e.dma_start(
            out=ids32[:, gg:gg + 1], in_=idsri[0:1, gg * 128:(gg + 1) * 128]))
    P.wait("gpsimd", SD)
    for gg in range(20):
        P.do("gpsimd", GP1, 16, lambda e, gg=gg: e.indirect_dma_start(
            out=ivpv[:, gg:gg + 1], out_offset=None,
            in_=allivp.ap(), in_offset=IndirectOffsetOnAxis(ids32[:, gg:gg + 1],
                                                            0)))
    P.wait("vector", GP1)
    P.do("vector", SV, 1, lambda e: e.tensor_copy(S_gi32[:, :], ivpv[:, :]))
    P.do("gpsimd", SG, 1, lambda e: e.iota(scioc[:, :], [[128, 20]], base=0,
                                           channel_multiplier=1,
                                           allow_small_or_imprecise_dtypes=True))
    P.wait("vector", SG)
    P.do("vector", SV, 1, lambda e: e.tensor_scalar(vcol[:, :], scioc[:, :],
                                                    fillc[:, 0:1], None, ALU.is_lt))
    P.wait("tensor", SV)
    P.bank_wait(3)
    P.do("tensor", ST, 1, lambda e: e.matmul(pp[3][0:128, 0:8], C_ones1[0:1, :],
                                             M_lb[0:1, :], start=True, stop=True))
    P.wait("vector", ST)
    P.do("vector", SV, 1, lambda e: e.tensor_copy(M_lbbc[:, :], pp[3][0:128, 0:8]))
    P.bank_set(3, SV)
    lbsel16(M_lbbc)
    P.do("vector", SV, 1, lambda e: e.tensor_add(R_b[:, 0:16], R_a[:, 0:16],
                                                 S_pos[:, :]))
    P.wait("vector", SV)
    P.do("vector", SV, 1, lambda e: e.tensor_scalar_add(M_gb[:, :],
                                                        R_b[:, 0:16], -1.0))
    P.wait("vector", SV)
    P.do("vector", SV, 1, lambda e: e.tensor_scalar(
        S_kept[:, :], M_gb[:, :], float(CAP), None, ALU.is_lt))
    P.do("vector", SV, 1, lambda e: e.tensor_scalar_min(
        M_sl[:, :], M_gb[:, :], float(CAP - 1)))
    P.do("vector", SV, 1, lambda e: e.tensor_scalar_add(
        M_pm1[:, :], S_pmax[:, :], -1.0))
    P.wait("vector", SV)
    P.do("vector", SV, 1, lambda e: e.tensor_copy(S_j32[:, 0:16], M_sl[:, :]))
    P.do("vector", SV, 1, lambda e: e.tensor_copy(S_j32[:, 48:64], S_idx[:, :]))
    P.do("vector", SV, 1, lambda e: e.tensor_mul(R_a[:, 0:16], S_kept[:, :],
                                                 M_pm1[:, :]))
    P.wait("vector", SV)
    P.do("vector", SV, 1, lambda e: e.tensor_scalar(
        S_j32[:, 16:32], S_j32[:, 0:16], 9, None, ALU.arith_shift_right))
    P.do("vector", SV, 1, lambda e: e.tensor_scalar(
        S_j32[:, 32:48], S_j32[:, 0:16], 511, None, ALU.bitwise_and))
    P.do("vector", SV, 1, lambda e: e.tensor_scalar(
        S_j32[:, 80:96], S_j32[:, 48:64], FTC, None, ALU.mult))
    P.do("vector", SV, 1, lambda e: e.tensor_scalar_add(
        S_fsel[:, :], R_a[:, 0:16], 1.0))
    P.wait("vector", SV)
    P.do("vector", SV, 1, lambda e: e.tensor_scalar(
        S_j32[:, 64:80], S_j32[:, 16:32], NCOR * FTC, None, ALU.mult))
    P.wait("vector", SV)
    P.do("vector", SV, 1, lambda e: e.tensor_tensor(
        S_j32[:, 0:16], S_j32[:, 64:80], S_j32[:, 32:48], ALU.add))
    P.wait("vector", SV)
    P.do("vector", SV, 1, lambda e: e.tensor_tensor(
        S_rows[:, :], S_j32[:, 0:16], S_j32[:, 80:96], ALU.add))

    P.block()
    P.wait("gpsimd", SD)
    P.wait("gpsimd", SV)
    g_tok = {}
    tr_done = {}
    for g in range(20):
        buf = M_t if g % 2 == 0 else M_t2
        if g >= 2:
            P.wait("gpsimd", SV, g_tok[g - 2])
        P.do("gpsimd", GP[g % 2], 16, lambda e, g=g, buf=buf: e.indirect_dma_start(
            out=buf[:, 0:1024], out_offset=None,
            in_=allxn.ap(), in_offset=IndirectOffsetOnAxis(S_gi32[:, g:g + 1],
                                                           0)))
        P.wait("vector", GP[g % 2])
        if g >= 1:
            P.wait("vector", ST, tr_done[g - 1])
        P.do("vector", SV, 1, lambda e, g=g, buf=buf: e.tensor_scalar_mul(
            M_t3[:, 0:1024], buf[:, 0:1024], vcol[:, g:g + 1]))
        g_tok[g] = SV.n
        ebuf = M_x1
        for k in range(KT):
            pb = 4 + (k % 2)
            P.wait("tensor", SV)
            P.bank_wait(pb)
            P.do("tensor", ST, 1, lambda e, k=k, pb=pb: e.transpose(
                pp[pb][:, 0:128], M_t3[:, k * 128:(k + 1) * 128], C_id[:, :]))

            P.wait("vector", ST)
            P.do("vector", SV, 1, lambda e, k=k, pb=pb: e.tensor_copy(
                ebuf[:, k * 128:(k + 1) * 128], pp[pb][:, 0:128]))
            P.bank_set(pb, SV)
            if k == KT - 1:
                tr_done[g] = ST.n
        P.wait("sync", SV)
        for k in range(KT):
            ld(disph[k * 128:(k + 1) * 128, g * 128:(g + 1) * 128],
               ebuf[:, k * 128:(k + 1) * 128], SST)
        P.wait("vector", SST)

    es3.close()

    es4 = ExitStack()
    N_d = sb("N_d", [128, 2 * KT * FTC], st=es4)
    N_h = sb("N_h", [128, 32 * FTC], st=es4)
    N_w = sb("N_w", [128, 4 * 1024], st=es4)
    N_eo = sb("N_eo", [128, 8 * FTC], st=es4)
    N_t = sb("N_t", [128, 1024], st=es4)

    rd_tok = {}
    w1_last = {}
    si = 0
    for ftc in range(NFTC):
        f0 = ftc * FTC
        nb = (ftc % 2) * KT * FTC
        if ftc == 0:
            P.block()
            fence_dram()
        else:
            P.wait("sync", SD)
            if ftc >= 2:
                P.wait("sync", ST, w1_last[ftc - 2])
        for kt in range(KT):
            ld(N_d[:, nb + kt * FTC:nb + (kt + 1) * FTC],
               disph[kt * 128:(kt + 1) * 128, f0:f0 + FTC])
        dln = SD.n
        for dm in range(32):
            sl = si % 4
            if si - 4 in rd_tok:
                P.wait("sync", ST, rd_tok[si - 4])
            wtok = ld(N_w[:, sl * 1024:sl * 1024 + 1024],
                      w1_in[:, dm, :, :].rearrange("p k j -> p (k j)"), SW[sl])
            pb = dm % 2
            P.wait("tensor", SW[sl], wtok)
            P.wait("tensor", SD, dln)
            P.bank_wait(pb)

            def fh(e, sl=sl, pb=pb, nb=nb):
                for k in range(KT):
                    mm = e.matmul(pp[pb][:, 0:FTC],
                                  N_w[:, sl * 1024 + k * 128:sl * 1024 + k * 128 + 128],
                                  N_d[:, nb + k * FTC:nb + (k + 1) * FTC],
                                  start=(k == 0), stop=(k == KT - 1))
                return mm
            P.do("tensor", ST, 1, fh)
            rd_tok[si] = ST.n
            si += 1
            P.wait("vector", ST)
            P.do("vector", SV, 1, lambda e, dm=dm, pb=pb: e.tensor_scalar_max(
                N_h[:, dm * FTC:(dm + 1) * FTC], pp[pb][:, 0:FTC], 0.0))
            P.bank_set(pb, SV)
        w1_last[ftc] = ST.n
        hn = SV.n
        for m in range(8):
            pb = 2 + (m % 2)
            P.bank_wait(pb)
            for quart in range(4):
                sl = si % 4
                P.wait("sync", ST, rd_tok[si - 4])
                wtok = ld(N_w[:, sl * 1024:sl * 1024 + 1024],
                          w2_in[:, m, quart * 8:(quart + 1) * 8, :]
                          .rearrange("p k j -> p (k j)"), SW[sl])
                P.wait("tensor", SW[sl], wtok)
                if m == 0 and quart == 0:
                    P.wait("tensor", SV, hn)

                P.do("tensor", ST, 1, lambda e, quart=quart, pb=pb, sl=sl: (
                    [e.matmul(pp[pb][:, 0:FTC],
                              N_w[:, sl * 1024 + kk * 128:sl * 1024 + kk * 128 + 128],
                              N_h[:, (quart * 8 + kk) * FTC:(quart * 8 + kk + 1) * FTC],
                              start=(quart * 8 + kk == 0),
                              stop=(quart * 8 + kk == 31))
                     for kk in range(8)][-1]))
                rd_tok[si] = ST.n
                si += 1
            P.wait("vector", ST)
            P.do("vector", SV, 1, lambda e, m=m, pb=pb: e.tensor_copy(
                N_eo[:, m * FTC:(m + 1) * FTC], pp[pb][:, 0:FTC]))
            P.bank_set(pb, SV)
        eon = SV.n
        for tb in range(4):
            for m in range(8):
                pb = 6 + (m % 2)
                P.wait("tensor", SV, max(eon, SV.n))
                P.bank_wait(pb)
                P.do("tensor", ST, 1, lambda e, tb=tb, m=m, pb=pb: e.transpose(
                    pp[pb][:, 0:128],
                    N_eo[:, m * FTC + tb * 128:m * FTC + tb * 128 + 128],
                    C_id[:, :]))
                P.wait("vector", ST)
                P.do("vector", SV, 1, lambda e, m=m, pb=pb: e.tensor_copy(
                    N_t[:, m * 128:(m + 1) * 128], pp[pb][:, 0:128]))
                P.bank_set(pb, SV)
            P.wait("gpsimd", SV)
            P.do("gpsimd", SQ1, 16, lambda e, f0=f0, tb=tb:
                 e.dma_start(out=eouth[f0 + tb * 128:f0 + tb * 128 + 128, :],
                             in_=N_t[:, 0:1024]))
            P.wait("vector", SQ1)
        P.wait("gpsimd", SQ1)
        P.do("gpsimd", SCC, 1, lambda e, f0=f0, ftc=ftc: e.collective_compute(
            "AllGather", ALU.bypass, replica_groups=[list(range(NCOR))],
            ins=[bass.AP(eouth, f0 * HID, [[1, FTC * HID]]).opt()],
            outs=[bass.AP(alleout, ftc * NCOR * FTC * HID,
                          [[1, NCOR * FTC * HID]]).opt()]))

    P.block()
    P.wait("gpsimd", SCC)
    if DEBUG:
        P.wait("sync", SCC)
    if DEBUG:
        P.do("sync", SST, 16, lambda e: e.dma_start(out=dbg_alleout.ap(),
                                                    in_=alleout.ap()))

    es4.close()
    es5 = ExitStack()
    C_gt = sb("C_gt", [128, 2 * 1024], st=es5)
    C_f = sb("C_f", [128, 2 * 1024], st=es5)
    C_cx = sb("C_cx", [128, 2 * 1024], st=es5)
    C_y = sb("C_y", [128, 2 * 1024], st=es5)
    C_t = sb("C_t", [128, 1024], st=es5)
    C_t2 = sb("C_t2", [128, 1024], st=es5)

    gtok = {}
    ldtok = {}
    subtok = {}
    evactok = {}
    ytok = {}
    P.block()
    for g in range(16):
        sl = g % 2
        gb = sl * 1024
        if g >= 2:
            P.wait("gpsimd", SV, subtok[g - 2])
        P.do("gpsimd", GP[sl], 16, lambda e, g=g, gb=gb: e.indirect_dma_start(
            out=C_gt[:, gb:gb + 1024], out_offset=None,
            in_=alleout.ap(), in_offset=IndirectOffsetOnAxis(S_rows[:, g:g + 1],
                                                             0)))
        gtok[g] = GP[sl].n
        if g >= 2:
            P.wait("sync", SV, evactok[g - 2])
        lsm = SW[2 + sl]
        ld(C_f[:, gb:gb + 1024], xnth[g * 128:(g + 1) * 128, :], lsm)
        for k in range(KT):
            ld(C_cx[:, gb + k * 128:gb + (k + 1) * 128],
               x1h[k * 128:(k + 1) * 128, g * 128:(g + 1) * 128], lsm)
        ldtok[g] = lsm.n
        P.wait("vector", GP[sl], gtok[g])
        P.wait("vector", lsm, ldtok[g])
        P.do("vector", SV, 1, lambda e, g=g, gb=gb: (
            e.tensor_sub(C_t[:, :], C_gt[:, gb:gb + 1024], C_f[:, gb:gb + 1024]),
            e.scalar_tensor_tensor(C_t2[:, :], C_t[:, :], S_kept[:, g:g + 1],
                                   C_f[:, gb:gb + 1024], ALU.mult, ALU.add),
            e.tensor_scalar_mul(C_t[:, :], C_t2[:, :], S_fsel[:, g:g + 1]))[2])
        subtok[g] = SV.n
        for k in range(KT):
            pb = k % 2
            P.wait("tensor", SV)
            P.bank_wait(pb)
            P.do("tensor", ST, 1, lambda e, k=k, pb=pb: e.transpose(
                pp[pb][:, 0:128], C_t[:, k * 128:(k + 1) * 128], C_id[:, :]))
            P.wait("vector", ST)
            if g >= 2 and k == 0:
                P.wait("vector", SQ1, ytok[g - 2])
            P.do("vector", SV, 1, lambda e, k=k, pb=pb, gb=gb: e.tensor_add(
                C_y[:, gb + k * 128:gb + (k + 1) * 128], pp[pb][:, 0:128],
                C_cx[:, gb + k * 128:gb + (k + 1) * 128]))
            P.bank_set(pb, SV)
        evactok[g] = SV.n
        P.wait("gpsimd", SV, evactok[g])
        for k in range(KT):
            P.do("gpsimd", SQ1, 16, lambda e, k=k, g=g, gb=gb:
                 e.dma_start(out=y_out[k * 128:(k + 1) * 128,
                                       g * 128:(g + 1) * 128],
                             in_=C_y[:, gb + k * 128:gb + (k + 1) * 128]))
        ytok[g] = SQ1.n

    P.block()
    for s in ALL_DMA_SEMS:
        P.wait("sync", s)

    es5.close()
    P.emit()
    es.close()
    return nc


def _pack_w(w):
    K, M = w.shape
    k, m = K // 128, M // 128
    return np.ascontiguousarray(
        w.reshape(k, 128, m, 128).transpose(1, 2, 0, 3))


def kernel(L0_attn_x, L0_attn_w_qkv, L0_attn_w_out, L0_attn_alpha, L0_attn_beta,
           L0_ffn_Wr, L0_ffn_W1, L0_ffn_W2):
    x = np.ascontiguousarray(np.asarray(L0_attn_x, dtype=np.float32))
    w_qkv = np.asarray(L0_attn_w_qkv, dtype=np.float32)
    w_out = np.asarray(L0_attn_w_out, dtype=np.float32)
    alpha = np.asarray(L0_attn_alpha, dtype=np.float32)
    beta = np.asarray(L0_attn_beta, dtype=np.float32)
    Wr = np.asarray(L0_ffn_Wr, dtype=np.float32)
    W1 = np.asarray(L0_ffn_W1, dtype=np.float32)
    W2 = np.asarray(L0_ffn_W2, dtype=np.float32)

    wq_p = _pack_w(w_qkv)
    wo_p = _pack_w(w_out)
    wr_p = np.ascontiguousarray(Wr.reshape(KT, 128, 8).transpose(1, 0, 2))
    selc = np.zeros((128, KT, 16), np.float32)
    seltc = np.zeros((16, KT, 128), np.float32)
    for kt in range(KT):
        for p in range(128):
            h = 2 * kt + p // 64
            selc[p, kt, h] = 1.0
            seltc[h, kt, p] = 1.0
    utri = np.triu(np.ones((128, 128), np.float32))
    ident = np.eye(128, dtype=np.float32)
    su9 = (np.arange(8)[:, None] < np.arange(9)[None, :]).astype(np.float32)

    in_maps = []
    for c in range(NCOR):
        b, hf = c // 2, c % 2
        x_fm = np.ascontiguousarray(x[b, hf * TL:(hf + 1) * TL, :].T)
        in_maps.append({
            "x_fm": x_fm,
            "wq": wq_p, "wo": wo_p,
            "w1": _pack_w(W1[c]),
            "w2": _pack_w(W2[c]),
            "wr": wr_p,
            "alpha16": alpha.reshape(16, 1),
            "beta16": beta.reshape(16, 1),
            "selc": selc, "seltc": seltc, "utri": utri, "ident": ident,
            "su9": su9,
            "ltm": (np.arange(8) < c).astype(np.float32).reshape(8, 1),
            "eown": (np.arange(8) == c).astype(np.float32).reshape(8, 1),
            "maskcol": np.full((128, 1), float(hf), np.float32),
            "pairidx": np.arange((c - hf) * 128, (c - hf) * 128 + 128,
                                 dtype=np.int32).reshape(128, 1),
            "gtokf": np.where(
                np.arange(16)[None, :] < 12,
                np.arange(128)[:, None]
                + np.arange(16)[None, :] * (NCOR * 128) + c * 128,
                12 * NCOR * 128 + c * TCS
                + (np.arange(16)[None, :] - 12) * 128
                + np.arange(128)[:, None]).astype(np.float32),
        })

    nc = build_program()
    res = run_bass_kernel_spmd(nc, in_maps, core_ids=list(range(NCOR)))
    y = np.empty((B, T, HID), np.float32)
    for c in range(NCOR):
        b, hf = c // 2, c % 2
        y[b, hf * TL:(hf + 1) * TL, :] = np.asarray(res.results[c]["y_fm"]).T
    return y
